# version 10
# baseline (speedup 1.0000x reference)
import math
from contextlib import ExitStack

import numpy as np
import ml_dtypes
import concourse.bass as bass
import concourse.mybir as mybir
from concourse.bass_utils import run_bass_kernel_spmd
from concourse.alu_op_type import AluOpType

F32 = mybir.dt.float32
BF16 = mybir.dt.bfloat16
I32 = mybir.dt.int32
AF = mybir.ActivationFunctionType
AX = mybir.AxisListType.X

NCORES = 8
D = 1024
DC = D // 128
T = 4096
BPC = 2
NTOK = BPC * T
TK = 512
NTK = NTOK // TK
E = 8
FF = 4096
FC = FF // 128
H = 16
DH = 64
WIDTH = 3 * D
LN_EPS = 1e-5

bf16 = ml_dtypes.bfloat16



def _rope_tables():
    inv_freq = 1.0 / 10000.0 ** (np.arange(0, DH, 2, dtype=np.float64) / DH)
    t = np.arange(T, dtype=np.float64)
    freqs = t[:, None] * inv_freq[None, :]
    j = np.arange(DH)
    C = np.cos(freqs[:, j % 32]).T
    S = np.sin(freqs[:, j % 32]).T
    sign = np.where(j < 32, -1.0, 1.0)[:, None]
    Crep = np.tile(C, (2, 1)).astype(np.float32)
    Srep = np.tile(S * sign, (2, 1)).astype(np.float32)
    return Crep, Srep


def _fft_consts():
    c = {}
    n1 = np.arange(64)[:, None]
    k1 = np.arange(128)[None, :]
    ang1 = 2 * np.pi * n1 * k1 / 128.0
    c["G1c"] = np.cos(ang1)
    c["G1s"] = -np.sin(ang1)
    c["G1Kc"] = np.cos(ang1)
    c["G1Ks"] = np.sin(ang1)
    k1c = np.arange(128)[:, None]
    n2 = np.arange(64)[None, :]
    angt = 2 * np.pi * n2 * k1c / 8192.0
    c["Twc"] = np.cos(angt)
    c["Tws"] = -np.sin(angt)
    twk = np.exp(2j * np.pi * n2 * k1c / 8192.0) * np.exp(-2j * np.pi * k1c * 4095.0 / 8192.0)
    c["TwKc"] = twk.real
    c["TwKs"] = twk.imag
    n2c = np.arange(64)[:, None]
    k2 = np.arange(64)[None, :]
    c2 = np.cos(2 * np.pi * n2c * k2 / 64.0)
    s2 = np.sin(2 * np.pi * n2c * k2 / 64.0)
    L = np.zeros((128, 128))
    L[:64, :64] = c2; L[64:, :64] = s2; L[:64, 64:] = -s2; L[64:, 64:] = c2
    c["L"] = L
    p2 = np.exp(-2j * np.pi * np.arange(64) * 4095.0 / 64.0)
    LK0 = np.zeros((128, 128))
    LK0[:64, :64] = c2; LK0[64:, :64] = -s2; LK0[:64, 64:] = s2; LK0[64:, 64:] = c2
    LK = np.zeros((128, 128))
    LK[:, :64] = LK0[:, :64] * p2.real[None, :] - LK0[:, 64:] * p2.imag[None, :]
    LK[:, 64:] = LK0[:, :64] * p2.imag[None, :] + LK0[:, 64:] * p2.real[None, :]
    c["LK"] = LK
    k2c = np.arange(64)[:, None]
    n2r = np.arange(64)[None, :]
    ci = np.cos(2 * np.pi * k2c * n2r / 64.0)
    si = np.sin(2 * np.pi * k2c * n2r / 64.0)
    Linv = np.zeros((128, 128))
    Linv[:64, :64] = ci; Linv[64:, :64] = -si; Linv[:64, 64:] = si; Linv[64:, 64:] = ci
    c["Linv"] = Linv
    n2i = np.arange(64)[:, None]
    k1i = np.arange(128)[None, :]
    angti = 2 * np.pi * n2i * k1i / 8192.0
    c["TwIc2"] = np.concatenate([np.cos(angti), np.cos(angti)], axis=0)
    c["TwIs2"] = np.concatenate([np.sin(angti), np.sin(angti)], axis=0)
    k1b = np.arange(128)[:, None]
    n1b = np.arange(64)[None, :]
    c["GBc"] = np.cos(2 * np.pi * k1b * n1b / 128.0) / 8192.0
    c["GBs"] = -np.sin(2 * np.pi * k1b * n1b / 128.0) / 8192.0
    return {k: v.astype(np.float32) for k, v in c.items()}


def _filter_consts():
    t = np.arange(T, dtype=np.float64)
    th = t / (T - 1)
    ang = 2.0 * np.pi * t / T
    phase = 0.0001 * ang
    peT = np.stack([th, np.cos(phase), -np.sin(phase)], axis=0).astype(np.float32)
    dmax, dmin = (math.log(0.01) / 0.3, math.log(0.01) / 1.5)
    deltas = np.abs(np.linspace(dmax, dmin, D)).astype(np.float32)
    return peT, deltas, th.astype(np.float32)



class Prog:

    def __init__(self, name):
        self.nc = bass.Bass("TRN2", target_bir_lowering=False, debug=False)
        self.ctx = ExitStack()
        self.name = name
        self.phase_ctx = None
        self.prefix = ""

    def begin_phase(self, name):
        self.phase_ctx = ExitStack()
        self.prefix = name + "_"

    def end_phase(self):
        self.phase_ctx.close()
        self.phase_ctx = None
        self.prefix = ""

    def dram_in(self, name, shape, dt=F32):
        return self.nc.dram_tensor(name, shape, dt, kind="ExternalInput").ap()

    def dram_out(self, name, shape, dt=F32):
        return self.nc.dram_tensor(name, shape, dt, kind="ExternalOutput").ap()

    def dram_tmp(self, name, shape, dt=F32):
        return self.nc.dram_tensor(name, shape, dt, kind="Internal").ap()

    def sb(self, name, shape, dt=F32):
        ctx = self.phase_ctx if self.phase_ctx is not None else self.ctx
        return ctx.enter_context(self.nc.sbuf_tensor(self.prefix + name, shape, dt))

    def ps(self, name, shape, dt=F32):
        ctx = self.phase_ctx if self.phase_ctx is not None else self.ctx
        return ctx.enter_context(self.nc.psum_tensor(self.prefix + name, shape, dt))

    def sem(self, name):
        return self.ctx.enter_context(self.nc.semaphore(name))

    def block(self):
        return self.ctx.enter_context(self.nc.Block())

    def done(self):
        self.ctx.close()
        return self.nc



ENGINES = ("sync", "gpsimd", "vector", "scalar", "tensor")


class Op:
    __slots__ = ("eng", "fn", "deps", "idx", "is_dma", "milestone", "group", "gcount", "_needed")

    def __init__(self, eng, fn, deps, idx, is_dma, group=None, gcount=0):
        self.eng = eng
        self.fn = fn
        self.deps = deps
        self.idx = idx
        self.is_dma = is_dma
        self.group = group
        self.gcount = gcount
        self.milestone = None
        self._needed = False


class Sched:

    def __init__(self, p):
        self.p = p
        self.ops = {e: [] for e in ENGINES}
        self.all = []
        self.gcounts = {}
        self.emitted = {e: 0 for e in ENGINES}
        self.ms_count = {e: 0 for e in ENGINES}
        self.ms_of_idx = {e: {} for e in ENGINES}
        self.ms_pending = {e: [] for e in ENGINES}
        self.sems = {}
        self.gsems = {}
        self.waited_ms = {e: {f: 0 for f in ENGINES} for e in ENGINES}
        self.waited_dma = {e: {} for e in ENGINES}
        self.pending_barrier = []
        self.group_last = {}

    def op(self, eng, fn, deps=()):
        dl = [d for d in deps if d is not None] + self.pending_barrier
        o = Op(eng, fn, dl, len(self.ops[eng]), False)
        self.ops[eng].append(o)
        self.all.append(o)
        return o

    def dma(self, eng, fn, deps=(), group="g"):
        group = eng + ":" + group
        self.gcounts[group] = self.gcounts.get(group, 0) + 16
        dl = [d for d in deps if d is not None] + self.pending_barrier
        o = Op(eng, fn, dl, len(self.ops[eng]), True, group=group, gcount=self.gcounts[group])
        self.ops[eng].append(o)
        self.all.append(o)
        self.group_last[group] = o
        return o

    def next_phase(self):
        nc = self.p.nc
        new_ops = {e: self.ops[e][self.emitted[e]:] for e in ENGINES}
        barrier = []
        groups_touched = set()
        for e in ENGINES:
            if new_ops[e]:
                barrier.append(new_ops[e][-1])
            for o in new_ops[e]:
                if o.is_dma:
                    groups_touched.add(o.group)
        for g in groups_touched:
            barrier.append(self.group_last[g])
        for e in ENGINES:
            for o in new_ops[e]:
                for d in o.deps:
                    if not d.is_dma and not (d.eng == "tensor" and o.eng == "tensor"):
                        d._needed = True
        for t in barrier:
            if not t.is_dma and t.eng != "tensor":
                t._needed = True
        for e in ENGINES:
            for o in new_ops[e]:
                if o.is_dma:
                    continue
                self.ms_pending[e].append(o)
                if o._needed:
                    self.ms_count[e] += 1
                    o.milestone = self.ms_count[e]
                    for q in self.ms_pending[e]:
                        self.ms_of_idx[e][q.idx] = self.ms_count[e]
                    self.ms_pending[e] = []
        for e in ENGINES:
            if new_ops[e] and e not in self.sems:
                self.sems[e] = self.p.sem("S_" + e)
        for g in groups_touched:
            if g not in self.gsems:
                self.gsems[g] = self.p.sem("G_" + g.replace(":", "_"))
        sched = self

        def section(eng_name):
            ops = new_ops[eng_name]

            def run(engine):
                wm = sched.waited_ms[eng_name]
                wd = sched.waited_dma[eng_name]
                for o in ops:
                    for d in o.deps:
                        if d.is_dma:
                            if wd.get(d.group, 0) < d.gcount:
                                engine.wait_ge(sched.gsems[d.group], d.gcount)
                                wd[d.group] = d.gcount
                        elif not (d.eng == "tensor" and eng_name == "tensor"):
                            m = sched.ms_of_idx[d.eng].get(d.idx)
                            assert m is not None, f"dep of {eng_name} on {d.eng}#{d.idx} lacks milestone"
                            if d.eng == eng_name and d.idx < o.idx and m > sched.ms_count[eng_name]:
                                m = None
                            if m is not None and wm[d.eng] < m:
                                if not (d.eng == eng_name and sched._self_ok(d, o)):
                                    engine.wait_ge(sched.sems[d.eng], m)
                                    wm[d.eng] = m
                    inst = o.fn(engine)
                    if o.is_dma:
                        inst.then_inc(sched.gsems[o.group], 16)
                    elif o.milestone is not None:
                        inst.then_inc(sched.sems[o.eng], 1)
            return run

        with nc.Block() as block:
            for e in ENGINES:
                if not new_ops[e]:
                    continue
                run = section(e)
                getattr(block, e)(run)
        for e in ENGINES:
            self.emitted[e] = len(self.ops[e])
        self.pending_barrier = barrier

    def _self_ok(self, d, o):
        return False

    def emit_tail(self, out_dmas):
        nc = self.p.nc
        tail = {}
        for t in out_dmas:
            tail[t.group] = max(tail.get(t.group, 0), self.gcounts[t.group])
        sched = self

        with nc.Block() as block:
            def run(engine):
                for g, c in tail.items():
                    engine.wait_ge(sched.gsems[g], c)
            block.sync(run)

    def finalize(self, out_dmas):
        self.next_phase()
        self.emit_tail(out_dmas)



def _phase_io(p, io, name, shape, dt=F32, out=False):
    if io is not None:
        return io[name]
    return (p.dram_out if out else p.dram_in)(name, shape, dt)


def build_ln(p=None, S=None, io=None):
    standalone = p is None
    if standalone:
        p = Prog("ln")
        S = Sched(p)
    x = _phase_io(p, io, "x", [D, NTOK])
    xn = _phase_io(p, io, "xn", [D, NTOK], BF16, out=True)
    xn32 = _phase_io(p, io, "xn32", [D, NTOK], out=True)

    xt = [p.sb(f"xt{i}", [128, TK]) for i in range(2)]
    sqt = [p.sb(f"sqt{i}", [128, TK]) for i in range(2)]
    x2 = [p.sb(f"x2_{i}", [128, TK]) for i in range(2)]
    xnt = [p.sb(f"xnt{i}", [128, TK], BF16) for i in range(2)]
    xnt32 = [p.sb(f"xnt32_{i}", [128, TK]) for i in range(2)]
    tmpt = [p.sb(f"tmpt{i}", [128, TK]) for i in range(2)]
    ones = p.sb("ones", [128, 1])
    eps1 = p.sb("eps1", [1, 1])
    ones_r = p.sb("ones_r", [1, 128])
    Srow = p.sb("Srow", [1, NTOK])
    Qrow = p.sb("Qrow", [1, NTOK])
    mu = p.sb("mu", [1, NTOK])
    var = p.sb("var", [1, NTOK])
    rstd = p.sb("rstd", [1, NTOK])
    psS = [p.ps(f"psS{i}", [1, TK]) for i in range(2)]
    psQ = [p.ps(f"psQ{i}", [1, TK]) for i in range(2)]
    psB = [p.ps(f"psB{i}", [128, TK]) for i in range(2)]

    t_o1 = S.op("vector", lambda v: v.memset(ones[:], 1.0))
    t_o2 = S.op("vector", lambda v: v.memset(eps1[:], float(LN_EPS)))
    t_o3 = S.op("vector", lambda v: v.memset(ones_r[:], 1.0))

    ra = lambda a: a.rearrange("(c p) t -> c p t", p=128)
    war = {}
    i = 0
    ev_toks = []
    for tk in range(NTK):
        mm_last = None
        for dc in range(DC):
            b = i % 2
            tx = S.dma("gpsimd", lambda g, dc=dc, tk=tk, b=b: g.dma_start(
                out=xt[b][:], in_=ra(x)[dc, :, tk * TK:(tk + 1) * TK]),
                deps=[war.get(("mmx", b))], group="lx" + str(b))
            sq = S.op("scalar", lambda s, b=b: s.activation(out=sqt[b][:], in_=xt[b][:], func=AF.Square),
                      deps=[tx, war.get(("mmq", b))])
            m1 = S.op("tensor", lambda t, b=b, dc=dc, tk=tk: t.matmul(
                psS[tk % 2][:], ones[:], xt[b][:], start=(dc == 0), stop=(dc == DC - 1)),
                deps=[tx, t_o1, war.get(("evS", tk % 2)) if dc == 0 else None])
            m2 = S.op("tensor", lambda t, b=b, dc=dc, tk=tk: t.matmul(
                psQ[tk % 2][:], ones[:], sqt[b][:], start=(dc == 0), stop=(dc == DC - 1)),
                deps=[sq, war.get(("evQ", tk % 2)) if dc == 0 else None])
            war[("mmx", b)] = m2
            war[("mmq", b)] = m2
            mm_last = m2
            i += 1
        e1 = S.op("vector", lambda v, tk=tk: v.tensor_copy(
            out=Srow[:, tk * TK:(tk + 1) * TK], in_=psS[tk % 2][:]), deps=[mm_last])
        e2 = S.op("vector", lambda v, tk=tk: v.tensor_copy(
            out=Qrow[:, tk * TK:(tk + 1) * TK], in_=psQ[tk % 2][:]), deps=[mm_last, e1])
        war[("evS", tk % 2)] = e1
        war[("evQ", tk % 2)] = e2
        ev_toks.append(e2)
    r1 = S.op("vector", lambda v: v.tensor_scalar(out=mu[:], in0=Srow[:], scalar1=1.0 / D,
                                                  scalar2=None, op0=AluOpType.mult), deps=ev_toks[-1:])
    r2 = S.op("vector", lambda v: v.tensor_tensor(out=var[:], in0=mu[:], in1=mu[:],
                                                  op=AluOpType.mult), deps=[r1])
    r3 = S.op("vector", lambda v: v.scalar_tensor_tensor(out=var[:], in0=Qrow[:], scalar=1.0 / D,
                                                         in1=var[:], op0=AluOpType.mult,
                                                         op1=AluOpType.subtract), deps=[r2])
    r4 = S.op("scalar", lambda s: s.activation(out=var[:], in_=var[:], func=AF.Ln, bias=eps1[:]),
              deps=[r3, t_o2])
    r5 = S.op("scalar", lambda s: s.activation(out=rstd[:], in_=var[:], func=AF.Exp, scale=-0.5),
              deps=[r4])
    outs = []
    i = 0
    for tk in range(NTK):
        bc1 = S.op("tensor", lambda t, tk=tk: t.matmul(
            psB[0][:], ones_r[:], mu[:, tk * TK:(tk + 1) * TK], start=True, stop=True),
            deps=[r1, t_o3, war.get("sub_last")])
        bc2 = S.op("tensor", lambda t, tk=tk: t.matmul(
            psB[1][:], ones_r[:], rstd[:, tk * TK:(tk + 1) * TK], start=True, stop=True),
            deps=[r5, t_o3, war.get("mul_last")])
        for dc in range(DC):
            b = i % 2
            tx2 = S.dma("gpsimd", lambda g, dc=dc, tk=tk, b=b: g.dma_start(
                out=x2[b][:], in_=ra(x)[dc, :, tk * TK:(tk + 1) * TK]),
                deps=[war.get(("sub2", b))], group="l2" + str(b))
            o1 = S.op("vector", lambda v, b=b: v.tensor_tensor(
                out=tmpt[b][:], in0=x2[b][:], in1=psB[0][:], op=AluOpType.subtract),
                deps=[tx2, bc1, war.get(("mul2", b))])
            o2 = S.op("vector", lambda v, b=b: v.tensor_tensor(
                out=xnt32[b][:], in0=tmpt[b][:], in1=psB[1][:], op=AluOpType.mult),
                deps=[o1, bc2, war.get(("st32", b))])
            o3 = S.op("vector", lambda v, b=b: v.tensor_copy(out=xnt[b][:], in_=xnt32[b][:]),
                      deps=[o2, war.get(("stbf", b))])
            s1_ = S.dma("sync", lambda sy, dc=dc, tk=tk, b=b: sy.dma_start(
                out=ra(xn)[dc, :, tk * TK:(tk + 1) * TK], in_=xnt[b][:]),
                deps=[o3], group="lob" + str(b))
            s2_ = S.dma("sync", lambda sy, dc=dc, tk=tk, b=b: sy.dma_start(
                out=ra(xn32)[dc, :, tk * TK:(tk + 1) * TK], in_=xnt32[b][:]),
                deps=[o2, o3], group="lo3" + str(b))
            war[("sub2", b)] = o1
            war[("mul2", b)] = o2
            war[("st32", b)] = s2_
            war[("stbf", b)] = s1_
            war["sub_last"] = o1
            war["mul_last"] = o2
            outs += [s1_, s2_]
            i += 1
    if standalone:
        S.finalize(outs[-4:])
        return p.done()
    return outs



def build_l0att(p=None, S=None, io=None):
    TKA = 256
    NCH = NTOK // TKA
    G = 64
    NG = TKA // G
    standalone = p is None
    if standalone:
        p = Prog("l0att")
        S = Sched(p)
    nc = p.nc
    xn = _phase_io(p, io, "xn", [D, NTOK], BF16)
    wq = _phase_io(p, io, "wq", [D, D])
    wd = _phase_io(p, io, "wd", [D, 256])
    wu = _phase_io(p, io, "wu", [256, 2048])
    wo = _phase_io(p, io, "wo", [D, D])
    bqr = _phase_io(p, io, "bqr", [1, D])
    bdr = _phase_io(p, io, "bdr", [1, 256])
    bur = _phase_io(p, io, "bur", [1, 2048])
    bor = _phase_io(p, io, "bor", [1, D])
    ropec = _phase_io(p, io, "ropec", [128, T])
    ropes = _phase_io(p, io, "ropes", [128, T])
    s0 = _phase_io(p, io, "s0", [D, NTOK], out=True)

    wq_b = p.sb("wq_b", [128, DC, D], BF16)
    wd_b = p.sb("wd_b", [128, DC, 256], BF16)
    wu_b = p.sb("wu_b", [128, 2, 2048], BF16)
    wo_b = p.sb("wo_b", [128, DC, D], BF16)
    bq_s = p.sb("bq_s", [1, D])
    bd_s = p.sb("bd_s", [1, 256])
    bu_s = p.sb("bu_s", [1, 2048])
    bo_s = p.sb("bo_s", [1, D])
    onesTK = p.sb("onesTK", [1, 256])
    rc_s = p.sb("rc_s", [128, T])
    rs_s = p.sb("rs_s", [128, T])
    ones16 = p.sb("ones16", [16, 1])
    onesr16 = p.sb("onesr16", [1, 16])

    xnc = p.sb("xnc", [128, DC, TKA], BF16)
    qT = p.sb("qT", [128, DC, TKA])
    kT = p.sb("kT", [128, DC, TKA])
    vT = p.sb("vT", [128, DC, TKA])
    kvl = p.sb("kvl", [128, 2, TKA], BF16)
    swp = p.sb("swp", [128, DC, TKA])
    tmpq = p.sb("tmpq", [128, DC * TKA])
    tmpk = tmpq
    Qb = p.sb("Qb", [64, H * TKA])
    Kb = p.sb("Kb", [64, H * TKA])
    Vb = p.sb("Vb", [16, DH * 64])
    exps = p.sb("exps", [16, G * 16])
    recip = p.sb("recip", [1, G * 16])
    attsm = p.sb("attsm", [16, G * 16])
    ao_bf = p.sb("ao_bf", [128, DC, TKA], BF16)
    s0t = [p.sb(f"s0t{i}", [128, TKA]) for i in range(2)]

    psA = p.ps("psA", [128, TKA])[:]
    psB = p.ps("psB", [128, TKA])[:]
    attps = p.ps("attps", [16, G * 16])
    nsps = p.ps("nsps", [1, 512])
    repps = p.ps("repps", [16, 512])
    outps = p.ps("outps", [64, G * 16])

    S.dma("gpsimd", lambda g: g.dma_start(out=wq_b[:], in_=wq.rearrange("(c p) o -> p c o", p=128)), group="setup")
    S.dma("gpsimd", lambda g: g.dma_start(out=wd_b[:], in_=wd.rearrange("(c p) o -> p c o", p=128)), group="setup")
    S.dma("gpsimd", lambda g: g.dma_start(out=wu_b[:], in_=wu.rearrange("(c p) o -> p c o", p=128)), group="setup")
    t_setup_g = S.dma("gpsimd", lambda g: g.dma_start(out=wo_b[:], in_=wo.rearrange("(c p) o -> p c o", p=128)), group="setup")
    S.dma("sync", lambda s: s.dma_start(out=bq_s[:], in_=bqr[:]), group="setup")
    S.dma("sync", lambda s: s.dma_start(out=bd_s[:], in_=bdr[:]), group="setup")
    S.dma("sync", lambda s: s.dma_start(out=bu_s[:], in_=bur[:]), group="setup")
    S.dma("sync", lambda s: s.dma_start(out=bo_s[:], in_=bor[:]), group="setup")
    S.dma("sync", lambda s: s.dma_start(out=rc_s[:], in_=ropec[:]), group="setup")
    t_setup = S.dma("sync", lambda s: s.dma_start(out=rs_s[:], in_=ropes[:]), group="setup")
    t_wq = t_wd = t_wu = t_wo = t_setup_g
    t_bq = t_bd = t_bu = t_bo = t_rc = t_rs = t_setup
    t_o1 = S.op("vector", lambda v: v.memset(ones16[:], 1.0))
    t_o2 = S.op("vector", lambda v: v.memset(onesr16[:], 1.0))
    t_o3 = S.op("vector", lambda v: v.memset(onesTK[:], 1.0))

    last_xnc_use = None
    last_qkv_use = [None]
    last_blk_use = None
    last_exp = None
    last_ns_use = None
    last_rep_use = None
    last_out_use = None
    last_wout = None
    last_s0_store = [None, None]
    last_evacA = None
    last_evacB = None
    last_ropek_done = [None]

    def MM(ps_, lhsT, rhs, start, stop, deps=()):
        return S.op("tensor", lambda t, a=ps_, b=lhsT, c=rhs, s1=start, s2=stop:
                    t.matmul(a, b, c, start=s1, stop=s2), deps=deps)

    for tk in range(NCH):
        b = tk // (T // TKA)
        tau0 = (tk % (T // TKA)) * TKA
        sl = slice(tk * TKA, (tk + 1) * TKA)

        t_xnc = S.dma("gpsimd", lambda g, sl=sl: g.dma_start(
            out=xnc[:], in_=xn.rearrange("(c p) t -> p c t", p=128)[:, :, sl]),
            deps=[last_xnc_use], group="xnc")

        q_evacs, kvl_evacs, k_evacs, v_evacs = [], [], [], []
        pj = 0

        def do_proj(w_b, w_tok, nin, rhs_sb, rhs_tok, ocol, dst_fn, bias_s, bias_tok, evlist):
            nonlocal pj, last_evacA, last_evacB
            ps_ = psA if (pj % 2 == 0) else psB
            war = last_evacA if (pj % 2 == 0) else last_evacB
            for ic in range(nin):
                deps = [rhs_tok, w_tok]
                if ic == 0:
                    deps.append(war)
                MM(ps_, w_b[:, ic, ocol * 128:(ocol + 1) * 128], rhs_sb[:, ic, :],
                   start=(ic == 0), stop=False, deps=deps)
            mmt = MM(ps_, bias_s[:, ocol * 128:(ocol + 1) * 128], onesTK[:],
                     start=False, stop=True, deps=[bias_tok, t_o3])
            ev = S.op("scalar", lambda s, ps2=ps_: s.copy(out=dst_fn(), in_=ps2),
                      deps=[mmt, last_qkv_use[0]])
            evlist.append(ev)
            if pj % 2 == 0:
                last_evacA = ev
            else:
                last_evacB = ev
            pj += 1

        for oc in range(DC):
            do_proj(wq_b, t_wq, DC, xnc, t_xnc, oc,
                    (lambda oc=oc: qT[:, oc, :]), bq_s, t_bq, q_evacs)
        for oc in range(2):
            do_proj(wd_b, t_wd, DC, xnc, t_xnc, oc,
                    (lambda oc=oc: kvl[:, oc, :]), bd_s, t_bd, kvl_evacs)
        for j in range(16):
            dst = kT if j < 8 else vT
            oc = j % 8
            do_proj(wu_b, t_wu, 2, kvl, kvl_evacs[-1], j,
                    (lambda dst=dst, oc=oc: dst[:, oc, :]), bu_s, t_bu,
                    k_evacs if j < 8 else v_evacs)

        SWAPS = [(32, 0), (0, 32), (96, 64), (64, 96)]
        qsw = []
        for dc in range(DC):
            for (src0, dst0) in SWAPS:
                qsw.append(S.dma("sync", lambda s, dc=dc, s0_=src0, d0=dst0: s.dma_start(
                    out=swp[d0:d0 + 32, dc, :], in_=qT[s0_:s0_ + 32, dc, :]),
                    deps=[q_evacs[dc], last_ropek_done[0]], group="qsw"))
        ropeq = []
        for dc in range(DC):
            a1 = S.op("vector", lambda v, dc=dc: v.tensor_tensor(
                out=tmpq[:, dc * TKA:(dc + 1) * TKA], in0=qT[:, dc, :],
                in1=rc_s[:, tau0:tau0 + TKA], op=AluOpType.mult),
                deps=[q_evacs[dc], t_rc, qsw[-1]])
            a2 = S.op("vector", lambda v, dc=dc: v.tensor_tensor(
                out=qT[:, dc, :], in0=swp[:, dc, :],
                in1=rs_s[:, tau0:tau0 + TKA], op=AluOpType.mult),
                deps=[qsw[-1], t_rs, a1])
            a3 = S.op("vector", lambda v, dc=dc: v.tensor_tensor(
                out=qT[:, dc, :], in0=qT[:, dc, :],
                in1=tmpq[:, dc * TKA:(dc + 1) * TKA], op=AluOpType.add), deps=[a1, a2])
            ropeq.append(a3)
        ksw = []
        for dc in range(DC):
            for (src0, dst0) in SWAPS:
                ksw.append(S.dma("sync", lambda s, dc=dc, s0_=src0, d0=dst0: s.dma_start(
                    out=swp[d0:d0 + 32, dc, :], in_=kT[s0_:s0_ + 32, dc, :]),
                    deps=[k_evacs[dc], ropeq[-1] if dc == 0 else None, ropeq[dc]], group="ksw"))
        ropek = []
        for dc in range(DC):
            a1 = S.op("gpsimd", lambda g, dc=dc: g.tensor_tensor(
                out=tmpk[:, dc * TKA:(dc + 1) * TKA], in0=kT[:, dc, :],
                in1=rc_s[:, tau0:tau0 + TKA], op=AluOpType.mult),
                deps=[k_evacs[dc], t_rc, ksw[-1], ropeq[-1]])
            a2 = S.op("gpsimd", lambda g, dc=dc: g.tensor_tensor(
                out=kT[:, dc, :], in0=swp[:, dc, :],
                in1=rs_s[:, tau0:tau0 + TKA], op=AluOpType.mult),
                deps=[ksw[-1], t_rs, a1])
            a3 = S.op("gpsimd", lambda g, dc=dc: g.tensor_tensor(
                out=kT[:, dc, :], in0=kT[:, dc, :],
                in1=tmpk[:, dc * TKA:(dc + 1) * TKA], op=AluOpType.add), deps=[a1, a2])
            ropek.append(a3)
        last_ropek_done[0] = ropek[-1]

        asmQ, asmK = [], []
        for h in range(H):
            asmQ.append(S.dma("gpsimd", lambda g, h=h: g.dma_start(
                out=Qb[:, h * TKA:(h + 1) * TKA],
                in_=qT[(h % 2) * 64:(h % 2) * 64 + 64, h // 2, :]),
                deps=[ropeq[h // 2], last_blk_use], group="asm"))
            asmK.append(S.dma("gpsimd", lambda g, h=h: g.dma_start(
                out=Kb[:, h * TKA:(h + 1) * TKA],
                in_=kT[(h % 2) * 64:(h % 2) * 64 + 64, h // 2, :]),
                deps=[ropek[h // 2], last_blk_use], group="asm"))
        stab = []
        prev = None
        for g_ in list(range(15, 0, -1)) + [0]:
            prev = S.op("vector", lambda v, g_=g_: v.tensor_tensor(
                out=Kb[:, g_ * TKA:(g_ + 1) * TKA], in0=Kb[:, g_ * TKA:(g_ + 1) * TKA],
                in1=Kb[:, 0:TKA], op=AluOpType.subtract),
                deps=[asmK[g_], asmK[0]] + ([prev] if prev else []))
            stab.append(prev)
        stab_done = stab[-1]

        p2_evac_tokens = []
        att_last_group = None
        for gi in range(NG):
            t0_ = gi * G
            asmV = []
            for h in range(H):
                asmV.append(S.dma("gpsimd", lambda g, h=h, gi=gi: g.dma_start(
                    out=Vb[h:h + 1, :],
                    in_=vT[(h % 2) * 64:(h % 2) * 64 + 64, h // 2, gi * G:(gi + 1) * G]),
                    deps=[v_evacs[h // 2], last_blk_use], group="asmv"))
            att_mms = []
            for ti in range(G):
                tok = t0_ + ti
                deps = [stab_done, asmK[-1]]
                if ti == 0:
                    deps += [last_exp, att_last_group]
                att_mms.append(S.op("tensor", lambda t, tok=tok, ti=ti: t.matmul(
                    attps[:, ti * 16:(ti + 1) * 16],
                    Kb.rearrange("p (h t) -> p t h", t=TKA)[:, tok, :],
                    Qb.rearrange("p (h t) -> p t h", t=TKA)[:, tok, :],
                    start=True, stop=True), deps=deps))
            att_last_group = att_mms[-1]
            exp_t = S.op("scalar", lambda s: s.activation(
                out=exps[:], in_=attps[:], func=AF.Exp, scale=0.125),
                deps=[att_mms[-1], last_rep_use, last_ns_use])
            last_exp = exp_t
            rec_ts = []
            for hf in range(2):
                ns_mm = S.op("tensor", lambda t, hf=hf: t.matmul(
                    nsps[:], ones16[:],
                    exps[:, hf * 512:(hf + 1) * 512], start=True, stop=True),
                    deps=[exp_t, t_o1, last_ns_use])
                rec_t = S.op("vector", lambda v, hf=hf: v.reciprocal(
                    out=recip[:, hf * 512:(hf + 1) * 512], in_=nsps[:]), deps=[ns_mm])
                last_ns_use = rec_t
                rec_ts.append(rec_t)
            sm_t = None
            for hf in range(2):
                rep_mm = S.op("tensor", lambda t, hf=hf: t.matmul(
                    repps[:], onesr16[:],
                    recip[:, hf * 512:(hf + 1) * 512], start=True, stop=True),
                    deps=[rec_ts[hf], t_o2, last_rep_use])
                sm_t = S.op("vector", lambda v, hf=hf: v.tensor_tensor(
                    out=attsm[:, hf * 512:(hf + 1) * 512],
                    in0=exps[:, hf * 512:(hf + 1) * 512], in1=repps[:], op=AluOpType.mult),
                    deps=[rep_mm, exp_t])
                last_rep_use = sm_t
            p2_mms = []
            for ti in range(G):
                tok = t0_ + ti
                deps = [sm_t, asmV[-1]]
                if ti == 0:
                    deps.append(last_out_use)
                p2_mms.append(S.op("tensor", lambda t, ti=ti: t.matmul(
                    outps[:, ti * 16:(ti + 1) * 16],
                    Vb.rearrange("p (d t) -> p t d", t=G)[:, ti, :],
                    attsm[:, ti * 16:(ti + 1) * 16],
                    start=True, stop=True), deps=deps))
            last_blk_use = p2_mms[-1]
            evs = []
            for par in range(2):
                evs.append(S.op("vector", lambda v, par=par, gi=gi: v.tensor_copy(
                    out=ao_bf[par * 64:par * 64 + 64, :, gi * G:(gi + 1) * G],
                    in_=outps.rearrange("p (t h) -> p h t", h=16)[:, par::2, :]),
                    deps=[p2_mms[-1], last_wout]))
            last_out_use = evs[-1]
            p2_evac_tokens += evs

        wout_mms_last = None
        for oc in range(DC):
            ps_ = psA if (oc % 2 == 0) else psB
            war = last_evacA if (oc % 2 == 0) else last_evacB
            for dc in range(DC):
                deps = [p2_evac_tokens[-1], t_wo]
                if dc == 0:
                    deps.append(war)
                MM(ps_, wo_b[:, dc, oc * 128:(oc + 1) * 128], ao_bf[:, dc, :],
                   start=(dc == 0), stop=False, deps=deps)
            mmt = MM(ps_, bo_s[:, oc * 128:(oc + 1) * 128], onesTK[:],
                     start=False, stop=True, deps=[t_bo, t_o3])
            ev = S.op("scalar", lambda s, ps2=ps_, oc=oc: s.copy(out=s0t[oc % 2][:], in_=ps2),
                      deps=[mmt, last_s0_store[oc % 2]])
            st_ = S.dma("sync", lambda s, oc=oc, sl=sl: s.dma_start(
                out=s0.rearrange("(c p) t -> c p t", p=128)[oc, :, sl], in_=s0t[oc % 2][:]),
                deps=[ev], group=("s0a" if oc % 2 == 0 else "s0b"))
            last_s0_store[oc % 2] = st_
            wout_mms_last = mmt
            if oc % 2 == 0:
                last_evacA = ev
            else:
                last_evacB = ev
        last_wout = wout_mms_last
        last_xnc_use = kvl_evacs[-1]
        last_qkv_use[0] = asmV[-1]

    if standalone:
        S.finalize([last_s0_store[0], last_s0_store[1]])
        return p.done()
    return [last_s0_store[0], last_s0_store[1]]



def build_router(kind, p=None, S=None, io=None):
    standalone = p is None
    if standalone:
        p = Prog("rt_" + kind)
        S = Sched(p)
    nc = p.nc
    xn = _phase_io(p, io, "xn", [D, NTOK])
    rw = _phase_io(p, io, "rw", [D, E])
    brr = _phase_io(p, io, "brr", [1, E])
    sc8 = _phase_io(p, io, "sc8", [1, E])
    ident = _phase_io(p, io, "ident", [128, 128])
    comb = _phase_io(p, io, "comb", [E, NTOK], out=True)

    rw_s = p.sb("rw_s", [128, DC, E])
    br_s = p.sb("br_s", [1, E])
    sc_s = p.sb("sc_s", [1, E])
    id_s = p.sb("id_s", [128, 128])
    onesr = p.sb("onesr", [1, 128])
    xt = [p.sb(f"xt{i}", [128, DC, 128]) for i in range(2)]
    rt = p.sb("rt", [128, E])
    pr = p.sb("pr", [128, E])
    pr2 = p.sb("pr2", [128, E])
    ge = p.sb("ge", [128, E])
    cu = p.sb("cu", [128, E])
    cu2 = p.sb("cu2", [128, E])
    mk1 = p.sb("mk1", [128, E])
    mk2 = p.sb("mk2", [128, E])
    m1 = p.sb("m1", [128, 1])
    m2 = p.sb("m2", [128, 1])
    nm = p.sb("nm", [128, 1])
    sm = p.sb("sm", [128, 1])
    rc = p.sb("rc", [128, 1])
    ee = p.sb("ee", [128, 2])
    w12 = p.sb("w12", [128, 2])
    cmb = p.sb("cmb", [128, E])
    screp = p.sb("screp", [128, E])
    crow = [p.sb(f"crow{i}", [8, 128]) for i in range(2)]
    psR = p.ps("psR", [128, E])
    psS = p.ps("psS", [128, E])
    psT = p.ps("psT", [128, 128])

    S.dma("sync", lambda s: s.dma_start(out=rw_s[:], in_=rw.rearrange("(c p) e -> p c e", p=128)), group="setup")
    S.dma("sync", lambda s: s.dma_start(out=br_s[:], in_=brr[:]), group="setup")
    S.dma("sync", lambda s: s.dma_start(out=sc_s[:], in_=sc8[:]), group="setup")
    t_set = S.dma("sync", lambda s: s.dma_start(out=id_s[:], in_=ident[:]), group="setup")
    t_on = S.op("vector", lambda v: v.memset(onesr[:], 1.0))
    t_screp_mm = S.op("tensor", lambda t: t.matmul(psS[:], onesr[:], sc_s[:], start=True, stop=True),
                      deps=[t_set, t_on])
    t_screp = S.op("vector", lambda v: v.tensor_copy(out=screp[:], in_=psS[:]), deps=[t_screp_mm])

    NTI = NTOK // 128
    last_store = [None, None]
    last_mm_use = [None]
    prev_chain = None
    for ti in range(NTI):
        sl = slice(ti * 128, (ti + 1) * 128)
        t_x = S.dma("gpsimd", lambda g, sl=sl, ti=ti: g.dma_start(
            out=xt[ti % 2][:], in_=xn.rearrange("(c p) t -> p c t", p=128)[:, :, sl]),
            deps=[last_mm_use[0]], group="xt" + str(ti % 2))
        mmt = None
        for dc in range(DC):
            mmt = S.op("tensor", lambda t, dc=dc, ti=ti: t.matmul(
                psR[:], xt[ti % 2][:, dc, :], rw_s[:, dc, :],
                start=(dc == 0), stop=False), deps=[t_x, t_set, prev_chain])
        mmt = S.op("tensor", lambda t: t.matmul(psR[:], onesr[:], br_s[:],
                                                start=False, stop=True),
                   deps=[t_set, t_on])
        last_mm_use[0] = mmt
        ops = []

        def V(fn, deps):
            o = S.op("vector", fn, deps)
            ops.append(o)
            return o

        cp = V(lambda v, ti=ti: v.tensor_copy(out=rt[:], in_=psR[:]), [mmt, prev_chain])
        if kind == "l0":
            mx = V(lambda v: v.tensor_reduce(out=m1[:], in_=rt[:], axis=AX, op=AluOpType.max), [cp])
            ng = V(lambda v: v.tensor_scalar(out=nm[:], in0=m1[:], scalar1=-1.0, scalar2=None,
                                             op0=AluOpType.mult), [mx])
            ex = S.op("scalar", lambda s: s.activation(out=pr[:], in_=rt[:], func=AF.Exp, bias=nm[:]),
                      deps=[ng, cp])
            sm_ = V(lambda v: v.tensor_reduce(out=sm[:], in_=pr[:], axis=AX, op=AluOpType.add), [ex])
            rc_ = V(lambda v: v.reciprocal(out=rc[:], in_=sm[:]), [sm_])
            nrm = V(lambda v: v.tensor_scalar(out=pr[:], in0=pr[:], scalar1=rc[:], scalar2=None,
                                              op0=AluOpType.mult), [rc_, ex])
            src = pr
        else:
            sg = S.op("scalar", lambda s: s.activation(out=pr[:], in_=rt[:], func=AF.Sigmoid),
                      deps=[cp])
            src = pr
            nrm = sg
        mxa = V(lambda v: v.tensor_reduce(out=m1[:], in_=src[:], axis=AX, op=AluOpType.max), [nrm])
        g1 = V(lambda v: v.tensor_scalar(out=ge[:], in0=src[:], scalar1=m1[:], scalar2=None,
                                         op0=AluOpType.is_ge), [mxa, nrm])
        c1 = V(lambda v: v.tensor_copy(out=cu[:], in_=ge[:]), [g1])
        c2 = V(lambda v: v.tensor_tensor(out=cu2[:, 1:E], in0=cu[:, 1:E], in1=cu[:, 0:E - 1],
                                         op=AluOpType.add), [c1])
        c2b = V(lambda v: v.tensor_copy(out=cu2[:, 0:1], in_=cu[:, 0:1]), [c1, c2])
        c3 = V(lambda v: v.tensor_tensor(out=cu[:, 2:E], in0=cu2[:, 2:E], in1=cu2[:, 0:E - 2],
                                         op=AluOpType.add), [c2, c2b])
        c3b = V(lambda v: v.tensor_copy(out=cu[:, 0:2], in_=cu2[:, 0:2]), [c2, c2b, c3])
        c4 = V(lambda v: v.tensor_tensor(out=cu2[:, 4:E], in0=cu[:, 4:E], in1=cu[:, 0:E - 4],
                                         op=AluOpType.add), [c3, c3b])
        c4b = V(lambda v: v.tensor_copy(out=cu2[:, 0:4], in_=cu[:, 0:4]), [c3, c3b, c4])
        e1 = V(lambda v: v.tensor_scalar(out=cu[:], in0=cu2[:], scalar1=1.5, scalar2=None,
                                         op0=AluOpType.is_le), [c4, c4b])
        mk = V(lambda v: v.tensor_tensor(out=mk1[:], in0=ge[:], in1=cu[:], op=AluOpType.mult),
               [e1, g1])
        if kind == "l0":
            p2_ = V(lambda v: v.scalar_tensor_tensor(out=pr2[:], in0=mk1[:], scalar=-1e9, in1=pr[:],
                                                     op0=AluOpType.mult, op1=AluOpType.add), [mk, nrm])
            mx2 = V(lambda v: v.tensor_reduce(out=m2[:], in_=pr2[:], axis=AX, op=AluOpType.max), [p2_])
            g2 = V(lambda v: v.tensor_scalar(out=ge[:], in0=pr2[:], scalar1=m2[:], scalar2=None,
                                             op0=AluOpType.is_ge), [mx2, p2_, mk])
            d1 = V(lambda v: v.tensor_copy(out=cu[:], in_=ge[:]), [g2, e1])
            d2 = V(lambda v: v.tensor_tensor(out=cu2[:, 1:E], in0=cu[:, 1:E], in1=cu[:, 0:E - 1],
                                             op=AluOpType.add), [d1])
            d2b = V(lambda v: v.tensor_copy(out=cu2[:, 0:1], in_=cu[:, 0:1]), [d1, d2])
            d3 = V(lambda v: v.tensor_tensor(out=cu[:, 2:E], in0=cu2[:, 2:E], in1=cu2[:, 0:E - 2],
                                             op=AluOpType.add), [d2, d2b])
            d3b = V(lambda v: v.tensor_copy(out=cu[:, 0:2], in_=cu2[:, 0:2]), [d2, d2b, d3])
            d4 = V(lambda v: v.tensor_tensor(out=cu2[:, 4:E], in0=cu[:, 4:E], in1=cu[:, 0:E - 4],
                                             op=AluOpType.add), [d3, d3b])
            d4b = V(lambda v: v.tensor_copy(out=cu2[:, 0:4], in_=cu[:, 0:4]), [d3, d3b, d4])
            e2_ = V(lambda v: v.tensor_scalar(out=cu[:], in0=cu2[:], scalar1=1.5, scalar2=None,
                                              op0=AluOpType.is_le), [d4, d4b])
            mk2_ = V(lambda v: v.tensor_tensor(out=mk2[:], in0=ge[:], in1=cu[:], op=AluOpType.mult),
                     [e2_, g2])
            pk1 = V(lambda v: v.tensor_copy(out=ee[:, 0:1], in_=m1[:]), [mxa, mk2_])
            pk2 = V(lambda v: v.tensor_copy(out=ee[:, 1:2], in_=m2[:]), [mx2, pk1])
            exw = S.op("scalar", lambda s: s.activation(out=ee[:], in_=ee[:], func=AF.Exp),
                       deps=[pk2])
            sw = V(lambda v: v.tensor_reduce(out=sm[:], in_=ee[:], axis=AX, op=AluOpType.add), [exw])
            rw_ = V(lambda v: v.reciprocal(out=rc[:], in_=sm[:]), [sw])
            ww = V(lambda v: v.tensor_scalar(out=w12[:], in0=ee[:], scalar1=rc[:], scalar2=None,
                                             op0=AluOpType.mult), [rw_, exw])
            q1 = V(lambda v: v.tensor_scalar(out=mk1[:], in0=mk1[:], scalar1=w12[:, 0:1], scalar2=None,
                                             op0=AluOpType.mult), [ww, mk])
            q2 = V(lambda v: v.scalar_tensor_tensor(out=cmb[:], in0=mk2[:], scalar=w12[:, 1:2],
                                                    in1=mk1[:], op0=AluOpType.mult, op1=AluOpType.add),
                   [q1, mk2_])
            fin = V(lambda v: v.tensor_tensor(out=cmb[:], in0=cmb[:], in1=screp[:], op=AluOpType.mult),
                    [q2, t_screp])
        else:
            fin = V(lambda v: v.tensor_scalar(out=cmb[:], in0=mk1[:], scalar1=m1[:], scalar2=None,
                                              op0=AluOpType.mult), [mk, mxa])
        trm = S.op("tensor", lambda t: t.transpose(psT[0:8, 0:128], cmb[:], id_s[:]),
                   deps=[fin, t_set, last_store[ti % 2]])
        evc = V(lambda v, ti=ti: v.tensor_copy(out=crow[ti % 2][:], in_=psT[0:8, 0:128]), [trm])
        st = S.dma("sync", lambda s, sl=sl, ti=ti: s.dma_start(out=comb[:, sl], in_=crow[ti % 2][:]),
                   deps=[evc], group="cst" + str(ti % 2))
        last_store[ti % 2] = st
        prev_chain = fin

    if standalone:
        S.finalize([last_store[0], last_store[1]])
        return p.done()
    return [last_store[0], last_store[1]]



def build_moe(act, p=None, S=None, io=None):
    TM = 512
    NTS = NTOK // TM
    AFUNC = AF.Silu if act == "silu" else AF.Gelu
    standalone = p is None
    if standalone:
        p = Prog("moe_" + act)
        S = Sched(p)
    nc = p.nc
    xn = _phase_io(p, io, "xn", [D, NTOK], BF16)
    w1 = _phase_io(p, io, "w1", [E, D, FF])
    w2 = _phase_io(p, io, "w2", [E, FF, D])
    comb = _phase_io(p, io, "comb", [E, NTOK])
    ymoe = _phase_io(p, io, "ymoe", [D, NTOK], out=True)
    w1bf = nc.dram_tensor("sc_w1bf_" + act, [E, DC, 128, DC, TM], BF16, kind="Internal").ap()
    w2bf = nc.dram_tensor("sc_w2bf_" + act, [E, DC, 128, 4, D], BF16, kind="Internal").ap()

    w2f = p.sb("w2f", [128, FC, D], BF16)
    w1f = p.sb("w1f", [128, DC, FF], BF16)
    xnt = [p.sb(f"xnt{i}", [128, DC, TM], BF16) for i in range(2)]
    hsb = p.sb("hsb", [128, FC, TM], BF16)
    cstb = p.sb("cstb", [128, DC, TM], BF16)
    cbr = p.sb("cbr", [128, TM])
    comb_sb = p.sb("comb_sb", [1, TM])
    onesr = p.sb("onesr", [1, 128])
    yt = [p.sb("yt0", [128, DC, TM])]
    yps = [p.ps(f"yps{i}", [128, TM]) for i in range(8)]

    t_on = S.op("vector", lambda v: v.memset(onesr[:], 1.0))

    last_w1_use = [None]
    last_xn_use = [None, None]
    last_silu = [None] * 4
    last_yevac = [None] * 8
    last_yt_use = [None]
    last_phaseB = [None]
    last_cl = [None]
    last_yev_last = [None]
    last_cst = [None]
    cast_tok = {}
    region_last = {}
    prev_acc = {}

    def cast_expert(e):
        for kind in ("w1", "w2"):
            for fc in range(DC):
                if kind == "w1":
                    src = w1.rearrange("e (dc p) f -> e p dc f", p=128)[e, :, :, fc * TM:(fc + 1) * TM]
                    dst = w1bf[e, fc]
                    cview = lambda: cstb[:]
                else:
                    src = w2.rearrange("e (fc fp p) o -> e fc p fp o", fc=DC, p=128)[e, fc]
                    dst = w2bf[e, fc]
                    cview = lambda: cstb[:].rearrange("p a b -> p (a b)").rearrange("p (x y) -> p x y", x=4)
                ld = S.dma("gpsimd", lambda g, src=src, cview=cview: g.dma_start(
                    out=cview(), in_=src), deps=[last_cst[0]], group="cst")
                wr = S.dma("gpsimd", lambda g, dst=dst, cview=cview: g.dma_start(
                    out=dst, in_=cview()), deps=[ld], group="cwr")
                last_cst[0] = wr
                cast_tok[(e, kind, fc)] = wr

    for e in range(E):
        if e == 0:
            cast_expert(0)
        if e + 1 < E:
            cast_expert(e + 1)
        w1ld, w2ld = [], []
        for fc in range(DC):
            w1ld.append(S.dma("sync", lambda s, e=e, fc=fc: s.dma_start(
                out=w1f[:, :, fc * TM:(fc + 1) * TM], in_=w1bf[e, fc]),
                deps=[cast_tok[(e, "w1", fc)], last_w1_use[0]], group="w1l"))
        for fc in range(DC):
            w2ld.append(S.dma("sync", lambda s, e=e, fc=fc: s.dma_start(
                out=w2f[:, fc * 4:(fc + 1) * 4, :], in_=w2bf[e, fc]),
                deps=[cast_tok[(e, "w2", fc)], last_phaseB[0]], group="w2l"))
        for ts in range(NTS):
            gi = e * NTS + ts
            xb = gi % 2
            sl = slice(ts * TM, (ts + 1) * TM)
            t_xn = S.dma("sync", lambda s, sl=sl, xb=xb: s.dma_start(
                out=xnt[xb][:], in_=xn.rearrange("(c p) t -> p c t", p=128)[:, :, sl]),
                deps=[last_xn_use[xb]], group="xn" + str(xb))
            t_cl = S.dma("sync", lambda s, e=e, sl=sl: s.dma_start(
                out=comb_sb[:], in_=comb[e:e + 1, sl]),
                deps=[last_cl[0]], group="cmb")
            silu_done = []
            first_hb = [True] * 4
            for fc in range(DC):
                for fp in range(4):
                    fpg = fc * 4 + fp
                    hb = fpg % 4
                    mmt = None
                    for dc in range(DC):
                        deps = [w1ld[fc], t_xn]
                        if dc == 0:
                            deps.append(last_silu[hb])
                            if first_hb[hb]:
                                deps.append(last_yevac[hb])
                                first_hb[hb] = False
                        mmt = S.op("tensor", lambda t, fpg=fpg, dc=dc, hb=hb, xb=xb: t.matmul(
                            yps[hb][:], w1f[:, dc, fpg * 128:(fpg + 1) * 128],
                            xnt[xb][:, dc, :],
                            start=(dc == 0), stop=(dc == DC - 1)), deps=deps)
                    sv = S.op("scalar", lambda s, fpg=fpg, hb=hb: s.activation(
                        out=hsb[:, fpg, :], in_=yps[hb][:], func=AFUNC),
                        deps=[mmt, last_phaseB[0]])
                    last_silu[hb] = sv
                    silu_done.append(sv)
            last_xn_use[xb] = mmt
            if ts == NTS - 1:
                last_w1_use[0] = mmt
            t_cbm = S.op("tensor", lambda t: t.matmul(
                yps[4][:], onesr[:], comb_sb[:],
                start=True, stop=True), deps=[t_on, t_cl, last_yevac[4]])
            e_cb1 = S.op("vector", lambda v: v.tensor_copy(
                out=cbr[:], in_=yps[4][:]), deps=[t_cbm, last_yev_last[0]])
            last_cl[0] = t_cbm
            chain_last = [None] * 8
            for fpg in range(FC):
                for dp in range(DC):
                    deps = [silu_done[-1], w2ld[fpg // 4]]
                    if fpg == 0:
                        deps.append(last_yevac[dp])
                        if dp == 4:
                            deps.append(e_cb1)
                    mmt = S.op("tensor", lambda t, dp=dp, fpg=fpg: t.matmul(
                        yps[dp][:], w2f[:, fpg, dp * 128:(dp + 1) * 128],
                        hsb[:, fpg, :],
                        start=(fpg == 0), stop=(fpg == FC - 1)), deps=deps)
                    chain_last[dp] = mmt
            for dp in range(DC):
                ev = S.op("vector", lambda v, dp=dp: v.tensor_tensor(
                    out=yt[0][:, dp, :], in0=yps[dp][:], in1=cbr[:],
                    op=AluOpType.mult),
                    deps=[chain_last[dp], e_cb1, last_yt_use[0]])
                last_yevac[dp] = ev
                last_yev_last[0] = ev
            acc = S.dma("gpsimd", lambda g, e=e, ts=ts: g.dma_start(
                out=ymoe.rearrange("(c p) t -> p c t", p=128)[:, :, ts * TM:(ts + 1) * TM],
                in_=yt[0][:], accum_op=(AluOpType.add if e > 0 else AluOpType.bypass)),
                deps=[last_yevac[0], last_yevac[1], last_yevac[2], last_yevac[3],
                      last_yevac[4], last_yevac[5], last_yevac[6], last_yevac[7],
                      prev_acc.get(ts)],
                group=f"acc{ts % 4}")
            prev_acc[ts] = acc
            last_yt_use[0] = acc
            region_last[ts % 4] = acc
            last_phaseB[0] = mmt
    if standalone:
        S.finalize(list(region_last.values()))
        return p.done()
    return list(region_last.values())



def build_ep(p=None, S=None, io=None):
    standalone = p is None
    if standalone:
        p = Prog("ep")
        S = Sched(p)
    x = _phase_io(p, io, "x", [D, NTOK])
    sT = _phase_io(p, io, "sT", [D, NTOK])
    ym = _phase_io(p, io, "ym", [D, NTOK])
    x1 = _phase_io(p, io, "x1", [D, NTOK], out=True)
    NB = 3
    xt = [p.sb(f"xt{i}", [128, DC, TK]) for i in range(NB)]
    st_ = [p.sb(f"st{i}", [128, DC, TK]) for i in range(NB)]
    yt = [p.sb(f"yt{i}", [128, DC, TK]) for i in range(NB)]
    gt = [p.sb(f"gt{i}", [128, DC, TK]) for i in range(NB)]
    last = {}
    outs = []
    ra = lambda a: a.rearrange("(c p) t -> p c t", p=128)
    for tk in range(NTK):
        b = tk % NB
        sl = slice(tk * TK, (tk + 1) * TK)
        tx = S.dma("sync", lambda s, sl=sl, b=b: s.dma_start(
            out=xt[b][:], in_=ra(x)[:, :, sl]), deps=[last.get(("o1", b))], group="ex" + str(b))
        ts_ = S.dma("sync", lambda s, sl=sl, b=b: s.dma_start(
            out=st_[b][:], in_=ra(sT)[:, :, sl]), deps=[last.get(("o0", b))], group="es" + str(b))
        ty = S.dma("gpsimd", lambda g, sl=sl, b=b: g.dma_start(
            out=yt[b][:], in_=ra(ym)[:, :, sl]), deps=[last.get(("g", b))], group="ey" + str(b))
        tg = S.op("scalar", lambda s, b=b: s.activation(out=gt[b][:], in_=yt[b][:], func=AF.Silu),
                  deps=[ty, last.get(("o1", b))])
        o0 = S.op("vector", lambda v, b=b: v.tensor_tensor(out=gt[b][:], in0=st_[b][:], in1=gt[b][:],
                                                           op=AluOpType.mult),
                  deps=[tg, ts_])
        o1 = S.op("vector", lambda v, b=b: v.tensor_tensor(out=gt[b][:], in0=xt[b][:], in1=gt[b][:],
                                                           op=AluOpType.add),
                  deps=[o0, tx, last.get(("st", b))])
        so = S.dma("gpsimd", lambda g, sl=sl, b=b: g.dma_start(
            out=ra(x1)[:, :, sl], in_=gt[b][:]), deps=[o1], group="eo" + str(b))
        last[("g", b)] = tg
        last[("o0", b)] = o0
        last[("o1", b)] = o1
        last[("st", b)] = so
        outs.append(so)
    if standalone:
        S.finalize(outs[-NB:])
        return p.done()
    return outs[-NB:]



def build_l1gate(p=None, S=None, io=None):
    WC = 24
    standalone = p is None
    if standalone:
        p = Prog("l1gate")
        S = Sched(p)
    xn = _phase_io(p, io, "xn", [D, NTOK], BF16)
    win = _phase_io(p, io, "win", [D, WIDTH])
    binr = _phase_io(p, io, "binr", [1, WIDTH])
    taps = _phase_io(p, io, "taps", [128, WC, 3])
    sbr = _phase_io(p, io, "sbr", [128, WC])
    vT = _phase_io(p, io, "vT", [D, NTOK], out=True)
    x0T = _phase_io(p, io, "x0T", [D, NTOK], out=True)

    win_b = p.sb("win_b", [128, DC, WIDTH], BF16)
    bin_s = p.sb("bin_s", [1, WIDTH])
    taps_s = p.sb("taps_s", [128, WC, 3])
    sb_s = p.sb("sb_s", [128, WC])
    onesT = p.sb("onesT", [1, TK])
    xnt = [p.sb(f"xnt{i}", [128, DC, TK], BF16) for i in range(2)]
    u = [p.sb(f"u{i}", [128, T]) for i in range(3)]
    uc = [p.sb(f"uc{i}", [128, T]) for i in range(3)]
    psA = p.ps("psA", [128, TK])
    psB = p.ps("psB", [128, TK])

    S.dma("gpsimd", lambda g: g.dma_start(out=win_b[:], in_=win.rearrange("(c p) o -> p c o", p=128)), group="setup")
    S.dma("sync", lambda s: s.dma_start(out=bin_s[:], in_=binr[:]), group="setup")
    S.dma("sync", lambda s: s.dma_start(out=taps_s[:], in_=taps[:]), group="setup")
    t_set = S.dma("sync", lambda s: s.dma_start(out=sb_s[:], in_=sbr[:]), group="setup")
    t_setg = S.ops["gpsimd"][-1]
    t_on = S.op("vector", lambda v: v.memset(onesT[:], 1.0))

    NTB = T // TK
    last_ev = {0: None, 1: None}
    last_u_use = [None, None, None]
    last_uc_use = [None, None, None]
    last_xn_use = [None, None]
    outs = []
    for b in range(BPC):
        for j in range(DC):
            cols = [j, 8 + j, 16 + j]
            uev = {}
            for tc in range(NTB):
                xb = tc % 2
                sl = slice(b * T + tc * TK, b * T + (tc + 1) * TK)
                t_x = S.dma("sync", lambda s, sl=sl, xb=xb: s.dma_start(
                    out=xnt[xb][:], in_=xn.rearrange("(c p) t -> p c t", p=128)[:, :, sl]),
                    deps=[last_xn_use[xb]], group="xn" + str(xb))
                for ci, wc in enumerate(cols):
                    pj = (tc * 3 + ci) % 2
                    ps_ = psA if pj == 0 else psB
                    war = last_ev[pj]
                    mmt = None
                    for dc in range(DC):
                        deps = [t_x, t_setg]
                        if dc == 0:
                            deps.append(war)
                        mmt = S.op("tensor", lambda t, dc=dc, wc=wc, xb=xb, ps_=ps_: t.matmul(
                            ps_[:], win_b[:, dc, wc * 128:(wc + 1) * 128], xnt[xb][:, dc, :],
                            start=(dc == 0), stop=False), deps=deps)
                    mmt = S.op("tensor", lambda t, wc=wc, ps_=ps_: t.matmul(
                        ps_[:], bin_s[:, wc * 128:(wc + 1) * 128], onesT[:],
                        start=False, stop=True), deps=[t_set, t_on])
                    ev = S.op("scalar", lambda s, ci=ci, tc=tc, ps_=ps_: s.copy(
                        out=u[ci][:, tc * TK:(tc + 1) * TK], in_=ps_[:]),
                        deps=[mmt, last_u_use[ci]])
                    uev[(ci, tc)] = ev
                    last_ev[pj] = ev
                last_xn_use[xb] = mmt
            conv_done = []
            for ci, wc in enumerate(cols):
                dep_all = [uev[(ci, tc)] for tc in range(NTB)]
                c1 = S.op("vector", lambda v, ci=ci, wc=wc: v.tensor_scalar(
                    out=uc[ci][:], in0=u[ci][:], scalar1=taps_s[:, wc, 2:3], scalar2=sb_s[:, wc:wc + 1],
                    op0=AluOpType.mult, op1=AluOpType.add),
                    deps=dep_all + [t_set, last_uc_use[ci]])
                c2 = S.op("vector", lambda v, ci=ci, wc=wc: v.scalar_tensor_tensor(
                    out=uc[ci][:, 1:T], in0=u[ci][:, 0:T - 1], scalar=taps_s[:, wc, 1:2],
                    in1=uc[ci][:, 1:T], op0=AluOpType.mult, op1=AluOpType.add), deps=[c1])
                c3 = S.op("vector", lambda v, ci=ci, wc=wc: v.scalar_tensor_tensor(
                    out=uc[ci][:, 2:T], in0=u[ci][:, 0:T - 2], scalar=taps_s[:, wc, 0:1],
                    in1=uc[ci][:, 2:T], op0=AluOpType.mult, op1=AluOpType.add), deps=[c2])
                conv_done.append(c3)
                last_u_use[ci] = c3
            vv = S.op("vector", lambda v: v.tensor_tensor(
                out=uc[2][:], in0=uc[2][:], in1=uc[1][:], op=AluOpType.mult),
                deps=[conv_done[1], conv_done[2]])
            sv = S.dma("gpsimd", lambda g, b=b, j=j: g.dma_start(
                out=vT.rearrange("(c p) t -> c p t", p=128)[j, :, b * T:(b + 1) * T],
                in_=uc[2][:]), deps=[vv], group="vst")
            sx = S.dma("gpsimd", lambda g, b=b, j=j: g.dma_start(
                out=x0T.rearrange("(c p) t -> c p t", p=128)[j, :, b * T:(b + 1) * T],
                in_=uc[0][:]), deps=[conv_done[0]], group="xst")
            last_uc_use[0] = sx
            last_uc_use[1] = vv
            last_uc_use[2] = sv
            outs += [sv, sx]
    if standalone:
        S.finalize(outs[-2:])
        return p.done()
    return outs[-2:]



def build_filter(p=None, S=None, io=None):
    standalone = p is None
    if standalone:
        p = Prog("filt")
        S = Sched(p)
    peT = _phase_io(p, io, "peT", [3, T])
    w1f = _phase_io(p, io, "w1f", [3, 64])
    b1r = _phase_io(p, io, "b1r", [1, 64])
    f1c = _phase_io(p, io, "f1c", [64, 1])
    w2f = _phase_io(p, io, "w2f", [64, 64])
    b2r = _phase_io(p, io, "b2r", [1, 64])
    f2c = _phase_io(p, io, "f2c", [64, 1])
    w3f = _phase_io(p, io, "w3f", [64, D])
    b3r = _phase_io(p, io, "b3r", [1, D])
    thr = _phase_io(p, io, "thr", [1, T])
    dlt = _phase_io(p, io, "dlt", [128, DC])
    kT = _phase_io(p, io, "kT", [D, T], out=True)

    pe_s = p.sb("pe_s", [3, T])
    w1_s = p.sb("w1_s", [3, 64])
    b1_s = p.sb("b1_s", [1, 64])
    f1_s = p.sb("f1_s", [64, 1])
    w2_s = p.sb("w2_s", [64, 64])
    b2_s = p.sb("b2_s", [1, 64])
    f2_s = p.sb("f2_s", [64, 1])
    w3_s = p.sb("w3_s", [64, D])
    b3_s = p.sb("b3_s", [1, D])
    th_s = p.sb("th_s", [1, T])
    dl_s = p.sb("dl_s", [128, DC])
    onesT = p.sb("onesT", [1, TK])
    h1 = p.sb("h1", [64, T])
    h2 = p.sb("h2", [64, T])
    tint = p.sb("tint", [64, T], I32)
    trnd = p.sb("trnd", [64, T])
    k0t = p.sb("k0t", [128, TK])
    dect = p.sb("dect", [128, TK])
    ktile = p.sb("ktile", [128, TK])
    psH = p.ps("psH", [64, TK])
    psK = p.ps("psK", [128, TK])
    psD = p.ps("psD", [128, TK])

    onesr = p.sb("onesr", [1, 128])

    for src, dst in [(peT, pe_s), (w1f, w1_s), (b1r, b1_s), (f1c, f1_s),
                     (w2f, w2_s), (b2r, b2_s), (f2c, f2_s), (w3f, w3_s),
                     (b3r, b3_s), (thr, th_s), (dlt, dl_s)]:
        t_set = S.dma("sync", lambda s, src=src, dst=dst: s.dma_start(out=dst[:], in_=src[:]),
                      group="setup")
    t_on = S.op("vector", lambda v: v.memset(onesT[:], 1.0))
    t_on2 = S.op("vector", lambda v: v.memset(onesr[:], 1.0))
    TWO_PI = float(2.0 * np.pi)

    def mlp_stage(wt, bt, ft, rhs_buf, out_buf, last_tok):
        toks = []
        for tc in range(T // TK):
            sl = slice(tc * TK, (tc + 1) * TK)
            m1 = S.op("tensor", lambda t, sl=sl: t.matmul(
                psH[:], wt[:], rhs_buf[:, sl], start=True, stop=False),
                deps=[t_set, last_tok, toks[-1] if toks else None])
            m2 = S.op("tensor", lambda t: t.matmul(
                psH[:], bt[:], onesT[:], start=False, stop=True), deps=[t_on])
            a1 = S.op("vector", lambda v, sl=sl: v.tensor_scalar(
                out=out_buf[:, sl], in0=psH[:], scalar1=ft[:], scalar2=None,
                op0=AluOpType.mult), deps=[m2])
            a2 = S.op("vector", lambda v, sl=sl: v.tensor_scalar(
                out=tint[:, sl], in0=out_buf[:, sl], scalar1=float(1.0 / TWO_PI), scalar2=None,
                op0=AluOpType.mult), deps=[a1])
            a3 = S.op("vector", lambda v, sl=sl: v.tensor_copy(
                out=trnd[:, sl], in_=tint[:, sl]), deps=[a2])
            a4 = S.op("vector", lambda v, sl=sl: v.scalar_tensor_tensor(
                out=out_buf[:, sl], in0=trnd[:, sl], scalar=-TWO_PI, in1=out_buf[:, sl],
                op0=AluOpType.mult, op1=AluOpType.add), deps=[a3, a1])
            a5 = S.op("scalar", lambda s, sl=sl: s.activation(
                out=out_buf[:, sl], in_=out_buf[:, sl], func=AF.Sin), deps=[a4])
            toks.append(a5)
        return toks

    h1_toks = mlp_stage(w1_s, b1_s, f1_s, pe_s, h1, None)
    h2_toks = mlp_stage(w2_s, b2_s, f2_s, h1, h2, h1_toks[-1])

    outs = []
    last_k = None
    last_d = None
    last_kt = None
    for dc in range(DC):
        for tc in range(T // TK):
            sl = slice(tc * TK, (tc + 1) * TK)
            mk1 = S.op("tensor", lambda t, dc=dc, sl=sl: t.matmul(
                psK[:], w3_s[:, dc * 128:(dc + 1) * 128], h2[:, sl], start=True, stop=False),
                deps=[h2_toks[-1], t_set, last_k])
            mk2 = S.op("tensor", lambda t, dc=dc: t.matmul(
                psK[:], b3_s[:, dc * 128:(dc + 1) * 128], onesT[:], start=False, stop=True),
                deps=[t_on])
            md = S.op("tensor", lambda t, sl=sl: t.matmul(
                psD[:], onesr[:], th_s[:, sl], start=True, stop=True),
                deps=[t_on2, t_set, last_d])
            de = S.op("scalar", lambda s, dc=dc: s.activation(
                out=dect[:], in_=psD[:], func=AF.Exp, scale=dl_s[:, dc:dc + 1]),
                deps=[md, last_kt])
            kt = S.op("vector", lambda v: v.tensor_tensor(
                out=ktile[:], in0=psK[:], in1=dect[:], op=AluOpType.mult),
                deps=[mk2, de, last_kt])
            so = S.dma("gpsimd", lambda g, dc=dc, sl=sl: g.dma_start(
                out=kT.rearrange("(c p) t -> c p t", p=128)[dc, :, sl], in_=ktile[:]),
                deps=[kt], group="kst")
            last_k = kt
            last_d = de
            last_kt = so
            outs.append(so)
    if standalone:
        S.finalize([outs[-1]])
        return p.done()
    return [outs[-1]]



def build_conv(p=None, S=None, io=None):
    CB = 32
    standalone = p is None
    if standalone:
        p = Prog("conv")
        S = Sched(p)
    nc = p.nc
    vT = _phase_io(p, io, "vT", [D, NTOK])
    kT = _phase_io(p, io, "kT", [D, T])
    G1c = _phase_io(p, io, "G1c", [64, 128])
    G1s = _phase_io(p, io, "G1s", [64, 128])
    G1Kc = _phase_io(p, io, "G1Kc", [64, 128])
    G1Ks = _phase_io(p, io, "G1Ks", [64, 128])
    Lm = _phase_io(p, io, "Lm", [128, 128])
    LKm = _phase_io(p, io, "LKm", [128, 128])
    Linv = _phase_io(p, io, "Linv", [128, 128])
    Twc32 = _phase_io(p, io, "Twc32", [128, 2048])
    Tws32 = _phase_io(p, io, "Tws32", [128, 2048])
    TwKc32 = _phase_io(p, io, "TwKc32", [128, 2048])
    TwKs32 = _phase_io(p, io, "TwKs32", [128, 2048])
    TwIc32 = _phase_io(p, io, "TwIc32", [128, 2048])
    TwIs32 = _phase_io(p, io, "TwIs32", [128, 2048])
    GBc = _phase_io(p, io, "GBc", [128, 64])
    GBs = _phase_io(p, io, "GBs", [128, 64])
    ident = _phase_io(p, io, "ident", [128, 128])
    ycT = _phase_io(p, io, "ycT", [D, NTOK], out=True)
    Md = nc.dram_tensor("Mdr", [D, 128, 128], F32, kind="Internal").ap()
    Msw = nc.dram_tensor("Mswr", [D, 128, 128], F32, kind="Internal").ap()

    cst = {}
    for nm, src, shape in [("G1c", G1c, [64, 128]), ("G1s", G1s, [64, 128]),
                           ("G1Kc", G1Kc, [64, 128]), ("G1Ks", G1Ks, [64, 128]),
                           ("Lm", Lm, [128, 128]), ("LKm", LKm, [128, 128]),
                           ("Linv", Linv, [128, 128]),
                           ("Twc32", Twc32, [128, 2048]), ("Tws32", Tws32, [128, 2048]),
                           ("TwKc32", TwKc32, [128, 2048]), ("TwKs32", TwKs32, [128, 2048]),
                           ("TwIc32", TwIc32, [128, 2048]), ("TwIs32", TwIs32, [128, 2048]),
                           ("GBc", GBc, [128, 64]), ("GBs", GBs, [128, 64]),
                           ("ident", ident, [128, 128])]:
        cst[nm] = (p.sb("c_" + nm, shape), src)

    W = 2048
    V1c = [p.sb(f"V1c{i}", [64, W]) for i in range(2)]
    V1n = [p.sb(f"V1n{i}", [64, W]) for i in range(2)]
    Ar = p.sb("Ar", [128, W])
    Ai = p.sb("Ai", [128, W])
    Bp = p.sb("Bp", [128, 2 * W])
    Rsb = [p.sb(f"Rsb{i}", [128, 512]) for i in range(2)]
    Xsb = p.sb("Xsb", [128, W])
    Mdup = Ar
    Mswp = Ai
    Pp = p.sb("Pp", [128, W])
    Csb = p.sb("Csb", [128, W])
    Dp = p.sb("Dp", [128, W])
    D2 = p.sb("D2", [128, W])
    osb = p.sb("osb", [64, 2 * 1024])
    ta = p.sb("ta", [64, W])
    tb = p.sb("tb", [64, W])
    tc2 = p.sb("tc2", [64, W])
    td2 = p.sb("td2", [64, W])

    psF0 = p.ps("psF0", [128, 512])
    psF1 = p.ps("psF1", [128, 512])
    psT = p.ps("psT", [128, 512])
    psX = p.ps("psX", [128, 512])
    psC = p.ps("psC", [128, 512])
    psT2 = p.ps("psT2", [128, 512])
    psO = p.ps("psO", [64, 512])

    t_set = None
    for nm, (dst, src) in cst.items():
        t_set = S.dma("sync", lambda s, dst=dst, src=src: s.dma_start(out=dst[:], in_=src[:]),
                      group="setup")
    C = lambda nm: cst[nm][0]

    war = {}

    def block32(kpath, b, cb):
        c0 = cb * CB
        pb = cb % 2
        src = kT if kpath else vT
        t_v1 = S.dma("gpsimd", lambda g, c0=c0, b=b, src=src, kpath=kpath, pb=pb: g.dma_start(
            out=V1c[pb][:].rearrange("p (c n) -> p c n", n=64),
            in_=(src.rearrange("c (n1 n2) -> n1 c n2", n2=64)[:, c0:c0 + CB, :]
                 if kpath else
                 src.rearrange("c (bb n1 n2) -> bb n1 c n2", bb=BPC, n2=64)[b, :, c0:c0 + CB, :])),
            deps=[war.get(("V1c", pb))], group="v1" + str(pb))
        ro = S.op("vector", lambda v, pb=pb: v.tensor_copy(
            out=V1n[pb][:].rearrange("p (n c) -> p n c", c=CB),
            in_=V1c[pb][:].rearrange("p (c n) -> p n c", n=64)), deps=[t_v1, war.get(("V1n", pb))])
        war[("V1c", pb)] = ro
        g1c, g1s = ("G1Kc", "G1Ks") if kpath else ("G1c", "G1s")
        ev_r, ev_i = [], []
        m2 = None
        for sb_ in range(4):
            slc = slice(sb_ * 512, (sb_ + 1) * 512)
            m1 = S.op("tensor", lambda t, slc=slc, pb=pb, g1c=g1c: t.matmul(
                psF0[:], C(g1c)[:], V1n[pb][:, slc], start=True, stop=True),
                deps=[ro, t_set, war.get("psF0")])
            e1 = S.op("scalar", lambda s, slc=slc: s.copy(out=Ar[:, slc], in_=psF0[:]),
                      deps=[m1, war.get("Ar")])
            war["psF0"] = e1
            ev_r.append(e1)
            m2 = S.op("tensor", lambda t, slc=slc, pb=pb, g1s=g1s: t.matmul(
                psF1[:], C(g1s)[:], V1n[pb][:, slc], start=True, stop=True),
                deps=[ro, t_set, war.get("psF1")])
            e2 = S.op("scalar", lambda s, slc=slc: s.copy(out=Ai[:, slc], in_=psF1[:]),
                      deps=[m2, war.get("Ai")])
            war["psF1"] = e2
            ev_i.append(e2)
        war[("V1n", pb)] = m2
        twc, tws = ("TwKc32", "TwKs32") if kpath else ("Twc32", "Tws32")
        bdst = lambda ri: Bp[:].rearrange("p (c x n) -> p x n c", x=2, n=64)[:, ri, :, :]
        w1_ = S.op("vector", lambda v, twc=twc: v.tensor_tensor(
            out=Pp[:], in0=Ar[:], in1=C(twc)[:], op=AluOpType.mult),
            deps=[ev_r[-1], t_set, war.get("Pp")])
        w2_ = S.op("vector", lambda v, tws=tws: v.tensor_tensor(
            out=Csb[:], in0=Ai[:], in1=C(tws)[:], op=AluOpType.mult),
            deps=[ev_i[-1], t_set, war.get("Csb")])
        w3_ = S.op("vector", lambda v: v.tensor_tensor(
            out=bdst(0), in0=Pp[:], in1=Csb[:], op=AluOpType.subtract),
            deps=[w1_, w2_, war.get("Bp")])
        w4_ = S.op("vector", lambda v, tws=tws: v.tensor_tensor(
            out=Pp[:], in0=Ar[:], in1=C(tws)[:], op=AluOpType.mult), deps=[w3_])
        w5_ = S.op("vector", lambda v, twc=twc: v.tensor_tensor(
            out=Csb[:], in0=Ai[:], in1=C(twc)[:], op=AluOpType.mult), deps=[w3_])
        w6_ = S.op("vector", lambda v: v.tensor_tensor(
            out=bdst(1), in0=Pp[:], in1=Csb[:], op=AluOpType.add), deps=[w4_, w5_])
        war["Ar"] = w4_
        war["Ai"] = w5_
        war["Pp"] = w6_
        war["Csb"] = w6_
        last_out = None
        for half in range(2):
            ch0 = half * 16
            ev_x = []
            for g4 in range(4):
                trs = None
                for cc in range(4):
                    ci = ch0 + g4 * 4 + cc
                    trs = S.op("tensor", lambda t, ci=ci, cc=cc: t.transpose(
                        psT[:, cc * 128:(cc + 1) * 128], Bp[:, ci * 128:(ci + 1) * 128], C("ident")[:]),
                        deps=[w6_, t_set, war.get("psT")])
                rb = g4 % 2
                evT = S.op("scalar", lambda s, rb=rb: s.copy(out=Rsb[rb][:], in_=psT[:]),
                           deps=[trs, war.get(("Rsb", rb))])
                war["psT"] = evT
                lmat = "LKm" if kpath else "Lm"
                mX = S.op("tensor", lambda t, rb=rb, lmat=lmat: t.matmul(
                    psX[:], C(lmat)[:], Rsb[rb][:], start=True, stop=True),
                    deps=[evT, t_set, war.get("psX")])
                war[("Rsb", rb)] = mX
                evX = S.op("scalar", lambda s, g4=g4: s.copy(out=Xsb[:, g4 * 512:(g4 + 1) * 512], in_=psX[:]),
                           deps=[mX, war.get("Xsb")])
                war["psX"] = evX
                ev_x.append(evX)
            if kpath:
                c0h = c0 + ch0
                s1 = S.dma("gpsimd", lambda g, c0h=c0h: g.dma_start(
                    out=Md.rearrange("c r k -> r c k")[:, c0h:c0h + 16, :], in_=Xsb[:]),
                    deps=[ev_x[-1]], group="mst")
                s2a = S.dma("gpsimd", lambda g, c0h=c0h: g.dma_start(
                    out=Msw.rearrange("c r k -> r c k")[0:64, c0h:c0h + 16, :], in_=Xsb[64:128, :]),
                    deps=[ev_x[-1]], group="mst")
                s2b = S.dma("gpsimd", lambda g, c0h=c0h: g.dma_start(
                    out=Msw.rearrange("c r k -> r c k")[64:128, c0h:c0h + 16, :], in_=Xsb[0:64, :]),
                    deps=[ev_x[-1]], group="mst")
                war["Xsb"] = s2b
                last_out = s2b
                continue
            c0h = c0 + ch0
            t_md = S.dma("sync", lambda s, c0h=c0h: s.dma_start(
                out=Mdup[:], in_=Md.rearrange("c r k -> r c k")[:, c0h:c0h + 16, :]),
                deps=[war.get("Ar"), mstore_done[0]], group="mld")
            t_ms = S.dma("sync", lambda s, c0h=c0h: s.dma_start(
                out=Mswp[:], in_=Msw.rearrange("c r k -> r c k")[:, c0h:c0h + 16, :]),
                deps=[war.get("Ai"), mstore_done[0]], group="mld")
            pa = S.op("vector", lambda v: v.tensor_tensor(out=ta[:], in0=Xsb[0:64, :], in1=Mdup[0:64, :],
                                                          op=AluOpType.mult),
                      deps=[ev_x[-1], t_md, war.get("ta")])
            pb_ = S.op("vector", lambda v: v.tensor_tensor(out=tb[:], in0=Xsb[64:128, :], in1=Mdup[64:128, :],
                                                           op=AluOpType.mult),
                      deps=[ev_x[-1], t_md, war.get("tb")])
            pc = S.op("vector", lambda v: v.tensor_tensor(out=tc2[:], in0=Xsb[0:64, :], in1=Mswp[0:64, :],
                                                          op=AluOpType.mult),
                      deps=[ev_x[-1], t_ms, war.get("tc2")])
            pd = S.op("vector", lambda v: v.tensor_tensor(out=td2[:], in0=Xsb[64:128, :], in1=Mswp[64:128, :],
                                                          op=AluOpType.mult),
                      deps=[ev_x[-1], t_ms, war.get("td2")])
            war["Xsb"] = pd
            war["Ar"] = pb_
            war["Ai"] = pd
            p3 = S.op("vector", lambda v: v.tensor_tensor(out=Pp[0:64, :], in0=ta[:], in1=tb[:],
                                                          op=AluOpType.subtract),
                      deps=[pa, pb_, war.get("Pp")])
            p4 = S.op("vector", lambda v: v.tensor_tensor(out=Pp[64:128, :], in0=tc2[:], in1=td2[:],
                                                          op=AluOpType.add),
                      deps=[pc, pd, p3])
            war["ta"] = p3
            war["tb"] = p3
            war["tc2"] = p4
            war["td2"] = p4
            ev_c = []
            mC = None
            for sb_ in range(4):
                slc = slice(sb_ * 512, (sb_ + 1) * 512)
                mC = S.op("tensor", lambda t, slc=slc: t.matmul(
                    psC[:], C("Linv")[:], Pp[:, slc], start=True, stop=True),
                    deps=[p4, t_set, war.get("psC")])
                eC = S.op("scalar", lambda s, slc=slc: s.copy(out=Csb[:, slc], in_=psC[:]),
                          deps=[mC, war.get("Csb")])
                war["psC"] = eC
                ev_c.append(eC)
            war["Pp"] = mC
            qa = S.op("gpsimd", lambda g: g.tensor_tensor(out=ta[:], in0=Csb[0:64, :], in1=C("TwIc32")[0:64, :],
                                                          op=AluOpType.mult),
                      deps=[ev_c[-1], t_set, war.get("ta")])
            qb = S.op("gpsimd", lambda g: g.tensor_tensor(out=tb[:], in0=Csb[64:128, :], in1=C("TwIs32")[64:128, :],
                                                          op=AluOpType.mult),
                      deps=[ev_c[-1], t_set, war.get("tb")])
            qc = S.op("gpsimd", lambda g: g.tensor_tensor(out=tc2[:], in0=Csb[0:64, :], in1=C("TwIs32")[0:64, :],
                                                          op=AluOpType.mult),
                      deps=[ev_c[-1], t_set, war.get("tc2")])
            qd = S.op("gpsimd", lambda g: g.tensor_tensor(out=td2[:], in0=Csb[64:128, :], in1=C("TwIc32")[64:128, :],
                                                          op=AluOpType.mult),
                      deps=[ev_c[-1], t_set, war.get("td2")])
            war["Csb"] = qd
            q3 = S.op("gpsimd", lambda g: g.tensor_tensor(out=Dp[0:64, :], in0=ta[:], in1=tb[:],
                                                          op=AluOpType.subtract),
                      deps=[qa, qb, war.get("Dp")])
            q4 = S.op("gpsimd", lambda g: g.tensor_tensor(out=Dp[64:128, :], in0=tc2[:], in1=td2[:],
                                                          op=AluOpType.add),
                      deps=[qc, qd, q3])
            war["ta"] = q3
            war["tb"] = q3
            war["tc2"] = q4
            war["td2"] = q4
            evO = None
            trs2 = None
            mB2 = None
            for g4 in range(4):
                for cc in range(4):
                    ci = g4 * 4 + cc
                    trs2 = S.op("tensor", lambda t, ci=ci, cc=cc: t.transpose(
                        psT2[:, cc * 128:(cc + 1) * 128], Dp[:, ci * 128:(ci + 1) * 128], C("ident")[:]),
                        deps=[q4, t_set, war.get("psT2")])
                evT2 = S.op("scalar", lambda s, g4=g4: s.copy(out=D2[:, g4 * 512:(g4 + 1) * 512], in_=psT2[:]),
                            deps=[trs2, war.get("D2")])
                war["psT2"] = evT2
                hb = g4 % 2
                d2v = lambda g4=g4, x=0: D2[:, g4 * 512:(g4 + 1) * 512].rearrange(
                    "p (c two n) -> p two c n", two=2, n=64)[:, x]
                mB1 = S.op("tensor", lambda t, g4=g4, hb=hb, d2v=d2v: t.matmul(
                    psO[:, hb * 256:(hb + 1) * 256], C("GBc")[:], d2v(x=0),
                    start=True, stop=False),
                    deps=[evT2, t_set, (war.get("psO") if hb == 0 and g4 < 2 else None)])
                mB2 = S.op("tensor", lambda t, g4=g4, hb=hb, d2v=d2v: t.matmul(
                    psO[:, hb * 256:(hb + 1) * 256], C("GBs")[:], d2v(x=1),
                    start=False, stop=True), deps=[])
                if hb == 1:
                    evO = S.op("scalar", lambda s, g4=g4, half=half: s.copy(
                        out=osb[:, half * 1024 + (g4 - 1) * 256: half * 1024 + (g4 + 1) * 256],
                        in_=psO[:]),
                        deps=[mB2, war.get("osb")])
                    war["psO"] = evO
            war["Dp"] = trs2
            war["D2"] = mB2
            last_evO = evO
        if kpath:
            return last_out
        st_ = S.dma("gpsimd", lambda g, cb=cb, b=b: g.dma_start(
            out=ycT.rearrange("c (bb n1 n2) -> bb n1 c n2", bb=BPC, n2=64)[b, :, cb * CB:cb * CB + CB, :],
            in_=osb[:].rearrange("p (c n) -> p c n", n=64)), deps=[last_evO], group="yst")
        war["osb"] = st_
        return st_

    mstore_done = [None]
    tk = None
    for cb in range(D // CB):
        tk = block32(True, 0, cb)
    mstore_done[0] = tk
    st_last = None
    for b in range(BPC):
        for cb in range(D // CB):
            st_last = block32(False, b, cb)

    if standalone:
        S.finalize([st_last])
        return p.done()
    return [st_last]



def build_l1out(p=None, S=None, io=None):
    standalone = p is None
    if standalone:
        p = Prog("l1out")
        S = Sched(p)
    yc = _phase_io(p, io, "yc", [D, NTOK])
    x0 = _phase_io(p, io, "x0", [D, NTOK])
    fb = _phase_io(p, io, "fb", [128, DC])
    wo = _phase_io(p, io, "wo", [D, D])
    bor = _phase_io(p, io, "bor", [1, D])
    s1 = _phase_io(p, io, "s1", [D, NTOK], out=True)

    wo_b = p.sb("wo_b", [128, DC, D], BF16)
    fb_s = p.sb("fb_s", [128, DC])
    bo_s = p.sb("bo_s", [1, D])
    onesT = p.sb("onesT", [1, TK])
    yct = [p.sb(f"yct{i}", [128, DC, TK]) for i in range(2)]
    x0t = [p.sb(f"x0t{i}", [128, DC, TK]) for i in range(2)]
    ygb = p.sb("ygb", [128, DC, TK], BF16)
    s1t = [p.sb(f"s1t{i}", [128, TK]) for i in range(2)]
    psA = p.ps("psA", [128, TK])[:]
    psB = p.ps("psB", [128, TK])[:]

    S.dma("gpsimd", lambda g: g.dma_start(out=wo_b[:], in_=wo.rearrange("(c p) o -> p c o", p=128)), group="setup")
    S.dma("sync", lambda s: s.dma_start(out=fb_s[:], in_=fb[:]), group="setup")
    t_set = S.dma("sync", lambda s: s.dma_start(out=bo_s[:], in_=bor[:]), group="setup")
    t_setg = S.ops["gpsimd"][-1]
    t_on = S.op("vector", lambda v: v.memset(onesT[:], 1.0))

    ra = lambda a: a.rearrange("(c p) t -> p c t", p=128)
    last_g = {}
    outs = []
    for tk in range(NTK):
        b = tk % 2
        sl = slice(tk * TK, (tk + 1) * TK)
        ty = S.dma("gpsimd", lambda g, sl=sl, b=b: g.dma_start(
            out=yct[b][:], in_=ra(yc)[:, :, sl]), deps=[last_g.get(("yg", b))], group="y" + str(b))
        tx = S.dma("sync", lambda s, sl=sl, b=b: s.dma_start(
            out=x0t[b][:], in_=ra(x0)[:, :, sl]), deps=[last_g.get(("yg", b))], group="x" + str(b))
        gops = []
        for dc in range(DC):
            gops.append(S.op("vector", lambda v, dc=dc, b=b: v.scalar_tensor_tensor(
                out=ygb[:, dc, :], in0=yct[b][:, dc, :], scalar=fb_s[:, dc:dc + 1],
                in1=x0t[b][:, dc, :], op0=AluOpType.add, op1=AluOpType.mult),
                deps=[ty, tx, t_set, last_g.get("mm")]))
        last_g[("yg", b)] = gops[-1]
        mm_last = None
        for oc in range(DC):
            ps_ = psA if oc % 2 == 0 else psB
            for dc in range(DC):
                deps = [gops[-1], t_setg]
                if dc == 0:
                    deps.append(last_g.get(("ev", oc % 2)))
                mm_last = S.op("tensor", lambda t, oc=oc, dc=dc, ps_=ps_: t.matmul(
                    ps_, wo_b[:, dc, oc * 128:(oc + 1) * 128], ygb[:, dc, :],
                    start=(dc == 0), stop=False), deps=deps)
            mm_last = S.op("tensor", lambda t, oc=oc, ps_=ps_: t.matmul(
                ps_, bo_s[:, oc * 128:(oc + 1) * 128], onesT[:], start=False, stop=True),
                deps=[t_set, t_on])
            ev = S.op("scalar", lambda s, oc=oc, ps_=ps_: s.copy(out=s1t[oc % 2][:], in_=ps_),
                      deps=[mm_last, last_g.get(("st", oc % 2))])
            st_ = S.dma("sync", lambda s, oc=oc, sl=sl: s.dma_start(
                out=s1.rearrange("(c p) t -> c p t", p=128)[oc, :, sl], in_=s1t[oc % 2][:]),
                deps=[ev], group="s" + str(oc % 2))
            last_g[("ev", oc % 2)] = ev
            last_g[("st", oc % 2)] = st_
            outs.append(st_)
        last_g["mm"] = mm_last
    if standalone:
        S.finalize(outs[-2:])
        return p.done()
    return outs[-2:]


def l1out_inmaps(yc_shards, x0_shards, inputs):
    ac = np.ascontiguousarray
    f32 = np.float32
    fb = ac(np.asarray(inputs["L1_attn_filter_bias"], dtype=f32).reshape(8, 128).T)
    wo = ac(inputs["L1_attn_W_out"], dtype=f32)
    bor = ac(np.asarray(inputs["L1_attn_b_out"], dtype=f32).reshape(1, -1))
    return [{"yc": np.ascontiguousarray(y), "x0": np.ascontiguousarray(x),
             "fb": fb, "wo": wo, "bor": bor} for y, x in zip(yc_shards, x0_shards)]


def run_l1out(yc_shards, x0_shards, inputs):
    nc = _get("l1out", build_l1out)
    res = run_bass_kernel_spmd(nc, l1out_inmaps(yc_shards, x0_shards, inputs),
                               core_ids=list(range(NCORES)))
    return [r["s1"] for r in res.results]



def _shard_x(xfull):
    shards = []
    for c in range(NCORES):
        xc = xfull[c * BPC:(c + 1) * BPC].reshape(BPC * T, D)
        shards.append(np.ascontiguousarray(xc.T.astype(np.float32)))
    return shards


_PROG_CACHE = {}


def _get(name, fn):
    if name not in _PROG_CACHE:
        _PROG_CACHE[name] = fn()
    return _PROG_CACHE[name]


def _kv_perm():
    k_cols = (np.arange(16)[:, None] * 128 + np.arange(64)[None, :]).ravel()
    v_cols = (np.arange(16)[:, None] * 128 + 64 + np.arange(64)[None, :]).ravel()
    return np.concatenate([k_cols, v_cols])


def l0att_inmaps(xn_shards, inputs):
    f32 = np.float32
    ac = np.ascontiguousarray
    wq = ac(inputs["L0_attn_Wq"], dtype=f32)
    wd = ac(inputs["L0_attn_Wdown"], dtype=f32)
    wu0 = np.asarray(inputs["L0_attn_Wup"], dtype=f32)
    wo = ac(inputs["L0_attn_Wout"], dtype=f32)
    perm = _kv_perm()
    wu = ac(wu0[:, perm])
    bu = ac(np.asarray(inputs["L0_attn_bup"], dtype=f32)[perm])
    bqr = ac(np.asarray(inputs["L0_attn_bq"], dtype=f32).reshape(1, -1))
    bdr = ac(np.asarray(inputs["L0_attn_bdown"], dtype=f32).reshape(1, -1))
    bur = ac(bu.reshape(1, -1))
    bor = ac(np.asarray(inputs["L0_attn_bout"], dtype=f32).reshape(1, -1))
    ropec, ropes = _rope_tables()
    base = {"wq": wq, "wd": wd, "wu": wu, "wo": wo, "bqr": bqr, "bdr": bdr,
            "bur": bur, "bor": bor, "ropec": ropec, "ropes": ropes}
    return [dict(base, xn=np.ascontiguousarray(xs)) for xs in xn_shards]


def run_l0att(xn_shards, inputs):
    nc = _get("l0att", build_l0att)
    res = run_bass_kernel_spmd(nc, l0att_inmaps(xn_shards, inputs), core_ids=list(range(NCORES)))
    return [r["s0"] for r in res.results]


def router_inmaps(kind, xn_shards, inputs):
    ac = np.ascontiguousarray
    f32 = np.float32
    if kind == "l0":
        rw = ac(np.asarray(inputs["L0_ffn_router_w"], dtype=f32).T)
        brr = np.zeros((1, E), f32)
        sc8 = ac(np.asarray(inputs["L0_ffn_expert_scale"], dtype=f32).reshape(1, E))
    else:
        rw = ac(inputs["L1_ffn_Wr"], dtype=f32)
        brr = ac(np.asarray(inputs["L1_ffn_br"], dtype=f32).reshape(1, E))
        sc8 = np.ones((1, E), f32)
    ident = np.eye(128, dtype=f32)
    base = {"rw": rw, "brr": brr, "sc8": sc8, "ident": ident}
    return [dict(base, xn=np.ascontiguousarray(xs)) for xs in xn_shards]


def run_router(kind, xn_shards, inputs):
    nc = _get("rt_" + kind, lambda: build_router(kind))
    res = run_bass_kernel_spmd(nc, router_inmaps(kind, xn_shards, inputs), core_ids=list(range(NCORES)))
    return [r["comb"] for r in res.results]


def moe_inmaps(act, xn_shards, comb_shards, inputs):
    ac = np.ascontiguousarray
    f32 = np.float32
    if act == "silu":
        w1 = ac(np.asarray(inputs["L0_ffn_w1"], dtype=f32).transpose(0, 2, 1))
        w2 = ac(np.asarray(inputs["L0_ffn_w2"], dtype=f32).transpose(0, 2, 1))
    else:
        w1 = ac(inputs["L1_ffn_W1"], dtype=f32)
        w2 = ac(inputs["L1_ffn_W2"], dtype=f32)
    return [{"xn": np.ascontiguousarray(xs), "w1": w1, "w2": w2,
             "comb": np.ascontiguousarray(cb)} for xs, cb in zip(xn_shards, comb_shards)]


def run_moe(act, xn_shards, comb_shards, inputs):
    nc = _get("moe_" + act, lambda: build_moe(act))
    res = run_bass_kernel_spmd(nc, moe_inmaps(act, xn_shards, comb_shards, inputs),
                               core_ids=list(range(NCORES)))
    return [r["ymoe"] for r in res.results]


def run_ep(x_shards, s_shards, y_shards):
    nc = _get("ep", build_ep)
    im = [{"x": np.ascontiguousarray(a), "sT": np.ascontiguousarray(b),
           "ym": np.ascontiguousarray(c)} for a, b, c in zip(x_shards, s_shards, y_shards)]
    res = run_bass_kernel_spmd(nc, im, core_ids=list(range(NCORES)))
    return [r["x1"] for r in res.results]


def l1gate_inmaps(xn_shards, inputs):
    ac = np.ascontiguousarray
    f32 = np.float32
    win = ac(inputs["L1_attn_W_in"], dtype=f32)
    binr = ac(np.asarray(inputs["L1_attn_b_in"], dtype=f32).reshape(1, -1))
    sw = np.asarray(inputs["L1_attn_short_w"], dtype=f32)
    taps = ac(sw.reshape(24, 128, 3).transpose(1, 0, 2))
    sbr = ac(np.asarray(inputs["L1_attn_short_b"], dtype=f32).reshape(24, 128).T)
    base = {"win": win, "binr": binr, "taps": taps, "sbr": sbr}
    return [dict(base, xn=np.ascontiguousarray(xs)) for xs in xn_shards]


def run_l1gate(xn_shards, inputs):
    nc = _get("l1gate", build_l1gate)
    res = run_bass_kernel_spmd(nc, l1gate_inmaps(xn_shards, inputs), core_ids=list(range(NCORES)))
    return [(r["vT"], r["x0T"]) for r in res.results]


def filter_inmaps(inputs):
    ac = np.ascontiguousarray
    f32 = np.float32
    peT, deltas, th = _filter_consts()
    m = {"peT": ac(peT), "thr": ac(th.reshape(1, -1)),
         "dlt": ac(-(deltas.reshape(8, 128).T)),
         "w1f": ac(inputs["L1_attn_W1"], dtype=f32),
         "b1r": ac(np.asarray(inputs["L1_attn_b1"], dtype=f32).reshape(1, -1)),
         "f1c": ac(np.asarray(inputs["L1_attn_f1"], dtype=f32).reshape(-1, 1)),
         "w2f": ac(inputs["L1_attn_W2"], dtype=f32),
         "b2r": ac(np.asarray(inputs["L1_attn_b2"], dtype=f32).reshape(1, -1)),
         "f2c": ac(np.asarray(inputs["L1_attn_f2"], dtype=f32).reshape(-1, 1)),
         "w3f": ac(inputs["L1_attn_W3"], dtype=f32),
         "b3r": ac(np.asarray(inputs["L1_attn_b3"], dtype=f32).reshape(1, -1))}
    return [dict(m) for _ in range(NCORES)]


def run_filter(inputs):
    nc = _get("filt", build_filter)
    res = run_bass_kernel_spmd(nc, filter_inmaps(inputs), core_ids=list(range(NCORES)))
    return [r["kT"] for r in res.results]


def conv_inmaps(v_shards, k_shards):
    c = _fft_consts()
    ac = np.ascontiguousarray
    base = {"G1c": c["G1c"], "G1s": c["G1s"], "G1Kc": c["G1Kc"], "G1Ks": c["G1Ks"],
            "Lm": c["L"], "LKm": c["LK"], "Linv": c["Linv"],
            "Twc32": ac(np.repeat(c["Twc"], 32, axis=1)),
            "Tws32": ac(np.repeat(c["Tws"], 32, axis=1)),
            "TwKc32": ac(np.repeat(c["TwKc"], 32, axis=1)),
            "TwKs32": ac(np.repeat(c["TwKs"], 32, axis=1)),
            "TwIc32": ac(np.vstack([np.tile(c["TwIc2"][:64], (1, 16))] * 2)),
            "TwIs32": ac(np.vstack([np.tile(c["TwIs2"][:64], (1, 16))] * 2)),
            "GBc": c["GBc"], "GBs": c["GBs"], "ident": np.eye(128, dtype=np.float32)}
    return [dict(base, vT=np.ascontiguousarray(v), kT=np.ascontiguousarray(k))
            for v, k in zip(v_shards, k_shards)]


def run_conv(v_shards, k_shards):
    nc = _get("conv", build_conv)
    res = run_bass_kernel_spmd(nc, conv_inmaps(v_shards, k_shards), core_ids=list(range(NCORES)))
    return [r["ycT"] for r in res.results]


def run_ln(x_shards):
    nc = _get("ln", build_ln)
    res = run_bass_kernel_spmd(nc, [{"x": np.ascontiguousarray(s)} for s in x_shards],
                               core_ids=list(range(NCORES)))
    return [r["xn"] for r in res.results], [r["xn32"] for r in res.results]



_FUSED_SPEC = [
    ("x", [D, NTOK], "f"),
    ("a_wq", [D, D], "f"), ("a_wd", [D, 256], "f"), ("a_wu", [256, 2048], "f"),
    ("a_wo", [D, D], "f"), ("a_bqr", [1, D], "f"), ("a_bdr", [1, 256], "f"),
    ("a_bur", [1, 2048], "f"), ("a_bor", [1, D], "f"),
    ("a_ropec", [128, T], "f"), ("a_ropes", [128, T], "f"),
    ("r0_rw", [D, E], "f"), ("r0_brr", [1, E], "f"), ("r0_sc8", [1, E], "f"),
    ("r0_ident", [128, 128], "f"),
    ("r1_rw", [D, E], "f"), ("r1_brr", [1, E], "f"), ("r1_sc8", [1, E], "f"),
    ("m0_w1", [E, D, FF], "f"), ("m0_w2", [E, FF, D], "f"),
    ("m1_w1", [E, D, FF], "f"), ("m1_w2", [E, FF, D], "f"),
    ("g_win", [D, WIDTH], "f"), ("g_binr", [1, WIDTH], "f"),
    ("g_taps", [128, 24, 3], "f"), ("g_sbr", [128, 24], "f"),
    ("f_peT", [3, T], "f"), ("f_w1f", [3, 64], "f"), ("f_b1r", [1, 64], "f"),
    ("f_f1c", [64, 1], "f"), ("f_w2f", [64, 64], "f"), ("f_b2r", [1, 64], "f"),
    ("f_f2c", [64, 1], "f"), ("f_w3f", [64, D], "f"), ("f_b3r", [1, D], "f"),
    ("f_thr", [1, T], "f"), ("f_dlt", [128, DC], "f"),
    ("c_G1c", [64, 128], "f"), ("c_G1s", [64, 128], "f"),
    ("c_G1Kc", [64, 128], "f"), ("c_G1Ks", [64, 128], "f"),
    ("c_Lm", [128, 128], "f"), ("c_LKm", [128, 128], "f"), ("c_Linv", [128, 128], "f"),
    ("c_Twc32", [128, 2048], "f"), ("c_Tws32", [128, 2048], "f"),
    ("c_TwKc32", [128, 2048], "f"), ("c_TwKs32", [128, 2048], "f"),
    ("c_TwIc32", [128, 2048], "f"), ("c_TwIs32", [128, 2048], "f"),
    ("c_GBc", [128, 64], "f"), ("c_GBs", [128, 64], "f"),
    ("o_fb", [128, DC], "f"), ("o_wo", [D, D], "f"), ("o_bor", [1, D], "f"),
]


def build_fused():
    p = Prog("mega")
    S = Sched(p)
    ext = {}
    for nm, shape, _ in _FUSED_SPEC:
        ext[nm] = p.dram_in(nm, shape, F32)
    out = p.dram_out("out", [D, NTOK], F32)
    tmp = {
        "xn": p.dram_tmp("t_xn", [D, NTOK], BF16),
        "xn32": p.dram_tmp("t_xn32", [D, NTOK]),
        "s0": p.dram_tmp("t_s0", [D, NTOK]),
        "cb0": p.dram_tmp("t_cb0", [E, NTOK]),
        "ym0": p.dram_tmp("t_ym0", [D, NTOK]),
        "x1": p.dram_tmp("t_x1", [D, NTOK]),
        "xn1": p.dram_tmp("t_xn1", [D, NTOK], BF16),
        "xn132": p.dram_tmp("t_xn132", [D, NTOK]),
        "vT": p.dram_tmp("t_vT", [D, NTOK]),
        "x0T": p.dram_tmp("t_x0T", [D, NTOK]),
        "kT": p.dram_tmp("t_kT", [D, T]),
        "ycT": p.dram_tmp("t_ycT", [D, NTOK]),
        "s1": p.dram_tmp("t_s1", [D, NTOK]),
        "cb1": p.dram_tmp("t_cb1", [E, NTOK]),
        "ym1": p.dram_tmp("t_ym1", [D, NTOK]),
    }
    E_ = ext
    phases = [
        ("ln0", build_ln, {"x": E_["x"], "xn": tmp["xn"], "xn32": tmp["xn32"]}),
        ("att", build_l0att, {"xn": tmp["xn"], "s0": tmp["s0"],
                              "wq": E_["a_wq"], "wd": E_["a_wd"], "wu": E_["a_wu"], "wo": E_["a_wo"],
                              "bqr": E_["a_bqr"], "bdr": E_["a_bdr"], "bur": E_["a_bur"],
                              "bor": E_["a_bor"], "ropec": E_["a_ropec"], "ropes": E_["a_ropes"]}),
        ("rt0", lambda p=None, S=None, io=None: build_router("l0", p, S, io),
         {"xn": tmp["xn32"], "rw": E_["r0_rw"], "brr": E_["r0_brr"], "sc8": E_["r0_sc8"],
          "ident": E_["r0_ident"], "comb": tmp["cb0"]}),
        ("moe0", lambda p=None, S=None, io=None: build_moe("silu", p, S, io),
         {"xn": tmp["xn"], "w1": E_["m0_w1"], "w2": E_["m0_w2"], "comb": tmp["cb0"],
          "ymoe": tmp["ym0"]}),
        ("ep0", build_ep, {"x": E_["x"], "sT": tmp["s0"], "ym": tmp["ym0"], "x1": tmp["x1"]}),
        ("ln1", build_ln, {"x": tmp["x1"], "xn": tmp["xn1"], "xn32": tmp["xn132"]}),
        ("gate", build_l1gate, {"xn": tmp["xn1"], "win": E_["g_win"], "binr": E_["g_binr"],
                                "taps": E_["g_taps"], "sbr": E_["g_sbr"],
                                "vT": tmp["vT"], "x0T": tmp["x0T"]}),
        ("filt", build_filter, {"peT": E_["f_peT"], "w1f": E_["f_w1f"], "b1r": E_["f_b1r"],
                                "f1c": E_["f_f1c"], "w2f": E_["f_w2f"], "b2r": E_["f_b2r"],
                                "f2c": E_["f_f2c"], "w3f": E_["f_w3f"], "b3r": E_["f_b3r"],
                                "thr": E_["f_thr"], "dlt": E_["f_dlt"], "kT": tmp["kT"]}),
        ("conv", build_conv, {"vT": tmp["vT"], "kT": tmp["kT"], "ycT": tmp["ycT"],
                              "ident": E_["r0_ident"],
                              **{k[2:]: E_[k] for k in ext if k.startswith("c_")}}),
        ("l1o", build_l1out, {"yc": tmp["ycT"], "x0": tmp["x0T"], "fb": E_["o_fb"],
                              "wo": E_["o_wo"], "bor": E_["o_bor"], "s1": tmp["s1"]}),
        ("rt1", lambda p=None, S=None, io=None: build_router("l1", p, S, io),
         {"xn": tmp["xn132"], "rw": E_["r1_rw"], "brr": E_["r1_brr"], "sc8": E_["r1_sc8"],
          "ident": E_["r0_ident"], "comb": tmp["cb1"]}),
        ("moe1", lambda p=None, S=None, io=None: build_moe("gelu", p, S, io),
         {"xn": tmp["xn1"], "w1": E_["m1_w1"], "w2": E_["m1_w2"], "comb": tmp["cb1"],
          "ymoe": tmp["ym1"]}),
        ("ep1", build_ep, {"x": tmp["x1"], "sT": tmp["s1"], "ym": tmp["ym1"], "x1": out}),
    ]
    toks = None
    for name, fn, io in phases:
        p.begin_phase(name)
        toks = fn(p=p, S=S, io=io)
        S.next_phase()
        p.end_phase()
    S.emit_tail(toks)
    return p.done()


def fused_inmaps(inputs):
    x = np.asarray(inputs["L0_attn_x"])
    shards = _shard_x(x)
    aw = l0att_inmaps([shards[0]], inputs)[0]
    r0 = router_inmaps("l0", [shards[0]], inputs)[0]
    r1 = router_inmaps("l1", [shards[0]], inputs)[0]
    m0 = moe_inmaps("silu", [shards[0]], [np.zeros((E, NTOK), np.float32)], inputs)[0]
    m1 = moe_inmaps("gelu", [shards[0]], [np.zeros((E, NTOK), np.float32)], inputs)[0]
    gw = l1gate_inmaps([shards[0]], inputs)[0]
    fw = filter_inmaps(inputs)[0]
    cw = conv_inmaps([shards[0]], [np.zeros((D, T), np.float32)])[0]
    ow = l1out_inmaps([shards[0]], [shards[0]], inputs)[0]
    base = {}
    for src, pref, keys in [
        (aw, "a_", ["wq", "wd", "wu", "wo", "bqr", "bdr", "bur", "bor", "ropec", "ropes"]),
        (r0, "r0_", ["rw", "brr", "sc8", "ident"]),
        (r1, "r1_", ["rw", "brr", "sc8"]),
        (m0, "m0_", ["w1", "w2"]),
        (m1, "m1_", ["w1", "w2"]),
        (gw, "g_", ["win", "binr", "taps", "sbr"]),
        (fw, "f_", ["peT", "w1f", "b1r", "f1c", "w2f", "b2r", "f2c", "w3f", "b3r", "thr", "dlt"]),
        (cw, "c_", ["G1c", "G1s", "G1Kc", "G1Ks", "Lm", "LKm", "Linv", "Twc32", "Tws32",
                    "TwKc32", "TwKs32", "TwIc32", "TwIs32", "GBc", "GBs"]),
        (ow, "o_", ["fb", "wo", "bor"]),
    ]:
        for k in keys:
            base[pref + k] = src[k]
    return [dict(base, x=np.ascontiguousarray(s)) for s in shards]


def kernel_fused(**inputs):
    nc = _get("fused", build_fused)
    im = fused_inmaps(inputs)
    res = run_bass_kernel_spmd(nc, im, core_ids=list(range(NCORES)))
    full = np.empty((16, T, D), np.float32)
    for c in range(NCORES):
        full[c * BPC:(c + 1) * BPC] = np.asarray(res.results[c]["out"]).T.reshape(BPC, T, D)
    return full


def kernel(**inputs):
    return kernel_fused(**inputs)


def kernel_multilaunch(**inputs):
    x = np.asarray(inputs["L0_attn_x"])
    shards = _shard_x(x)
    A = lambda lst: [np.asarray(a) for a in lst]
    xn, xn32 = run_ln(shards)
    xn = A(xn); xn32 = A(xn32)
    s0 = A(run_l0att(xn, inputs))
    comb0 = A(run_router("l0", xn32, inputs))
    ym0 = A(run_moe("silu", xn, comb0, inputs))
    x1 = A(run_ep(shards, s0, ym0))
    xn1, xn132 = run_ln(x1)
    xn1 = A(xn1); xn132 = A(xn132)
    gates = run_l1gate(xn1, inputs)
    vsh = A([g[0] for g in gates])
    x0sh = A([g[1] for g in gates])
    kTs = A(run_filter(inputs))
    yc = A(run_conv(vsh, kTs))
    s1 = A(run_l1out(yc, x0sh, inputs))
    comb1 = A(run_router("l1", xn132, inputs))
    ym1 = A(run_moe("gelu", xn1, comb1, inputs))
    out = A(run_ep(x1, s1, ym1))
    full = np.empty((16, T, D), np.float32)
    for c in range(NCORES):
        full[c * BPC:(c + 1) * BPC] = np.asarray(out[c]).T.reshape(BPC, T, D)
    return full


# version 11 (final)
# speedup vs baseline: 1.0386x; 1.0386x over previous
import math
from contextlib import ExitStack

import numpy as np
import ml_dtypes
import concourse.bass as bass
import concourse.mybir as mybir
from concourse.bass_utils import run_bass_kernel_spmd
from concourse.alu_op_type import AluOpType

F32 = mybir.dt.float32
BF16 = mybir.dt.bfloat16
I32 = mybir.dt.int32
AF = mybir.ActivationFunctionType
AX = mybir.AxisListType.X

NCORES = 8
D = 1024
DC = D // 128
T = 4096
BPC = 2
NTOK = BPC * T
TK = 512
NTK = NTOK // TK
E = 8
FF = 4096
FC = FF // 128
H = 16
DH = 64
WIDTH = 3 * D
LN_EPS = 1e-5

bf16 = ml_dtypes.bfloat16



def _rope_tables():
    inv_freq = 1.0 / 10000.0 ** (np.arange(0, DH, 2, dtype=np.float64) / DH)
    t = np.arange(T, dtype=np.float64)
    freqs = t[:, None] * inv_freq[None, :]
    j = np.arange(DH)
    C = np.cos(freqs[:, j % 32]).T
    S = np.sin(freqs[:, j % 32]).T
    sign = np.where(j < 32, -1.0, 1.0)[:, None]
    Crep = np.tile(C, (2, 1)).astype(np.float32)
    Srep = np.tile(S * sign, (2, 1)).astype(np.float32)
    return Crep, Srep


def _fft_consts():
    c = {}
    n1 = np.arange(64)[:, None]
    k1 = np.arange(128)[None, :]
    ang1 = 2 * np.pi * n1 * k1 / 128.0
    c["G1c"] = np.cos(ang1)
    c["G1s"] = -np.sin(ang1)
    c["G1Kc"] = np.cos(ang1)
    c["G1Ks"] = np.sin(ang1)
    k1c = np.arange(128)[:, None]
    n2 = np.arange(64)[None, :]
    angt = 2 * np.pi * n2 * k1c / 8192.0
    c["Twc"] = np.cos(angt)
    c["Tws"] = -np.sin(angt)
    twk = np.exp(2j * np.pi * n2 * k1c / 8192.0) * np.exp(-2j * np.pi * k1c * 4095.0 / 8192.0)
    c["TwKc"] = twk.real
    c["TwKs"] = twk.imag
    n2c = np.arange(64)[:, None]
    k2 = np.arange(64)[None, :]
    c2 = np.cos(2 * np.pi * n2c * k2 / 64.0)
    s2 = np.sin(2 * np.pi * n2c * k2 / 64.0)
    L = np.zeros((128, 128))
    L[:64, :64] = c2; L[64:, :64] = s2; L[:64, 64:] = -s2; L[64:, 64:] = c2
    c["L"] = L
    p2 = np.exp(-2j * np.pi * np.arange(64) * 4095.0 / 64.0)
    LK0 = np.zeros((128, 128))
    LK0[:64, :64] = c2; LK0[64:, :64] = -s2; LK0[:64, 64:] = s2; LK0[64:, 64:] = c2
    LK = np.zeros((128, 128))
    LK[:, :64] = LK0[:, :64] * p2.real[None, :] - LK0[:, 64:] * p2.imag[None, :]
    LK[:, 64:] = LK0[:, :64] * p2.imag[None, :] + LK0[:, 64:] * p2.real[None, :]
    c["LK"] = LK
    k2c = np.arange(64)[:, None]
    n2r = np.arange(64)[None, :]
    ci = np.cos(2 * np.pi * k2c * n2r / 64.0)
    si = np.sin(2 * np.pi * k2c * n2r / 64.0)
    Linv = np.zeros((128, 128))
    Linv[:64, :64] = ci; Linv[64:, :64] = -si; Linv[:64, 64:] = si; Linv[64:, 64:] = ci
    c["Linv"] = Linv
    n2i = np.arange(64)[:, None]
    k1i = np.arange(128)[None, :]
    angti = 2 * np.pi * n2i * k1i / 8192.0
    c["TwIc2"] = np.concatenate([np.cos(angti), np.cos(angti)], axis=0)
    c["TwIs2"] = np.concatenate([np.sin(angti), np.sin(angti)], axis=0)
    k1b = np.arange(128)[:, None]
    n1b = np.arange(64)[None, :]
    c["GBc"] = np.cos(2 * np.pi * k1b * n1b / 128.0) / 8192.0
    c["GBs"] = -np.sin(2 * np.pi * k1b * n1b / 128.0) / 8192.0
    return {k: v.astype(np.float32) for k, v in c.items()}


def _filter_consts():
    t = np.arange(T, dtype=np.float64)
    th = t / (T - 1)
    ang = 2.0 * np.pi * t / T
    phase = 0.0001 * ang
    peT = np.stack([th, np.cos(phase), -np.sin(phase)], axis=0).astype(np.float32)
    dmax, dmin = (math.log(0.01) / 0.3, math.log(0.01) / 1.5)
    deltas = np.abs(np.linspace(dmax, dmin, D)).astype(np.float32)
    return peT, deltas, th.astype(np.float32)



class Prog:

    def __init__(self, name):
        self.nc = bass.Bass("TRN2", target_bir_lowering=False, debug=False)
        self.ctx = ExitStack()
        self.name = name
        self.phase_ctx = None
        self.prefix = ""

    def begin_phase(self, name):
        self.phase_ctx = ExitStack()
        self.prefix = name + "_"

    def end_phase(self):
        self.phase_ctx.close()
        self.phase_ctx = None
        self.prefix = ""

    def dram_in(self, name, shape, dt=F32):
        return self.nc.dram_tensor(name, shape, dt, kind="ExternalInput").ap()

    def dram_out(self, name, shape, dt=F32):
        return self.nc.dram_tensor(name, shape, dt, kind="ExternalOutput").ap()

    def dram_tmp(self, name, shape, dt=F32):
        return self.nc.dram_tensor(name, shape, dt, kind="Internal").ap()

    def sb(self, name, shape, dt=F32):
        ctx = self.phase_ctx if self.phase_ctx is not None else self.ctx
        return ctx.enter_context(self.nc.sbuf_tensor(self.prefix + name, shape, dt))

    def ps(self, name, shape, dt=F32):
        ctx = self.phase_ctx if self.phase_ctx is not None else self.ctx
        return ctx.enter_context(self.nc.psum_tensor(self.prefix + name, shape, dt))

    def sem(self, name):
        return self.ctx.enter_context(self.nc.semaphore(name))

    def block(self):
        return self.ctx.enter_context(self.nc.Block())

    def done(self):
        self.ctx.close()
        return self.nc



ENGINES = ("sync", "gpsimd", "vector", "scalar", "tensor")


class Op:
    __slots__ = ("eng", "fn", "deps", "idx", "is_dma", "milestone", "group", "gcount", "_needed")

    def __init__(self, eng, fn, deps, idx, is_dma, group=None, gcount=0):
        self.eng = eng
        self.fn = fn
        self.deps = deps
        self.idx = idx
        self.is_dma = is_dma
        self.group = group
        self.gcount = gcount
        self.milestone = None
        self._needed = False


class Sched:

    def __init__(self, p):
        self.p = p
        self.ops = {e: [] for e in ENGINES}
        self.all = []
        self.gcounts = {}
        self.emitted = {e: 0 for e in ENGINES}
        self.ms_count = {e: 0 for e in ENGINES}
        self.ms_of_idx = {e: {} for e in ENGINES}
        self.ms_pending = {e: [] for e in ENGINES}
        self.sems = {}
        self.gsems = {}
        self.waited_ms = {e: {f: 0 for f in ENGINES} for e in ENGINES}
        self.waited_dma = {e: {} for e in ENGINES}
        self.pending_barrier = []
        self.group_last = {}

    def op(self, eng, fn, deps=()):
        dl = [d for d in deps if d is not None] + self.pending_barrier
        o = Op(eng, fn, dl, len(self.ops[eng]), False)
        self.ops[eng].append(o)
        self.all.append(o)
        return o

    def dma(self, eng, fn, deps=(), group="g"):
        group = eng + ":" + group
        self.gcounts[group] = self.gcounts.get(group, 0) + 16
        dl = [d for d in deps if d is not None] + self.pending_barrier
        o = Op(eng, fn, dl, len(self.ops[eng]), True, group=group, gcount=self.gcounts[group])
        self.ops[eng].append(o)
        self.all.append(o)
        self.group_last[group] = o
        return o

    def next_phase(self):
        nc = self.p.nc
        new_ops = {e: self.ops[e][self.emitted[e]:] for e in ENGINES}
        barrier = []
        groups_touched = set()
        for e in ENGINES:
            if new_ops[e]:
                barrier.append(new_ops[e][-1])
            for o in new_ops[e]:
                if o.is_dma:
                    groups_touched.add(o.group)
        for g in groups_touched:
            barrier.append(self.group_last[g])
        for e in ENGINES:
            for o in new_ops[e]:
                for d in o.deps:
                    if not d.is_dma and not (d.eng == "tensor" and o.eng == "tensor"):
                        d._needed = True
        for t in barrier:
            if not t.is_dma and t.eng != "tensor":
                t._needed = True
        for e in ENGINES:
            for o in new_ops[e]:
                if o.is_dma:
                    continue
                self.ms_pending[e].append(o)
                if o._needed:
                    self.ms_count[e] += 1
                    o.milestone = self.ms_count[e]
                    for q in self.ms_pending[e]:
                        self.ms_of_idx[e][q.idx] = self.ms_count[e]
                    self.ms_pending[e] = []
        for e in ENGINES:
            if new_ops[e] and e not in self.sems:
                self.sems[e] = self.p.sem("S_" + e)
        for g in groups_touched:
            if g not in self.gsems:
                self.gsems[g] = self.p.sem("G_" + g.replace(":", "_"))
        sched = self

        def section(eng_name):
            ops = new_ops[eng_name]

            def run(engine):
                wm = sched.waited_ms[eng_name]
                wd = sched.waited_dma[eng_name]
                for o in ops:
                    for d in o.deps:
                        if d.is_dma:
                            if wd.get(d.group, 0) < d.gcount:
                                engine.wait_ge(sched.gsems[d.group], d.gcount)
                                wd[d.group] = d.gcount
                        elif not (d.eng == "tensor" and eng_name == "tensor"):
                            m = sched.ms_of_idx[d.eng].get(d.idx)
                            assert m is not None, f"dep of {eng_name} on {d.eng}#{d.idx} lacks milestone"
                            if d.eng == eng_name and d.idx < o.idx and m > sched.ms_count[eng_name]:
                                m = None
                            if m is not None and wm[d.eng] < m:
                                if not (d.eng == eng_name and sched._self_ok(d, o)):
                                    engine.wait_ge(sched.sems[d.eng], m)
                                    wm[d.eng] = m
                    inst = o.fn(engine)
                    if o.is_dma:
                        inst.then_inc(sched.gsems[o.group], 16)
                    elif o.milestone is not None:
                        inst.then_inc(sched.sems[o.eng], 1)
            return run

        with nc.Block() as block:
            for e in ENGINES:
                if not new_ops[e]:
                    continue
                run = section(e)
                getattr(block, e)(run)
        for e in ENGINES:
            self.emitted[e] = len(self.ops[e])
        self.pending_barrier = barrier

    def _self_ok(self, d, o):
        return False

    def emit_tail(self, out_dmas):
        nc = self.p.nc
        tail = {}
        for t in out_dmas:
            tail[t.group] = max(tail.get(t.group, 0), self.gcounts[t.group])
        sched = self

        with nc.Block() as block:
            def run(engine):
                for g, c in tail.items():
                    engine.wait_ge(sched.gsems[g], c)
            block.sync(run)

    def finalize(self, out_dmas):
        self.next_phase()
        self.emit_tail(out_dmas)



def _phase_io(p, io, name, shape, dt=F32, out=False):
    if io is not None:
        return io[name]
    return (p.dram_out if out else p.dram_in)(name, shape, dt)


def build_ln(p=None, S=None, io=None):
    standalone = p is None
    if standalone:
        p = Prog("ln")
        S = Sched(p)
    x = _phase_io(p, io, "x", [D, NTOK])
    xn = _phase_io(p, io, "xn", [D, NTOK], BF16, out=True)
    xn32 = _phase_io(p, io, "xn32", [D, NTOK], out=True)

    xt = [p.sb(f"xt{i}", [128, TK]) for i in range(2)]
    sqt = [p.sb(f"sqt{i}", [128, TK]) for i in range(2)]
    x2 = [p.sb(f"x2_{i}", [128, TK]) for i in range(2)]
    xnt = [p.sb(f"xnt{i}", [128, TK], BF16) for i in range(2)]
    xnt32 = [p.sb(f"xnt32_{i}", [128, TK]) for i in range(2)]
    tmpt = [p.sb(f"tmpt{i}", [128, TK]) for i in range(2)]
    ones = p.sb("ones", [128, 1])
    eps1 = p.sb("eps1", [1, 1])
    ones_r = p.sb("ones_r", [1, 128])
    Srow = p.sb("Srow", [1, NTOK])
    Qrow = p.sb("Qrow", [1, NTOK])
    mu = p.sb("mu", [1, NTOK])
    var = p.sb("var", [1, NTOK])
    rstd = p.sb("rstd", [1, NTOK])
    psS = [p.ps(f"psS{i}", [1, TK]) for i in range(2)]
    psQ = [p.ps(f"psQ{i}", [1, TK]) for i in range(2)]
    psB = [p.ps(f"psB{i}", [128, TK]) for i in range(2)]

    t_o1 = S.op("vector", lambda v: v.memset(ones[:], 1.0))
    t_o2 = S.op("vector", lambda v: v.memset(eps1[:], float(LN_EPS)))
    t_o3 = S.op("vector", lambda v: v.memset(ones_r[:], 1.0))

    ra = lambda a: a.rearrange("(c p) t -> c p t", p=128)
    war = {}
    i = 0
    ev_toks = []
    for tk in range(NTK):
        mm_last = None
        for dc in range(DC):
            b = i % 2
            tx = S.dma("gpsimd", lambda g, dc=dc, tk=tk, b=b: g.dma_start(
                out=xt[b][:], in_=ra(x)[dc, :, tk * TK:(tk + 1) * TK]),
                deps=[war.get(("mmx", b))], group="lx" + str(b))
            sq = S.op("scalar", lambda s, b=b: s.activation(out=sqt[b][:], in_=xt[b][:], func=AF.Square),
                      deps=[tx, war.get(("mmq", b))])
            m1 = S.op("tensor", lambda t, b=b, dc=dc, tk=tk: t.matmul(
                psS[tk % 2][:], ones[:], xt[b][:], start=(dc == 0), stop=(dc == DC - 1)),
                deps=[tx, t_o1, war.get(("evS", tk % 2)) if dc == 0 else None])
            m2 = S.op("tensor", lambda t, b=b, dc=dc, tk=tk: t.matmul(
                psQ[tk % 2][:], ones[:], sqt[b][:], start=(dc == 0), stop=(dc == DC - 1)),
                deps=[sq, war.get(("evQ", tk % 2)) if dc == 0 else None])
            war[("mmx", b)] = m2
            war[("mmq", b)] = m2
            mm_last = m2
            i += 1
        e1 = S.op("vector", lambda v, tk=tk: v.tensor_copy(
            out=Srow[:, tk * TK:(tk + 1) * TK], in_=psS[tk % 2][:]), deps=[mm_last])
        e2 = S.op("vector", lambda v, tk=tk: v.tensor_copy(
            out=Qrow[:, tk * TK:(tk + 1) * TK], in_=psQ[tk % 2][:]), deps=[mm_last, e1])
        war[("evS", tk % 2)] = e1
        war[("evQ", tk % 2)] = e2
        ev_toks.append(e2)
    r1 = S.op("vector", lambda v: v.tensor_scalar(out=mu[:], in0=Srow[:], scalar1=1.0 / D,
                                                  scalar2=None, op0=AluOpType.mult), deps=ev_toks[-1:])
    r2 = S.op("vector", lambda v: v.tensor_tensor(out=var[:], in0=mu[:], in1=mu[:],
                                                  op=AluOpType.mult), deps=[r1])
    r3 = S.op("vector", lambda v: v.scalar_tensor_tensor(out=var[:], in0=Qrow[:], scalar=1.0 / D,
                                                         in1=var[:], op0=AluOpType.mult,
                                                         op1=AluOpType.subtract), deps=[r2])
    r4 = S.op("scalar", lambda s: s.activation(out=var[:], in_=var[:], func=AF.Ln, bias=eps1[:]),
              deps=[r3, t_o2])
    r5 = S.op("scalar", lambda s: s.activation(out=rstd[:], in_=var[:], func=AF.Exp, scale=-0.5),
              deps=[r4])
    outs = []
    i = 0
    for tk in range(NTK):
        bc1 = S.op("tensor", lambda t, tk=tk: t.matmul(
            psB[0][:], ones_r[:], mu[:, tk * TK:(tk + 1) * TK], start=True, stop=True),
            deps=[r1, t_o3, war.get("sub_last")])
        bc2 = S.op("tensor", lambda t, tk=tk: t.matmul(
            psB[1][:], ones_r[:], rstd[:, tk * TK:(tk + 1) * TK], start=True, stop=True),
            deps=[r5, t_o3, war.get("mul_last")])
        for dc in range(DC):
            b = i % 2
            tx2 = S.dma("gpsimd", lambda g, dc=dc, tk=tk, b=b: g.dma_start(
                out=x2[b][:], in_=ra(x)[dc, :, tk * TK:(tk + 1) * TK]),
                deps=[war.get(("sub2", b))], group="l2" + str(b))
            o1 = S.op("vector", lambda v, b=b: v.tensor_tensor(
                out=tmpt[b][:], in0=x2[b][:], in1=psB[0][:], op=AluOpType.subtract),
                deps=[tx2, bc1, war.get(("mul2", b))])
            o2 = S.op("vector", lambda v, b=b: v.tensor_tensor(
                out=xnt32[b][:], in0=tmpt[b][:], in1=psB[1][:], op=AluOpType.mult),
                deps=[o1, bc2, war.get(("st32", b))])
            o3 = S.op("vector", lambda v, b=b: v.tensor_copy(out=xnt[b][:], in_=xnt32[b][:]),
                      deps=[o2, war.get(("stbf", b))])
            s1_ = S.dma("sync", lambda sy, dc=dc, tk=tk, b=b: sy.dma_start(
                out=ra(xn)[dc, :, tk * TK:(tk + 1) * TK], in_=xnt[b][:]),
                deps=[o3], group="lob" + str(b))
            s2_ = S.dma("sync", lambda sy, dc=dc, tk=tk, b=b: sy.dma_start(
                out=ra(xn32)[dc, :, tk * TK:(tk + 1) * TK], in_=xnt32[b][:]),
                deps=[o2, o3], group="lo3" + str(b))
            war[("sub2", b)] = o1
            war[("mul2", b)] = o2
            war[("st32", b)] = s2_
            war[("stbf", b)] = s1_
            war["sub_last"] = o1
            war["mul_last"] = o2
            outs += [s1_, s2_]
            i += 1
    if standalone:
        S.finalize(outs[-4:])
        return p.done()
    return outs



def build_l0att(p=None, S=None, io=None):
    TKA = 256
    NCH = NTOK // TKA
    G = 64
    NG = TKA // G
    standalone = p is None
    if standalone:
        p = Prog("l0att")
        S = Sched(p)
    nc = p.nc
    xn = _phase_io(p, io, "xn", [D, NTOK], BF16)
    wq = _phase_io(p, io, "wq", [D, D])
    wd = _phase_io(p, io, "wd", [D, 256])
    wu = _phase_io(p, io, "wu", [256, 2048])
    wo = _phase_io(p, io, "wo", [D, D])
    bqr = _phase_io(p, io, "bqr", [1, D])
    bdr = _phase_io(p, io, "bdr", [1, 256])
    bur = _phase_io(p, io, "bur", [1, 2048])
    bor = _phase_io(p, io, "bor", [1, D])
    ropec = _phase_io(p, io, "ropec", [128, T])
    ropes = _phase_io(p, io, "ropes", [128, T])
    s0 = _phase_io(p, io, "s0", [D, NTOK], out=True)

    wq_b = p.sb("wq_b", [128, DC, D], BF16)
    wd_b = p.sb("wd_b", [128, DC, 256], BF16)
    wu_b = p.sb("wu_b", [128, 2, 2048], BF16)
    wo_b = p.sb("wo_b", [128, DC, D], BF16)
    bq_s = p.sb("bq_s", [1, D])
    bd_s = p.sb("bd_s", [1, 256])
    bu_s = p.sb("bu_s", [1, 2048])
    bo_s = p.sb("bo_s", [1, D])
    onesTK = p.sb("onesTK", [1, 256])
    rc_s = p.sb("rc_s", [128, T])
    rs_s = p.sb("rs_s", [128, T])
    ones16 = p.sb("ones16", [16, 1])
    onesr16 = p.sb("onesr16", [1, 16])

    xnc = p.sb("xnc", [128, DC, TKA], BF16)
    qT = p.sb("qT", [128, DC, TKA])
    kT = p.sb("kT", [128, DC, TKA])
    vT = p.sb("vT", [128, DC, TKA])
    kvl = p.sb("kvl", [128, 2, TKA], BF16)
    swp = p.sb("swp", [128, DC, TKA])
    tmpq = p.sb("tmpq", [128, DC * TKA])
    tmpk = tmpq
    Qb = p.sb("Qb", [64, H * TKA])
    Kb = p.sb("Kb", [64, H * TKA])
    Vb = p.sb("Vb", [16, DH * 64])
    exps = p.sb("exps", [16, G * 16])
    recip = p.sb("recip", [1, G * 16])
    attsm = p.sb("attsm", [16, G * 16])
    ao_bf = p.sb("ao_bf", [128, DC, TKA], BF16)
    s0t = [p.sb(f"s0t{i}", [128, TKA]) for i in range(2)]

    psA = p.ps("psA", [128, TKA])[:]
    psB = p.ps("psB", [128, TKA])[:]
    attps = p.ps("attps", [16, G * 16])
    nsps = p.ps("nsps", [1, 512])
    repps = p.ps("repps", [16, 512])
    outps = p.ps("outps", [64, G * 16])

    S.dma("gpsimd", lambda g: g.dma_start(out=wq_b[:], in_=wq.rearrange("(c p) o -> p c o", p=128)), group="setup")
    S.dma("gpsimd", lambda g: g.dma_start(out=wd_b[:], in_=wd.rearrange("(c p) o -> p c o", p=128)), group="setup")
    S.dma("gpsimd", lambda g: g.dma_start(out=wu_b[:], in_=wu.rearrange("(c p) o -> p c o", p=128)), group="setup")
    t_setup_g = S.dma("gpsimd", lambda g: g.dma_start(out=wo_b[:], in_=wo.rearrange("(c p) o -> p c o", p=128)), group="setup")
    S.dma("sync", lambda s: s.dma_start(out=bq_s[:], in_=bqr[:]), group="setup")
    S.dma("sync", lambda s: s.dma_start(out=bd_s[:], in_=bdr[:]), group="setup")
    S.dma("sync", lambda s: s.dma_start(out=bu_s[:], in_=bur[:]), group="setup")
    S.dma("sync", lambda s: s.dma_start(out=bo_s[:], in_=bor[:]), group="setup")
    S.dma("sync", lambda s: s.dma_start(out=rc_s[:], in_=ropec[:]), group="setup")
    t_setup = S.dma("sync", lambda s: s.dma_start(out=rs_s[:], in_=ropes[:]), group="setup")
    t_wq = t_wd = t_wu = t_wo = t_setup_g
    t_bq = t_bd = t_bu = t_bo = t_rc = t_rs = t_setup
    t_o1 = S.op("vector", lambda v: v.memset(ones16[:], 1.0))
    t_o2 = S.op("vector", lambda v: v.memset(onesr16[:], 1.0))
    t_o3 = S.op("vector", lambda v: v.memset(onesTK[:], 1.0))

    last_xnc_use = None
    last_qkv_use = [None]
    last_blk_use = None
    last_exp = None
    last_ns_use = None
    last_rep_use = None
    last_out_use = None
    last_wout = None
    last_s0_store = [None, None]
    last_evacA = None
    last_evacB = None
    last_ropek_done = [None]

    def MM(ps_, lhsT, rhs, start, stop, deps=()):
        return S.op("tensor", lambda t, a=ps_, b=lhsT, c=rhs, s1=start, s2=stop:
                    t.matmul(a, b, c, start=s1, stop=s2), deps=deps)

    for tk in range(NCH):
        b = tk // (T // TKA)
        tau0 = (tk % (T // TKA)) * TKA
        sl = slice(tk * TKA, (tk + 1) * TKA)

        t_xnc = S.dma("gpsimd", lambda g, sl=sl: g.dma_start(
            out=xnc[:], in_=xn.rearrange("(c p) t -> p c t", p=128)[:, :, sl]),
            deps=[last_xnc_use], group="xnc")

        q_evacs, kvl_evacs, k_evacs, v_evacs = [], [], [], []
        pj = 0

        def do_proj(w_b, w_tok, nin, rhs_sb, rhs_tok, ocol, dst_fn, bias_s, bias_tok, evlist):
            nonlocal pj, last_evacA, last_evacB
            ps_ = psA if (pj % 2 == 0) else psB
            war = last_evacA if (pj % 2 == 0) else last_evacB
            for ic in range(nin):
                deps = [rhs_tok, w_tok]
                if ic == 0:
                    deps.append(war)
                MM(ps_, w_b[:, ic, ocol * 128:(ocol + 1) * 128], rhs_sb[:, ic, :],
                   start=(ic == 0), stop=False, deps=deps)
            mmt = MM(ps_, bias_s[:, ocol * 128:(ocol + 1) * 128], onesTK[:],
                     start=False, stop=True, deps=[bias_tok, t_o3])
            ev = S.op("scalar", lambda s, ps2=ps_: s.copy(out=dst_fn(), in_=ps2),
                      deps=[mmt, last_qkv_use[0]])
            evlist.append(ev)
            if pj % 2 == 0:
                last_evacA = ev
            else:
                last_evacB = ev
            pj += 1

        for oc in range(DC):
            do_proj(wq_b, t_wq, DC, xnc, t_xnc, oc,
                    (lambda oc=oc: qT[:, oc, :]), bq_s, t_bq, q_evacs)
        for oc in range(2):
            do_proj(wd_b, t_wd, DC, xnc, t_xnc, oc,
                    (lambda oc=oc: kvl[:, oc, :]), bd_s, t_bd, kvl_evacs)
        for j in range(16):
            dst = kT if j < 8 else vT
            oc = j % 8
            do_proj(wu_b, t_wu, 2, kvl, kvl_evacs[-1], j,
                    (lambda dst=dst, oc=oc: dst[:, oc, :]), bu_s, t_bu,
                    k_evacs if j < 8 else v_evacs)

        SWAPS = [(32, 0), (0, 32), (96, 64), (64, 96)]
        qsw = []
        for dc in range(DC):
            for (src0, dst0) in SWAPS:
                qsw.append(S.dma("sync", lambda s, dc=dc, s0_=src0, d0=dst0: s.dma_start(
                    out=swp[d0:d0 + 32, dc, :], in_=qT[s0_:s0_ + 32, dc, :]),
                    deps=[q_evacs[dc], last_ropek_done[0]], group="qsw"))
        ropeq = []
        for dc in range(DC):
            a1 = S.op("vector", lambda v, dc=dc: v.tensor_tensor(
                out=tmpq[:, dc * TKA:(dc + 1) * TKA], in0=qT[:, dc, :],
                in1=rc_s[:, tau0:tau0 + TKA], op=AluOpType.mult),
                deps=[q_evacs[dc], t_rc, qsw[-1]])
            a2 = S.op("vector", lambda v, dc=dc: v.tensor_tensor(
                out=qT[:, dc, :], in0=swp[:, dc, :],
                in1=rs_s[:, tau0:tau0 + TKA], op=AluOpType.mult),
                deps=[qsw[-1], t_rs, a1])
            a3 = S.op("vector", lambda v, dc=dc: v.tensor_tensor(
                out=qT[:, dc, :], in0=qT[:, dc, :],
                in1=tmpq[:, dc * TKA:(dc + 1) * TKA], op=AluOpType.add), deps=[a1, a2])
            ropeq.append(a3)
        ksw = []
        for dc in range(DC):
            for (src0, dst0) in SWAPS:
                ksw.append(S.dma("sync", lambda s, dc=dc, s0_=src0, d0=dst0: s.dma_start(
                    out=swp[d0:d0 + 32, dc, :], in_=kT[s0_:s0_ + 32, dc, :]),
                    deps=[k_evacs[dc], ropeq[-1] if dc == 0 else None, ropeq[dc]], group="ksw"))
        ropek = []
        for dc in range(DC):
            a1 = S.op("gpsimd", lambda g, dc=dc: g.tensor_tensor(
                out=tmpk[:, dc * TKA:(dc + 1) * TKA], in0=kT[:, dc, :],
                in1=rc_s[:, tau0:tau0 + TKA], op=AluOpType.mult),
                deps=[k_evacs[dc], t_rc, ksw[-1], ropeq[-1]])
            a2 = S.op("gpsimd", lambda g, dc=dc: g.tensor_tensor(
                out=kT[:, dc, :], in0=swp[:, dc, :],
                in1=rs_s[:, tau0:tau0 + TKA], op=AluOpType.mult),
                deps=[ksw[-1], t_rs, a1])
            a3 = S.op("gpsimd", lambda g, dc=dc: g.tensor_tensor(
                out=kT[:, dc, :], in0=kT[:, dc, :],
                in1=tmpk[:, dc * TKA:(dc + 1) * TKA], op=AluOpType.add), deps=[a1, a2])
            ropek.append(a3)
        last_ropek_done[0] = ropek[-1]

        asmQ, asmK = [], []
        for h in range(H):
            asmQ.append(S.dma("gpsimd", lambda g, h=h: g.dma_start(
                out=Qb[:, h * TKA:(h + 1) * TKA],
                in_=qT[(h % 2) * 64:(h % 2) * 64 + 64, h // 2, :]),
                deps=[ropeq[h // 2], last_blk_use], group="asm"))
            asmK.append(S.dma("gpsimd", lambda g, h=h: g.dma_start(
                out=Kb[:, h * TKA:(h + 1) * TKA],
                in_=kT[(h % 2) * 64:(h % 2) * 64 + 64, h // 2, :]),
                deps=[ropek[h // 2], last_blk_use], group="asm"))
        stab = []
        prev = None
        for g_ in list(range(15, 0, -1)) + [0]:
            prev = S.op("vector", lambda v, g_=g_: v.tensor_tensor(
                out=Kb[:, g_ * TKA:(g_ + 1) * TKA], in0=Kb[:, g_ * TKA:(g_ + 1) * TKA],
                in1=Kb[:, 0:TKA], op=AluOpType.subtract),
                deps=[asmK[g_], asmK[0]] + ([prev] if prev else []))
            stab.append(prev)
        stab_done = stab[-1]

        p2_evac_tokens = []
        att_last_group = None
        for gi in range(NG):
            t0_ = gi * G
            asmV = []
            for h in range(H):
                asmV.append(S.dma("gpsimd", lambda g, h=h, gi=gi: g.dma_start(
                    out=Vb[h:h + 1, :],
                    in_=vT[(h % 2) * 64:(h % 2) * 64 + 64, h // 2, gi * G:(gi + 1) * G]),
                    deps=[v_evacs[h // 2], last_blk_use], group="asmv"))
            att_mms = []
            for ti in range(G):
                tok = t0_ + ti
                deps = [stab_done, asmK[-1]]
                if ti == 0:
                    deps += [last_exp, att_last_group]
                att_mms.append(S.op("tensor", lambda t, tok=tok, ti=ti: t.matmul(
                    attps[:, ti * 16:(ti + 1) * 16],
                    Kb.rearrange("p (h t) -> p t h", t=TKA)[:, tok, :],
                    Qb.rearrange("p (h t) -> p t h", t=TKA)[:, tok, :],
                    start=True, stop=True), deps=deps))
            att_last_group = att_mms[-1]
            exp_t = S.op("scalar", lambda s: s.activation(
                out=exps[:], in_=attps[:], func=AF.Exp, scale=0.125),
                deps=[att_mms[-1], last_rep_use, last_ns_use])
            last_exp = exp_t
            rec_ts = []
            for hf in range(2):
                ns_mm = S.op("tensor", lambda t, hf=hf: t.matmul(
                    nsps[:], ones16[:],
                    exps[:, hf * 512:(hf + 1) * 512], start=True, stop=True),
                    deps=[exp_t, t_o1, last_ns_use])
                rec_t = S.op("vector", lambda v, hf=hf: v.reciprocal(
                    out=recip[:, hf * 512:(hf + 1) * 512], in_=nsps[:]), deps=[ns_mm])
                last_ns_use = rec_t
                rec_ts.append(rec_t)
            sm_t = None
            for hf in range(2):
                rep_mm = S.op("tensor", lambda t, hf=hf: t.matmul(
                    repps[:], onesr16[:],
                    recip[:, hf * 512:(hf + 1) * 512], start=True, stop=True),
                    deps=[rec_ts[hf], t_o2, last_rep_use])
                sm_t = S.op("vector", lambda v, hf=hf: v.tensor_tensor(
                    out=attsm[:, hf * 512:(hf + 1) * 512],
                    in0=exps[:, hf * 512:(hf + 1) * 512], in1=repps[:], op=AluOpType.mult),
                    deps=[rep_mm, exp_t])
                last_rep_use = sm_t
            p2_mms = []
            for ti in range(G):
                tok = t0_ + ti
                deps = [sm_t, asmV[-1]]
                if ti == 0:
                    deps.append(last_out_use)
                p2_mms.append(S.op("tensor", lambda t, ti=ti: t.matmul(
                    outps[:, ti * 16:(ti + 1) * 16],
                    Vb.rearrange("p (d t) -> p t d", t=G)[:, ti, :],
                    attsm[:, ti * 16:(ti + 1) * 16],
                    start=True, stop=True), deps=deps))
            last_blk_use = p2_mms[-1]
            evs = []
            for par in range(2):
                evs.append(S.op("vector", lambda v, par=par, gi=gi: v.tensor_copy(
                    out=ao_bf[par * 64:par * 64 + 64, :, gi * G:(gi + 1) * G],
                    in_=outps.rearrange("p (t h) -> p h t", h=16)[:, par::2, :]),
                    deps=[p2_mms[-1], last_wout]))
            last_out_use = evs[-1]
            p2_evac_tokens += evs

        wout_mms_last = None
        for oc in range(DC):
            ps_ = psA if (oc % 2 == 0) else psB
            war = last_evacA if (oc % 2 == 0) else last_evacB
            for dc in range(DC):
                deps = [p2_evac_tokens[-1], t_wo]
                if dc == 0:
                    deps.append(war)
                MM(ps_, wo_b[:, dc, oc * 128:(oc + 1) * 128], ao_bf[:, dc, :],
                   start=(dc == 0), stop=False, deps=deps)
            mmt = MM(ps_, bo_s[:, oc * 128:(oc + 1) * 128], onesTK[:],
                     start=False, stop=True, deps=[t_bo, t_o3])
            ev = S.op("scalar", lambda s, ps2=ps_, oc=oc: s.copy(out=s0t[oc % 2][:], in_=ps2),
                      deps=[mmt, last_s0_store[oc % 2]])
            st_ = S.dma("sync", lambda s, oc=oc, sl=sl: s.dma_start(
                out=s0.rearrange("(c p) t -> c p t", p=128)[oc, :, sl], in_=s0t[oc % 2][:]),
                deps=[ev], group=("s0a" if oc % 2 == 0 else "s0b"))
            last_s0_store[oc % 2] = st_
            wout_mms_last = mmt
            if oc % 2 == 0:
                last_evacA = ev
            else:
                last_evacB = ev
        last_wout = wout_mms_last
        last_xnc_use = kvl_evacs[-1]
        last_qkv_use[0] = asmV[-1]

    if standalone:
        S.finalize([last_s0_store[0], last_s0_store[1]])
        return p.done()
    return [last_s0_store[0], last_s0_store[1]]



def build_router(kind, p=None, S=None, io=None):
    standalone = p is None
    if standalone:
        p = Prog("rt_" + kind)
        S = Sched(p)
    nc = p.nc
    xn = _phase_io(p, io, "xn", [D, NTOK])
    rw = _phase_io(p, io, "rw", [D, E])
    brr = _phase_io(p, io, "brr", [1, E])
    sc8 = _phase_io(p, io, "sc8", [1, E])
    ident = _phase_io(p, io, "ident", [128, 128])
    comb = _phase_io(p, io, "comb", [E, NTOK], out=True)

    rw_s = p.sb("rw_s", [128, DC, E])
    br_s = p.sb("br_s", [1, E])
    sc_s = p.sb("sc_s", [1, E])
    id_s = p.sb("id_s", [128, 128])
    onesr = p.sb("onesr", [1, 128])
    xt = [p.sb(f"xt{i}", [128, DC, 128]) for i in range(2)]
    rt = p.sb("rt", [128, E])
    pr = p.sb("pr", [128, E])
    pr2 = p.sb("pr2", [128, E])
    ge = p.sb("ge", [128, E])
    cu = p.sb("cu", [128, E])
    cu2 = p.sb("cu2", [128, E])
    mk1 = p.sb("mk1", [128, E])
    mk2 = p.sb("mk2", [128, E])
    m1 = p.sb("m1", [128, 1])
    m2 = p.sb("m2", [128, 1])
    nm = p.sb("nm", [128, 1])
    sm = p.sb("sm", [128, 1])
    rc = p.sb("rc", [128, 1])
    ee = p.sb("ee", [128, 2])
    w12 = p.sb("w12", [128, 2])
    cmb = p.sb("cmb", [128, E])
    screp = p.sb("screp", [128, E])
    crow = [p.sb(f"crow{i}", [8, 128]) for i in range(2)]
    psR = p.ps("psR", [128, E])
    psS = p.ps("psS", [128, E])
    psT = p.ps("psT", [128, 128])

    S.dma("sync", lambda s: s.dma_start(out=rw_s[:], in_=rw.rearrange("(c p) e -> p c e", p=128)), group="setup")
    S.dma("sync", lambda s: s.dma_start(out=br_s[:], in_=brr[:]), group="setup")
    S.dma("sync", lambda s: s.dma_start(out=sc_s[:], in_=sc8[:]), group="setup")
    t_set = S.dma("sync", lambda s: s.dma_start(out=id_s[:], in_=ident[:]), group="setup")
    t_on = S.op("vector", lambda v: v.memset(onesr[:], 1.0))
    t_screp_mm = S.op("tensor", lambda t: t.matmul(psS[:], onesr[:], sc_s[:], start=True, stop=True),
                      deps=[t_set, t_on])
    t_screp = S.op("vector", lambda v: v.tensor_copy(out=screp[:], in_=psS[:]), deps=[t_screp_mm])

    NTI = NTOK // 128
    last_store = [None, None]
    last_mm_use = [None]
    prev_chain = None
    for ti in range(NTI):
        sl = slice(ti * 128, (ti + 1) * 128)
        t_x = S.dma("gpsimd", lambda g, sl=sl, ti=ti: g.dma_start(
            out=xt[ti % 2][:], in_=xn.rearrange("(c p) t -> p c t", p=128)[:, :, sl]),
            deps=[last_mm_use[0]], group="xt" + str(ti % 2))
        mmt = None
        for dc in range(DC):
            mmt = S.op("tensor", lambda t, dc=dc, ti=ti: t.matmul(
                psR[:], xt[ti % 2][:, dc, :], rw_s[:, dc, :],
                start=(dc == 0), stop=False), deps=[t_x, t_set, prev_chain])
        mmt = S.op("tensor", lambda t: t.matmul(psR[:], onesr[:], br_s[:],
                                                start=False, stop=True),
                   deps=[t_set, t_on])
        last_mm_use[0] = mmt
        ops = []

        def V(fn, deps):
            o = S.op("vector", fn, deps)
            ops.append(o)
            return o

        cp = V(lambda v, ti=ti: v.tensor_copy(out=rt[:], in_=psR[:]), [mmt, prev_chain])
        if kind == "l0":
            mx = V(lambda v: v.tensor_reduce(out=m1[:], in_=rt[:], axis=AX, op=AluOpType.max), [cp])
            ng = V(lambda v: v.tensor_scalar(out=nm[:], in0=m1[:], scalar1=-1.0, scalar2=None,
                                             op0=AluOpType.mult), [mx])
            ex = S.op("scalar", lambda s: s.activation(out=pr[:], in_=rt[:], func=AF.Exp, bias=nm[:]),
                      deps=[ng, cp])
            sm_ = V(lambda v: v.tensor_reduce(out=sm[:], in_=pr[:], axis=AX, op=AluOpType.add), [ex])
            rc_ = V(lambda v: v.reciprocal(out=rc[:], in_=sm[:]), [sm_])
            nrm = V(lambda v: v.tensor_scalar(out=pr[:], in0=pr[:], scalar1=rc[:], scalar2=None,
                                              op0=AluOpType.mult), [rc_, ex])
            src = pr
        else:
            sg = S.op("scalar", lambda s: s.activation(out=pr[:], in_=rt[:], func=AF.Sigmoid),
                      deps=[cp])
            src = pr
            nrm = sg
        mxa = V(lambda v: v.tensor_reduce(out=m1[:], in_=src[:], axis=AX, op=AluOpType.max), [nrm])
        g1 = V(lambda v: v.tensor_scalar(out=ge[:], in0=src[:], scalar1=m1[:], scalar2=None,
                                         op0=AluOpType.is_ge), [mxa, nrm])
        c1 = V(lambda v: v.tensor_copy(out=cu[:], in_=ge[:]), [g1])
        c2 = V(lambda v: v.tensor_tensor(out=cu2[:, 1:E], in0=cu[:, 1:E], in1=cu[:, 0:E - 1],
                                         op=AluOpType.add), [c1])
        c2b = V(lambda v: v.tensor_copy(out=cu2[:, 0:1], in_=cu[:, 0:1]), [c1, c2])
        c3 = V(lambda v: v.tensor_tensor(out=cu[:, 2:E], in0=cu2[:, 2:E], in1=cu2[:, 0:E - 2],
                                         op=AluOpType.add), [c2, c2b])
        c3b = V(lambda v: v.tensor_copy(out=cu[:, 0:2], in_=cu2[:, 0:2]), [c2, c2b, c3])
        c4 = V(lambda v: v.tensor_tensor(out=cu2[:, 4:E], in0=cu[:, 4:E], in1=cu[:, 0:E - 4],
                                         op=AluOpType.add), [c3, c3b])
        c4b = V(lambda v: v.tensor_copy(out=cu2[:, 0:4], in_=cu[:, 0:4]), [c3, c3b, c4])
        e1 = V(lambda v: v.tensor_scalar(out=cu[:], in0=cu2[:], scalar1=1.5, scalar2=None,
                                         op0=AluOpType.is_le), [c4, c4b])
        mk = V(lambda v: v.tensor_tensor(out=mk1[:], in0=ge[:], in1=cu[:], op=AluOpType.mult),
               [e1, g1])
        if kind == "l0":
            p2_ = V(lambda v: v.scalar_tensor_tensor(out=pr2[:], in0=mk1[:], scalar=-1e9, in1=pr[:],
                                                     op0=AluOpType.mult, op1=AluOpType.add), [mk, nrm])
            mx2 = V(lambda v: v.tensor_reduce(out=m2[:], in_=pr2[:], axis=AX, op=AluOpType.max), [p2_])
            g2 = V(lambda v: v.tensor_scalar(out=ge[:], in0=pr2[:], scalar1=m2[:], scalar2=None,
                                             op0=AluOpType.is_ge), [mx2, p2_, mk])
            d1 = V(lambda v: v.tensor_copy(out=cu[:], in_=ge[:]), [g2, e1])
            d2 = V(lambda v: v.tensor_tensor(out=cu2[:, 1:E], in0=cu[:, 1:E], in1=cu[:, 0:E - 1],
                                             op=AluOpType.add), [d1])
            d2b = V(lambda v: v.tensor_copy(out=cu2[:, 0:1], in_=cu[:, 0:1]), [d1, d2])
            d3 = V(lambda v: v.tensor_tensor(out=cu[:, 2:E], in0=cu2[:, 2:E], in1=cu2[:, 0:E - 2],
                                             op=AluOpType.add), [d2, d2b])
            d3b = V(lambda v: v.tensor_copy(out=cu[:, 0:2], in_=cu2[:, 0:2]), [d2, d2b, d3])
            d4 = V(lambda v: v.tensor_tensor(out=cu2[:, 4:E], in0=cu[:, 4:E], in1=cu[:, 0:E - 4],
                                             op=AluOpType.add), [d3, d3b])
            d4b = V(lambda v: v.tensor_copy(out=cu2[:, 0:4], in_=cu[:, 0:4]), [d3, d3b, d4])
            e2_ = V(lambda v: v.tensor_scalar(out=cu[:], in0=cu2[:], scalar1=1.5, scalar2=None,
                                              op0=AluOpType.is_le), [d4, d4b])
            mk2_ = V(lambda v: v.tensor_tensor(out=mk2[:], in0=ge[:], in1=cu[:], op=AluOpType.mult),
                     [e2_, g2])
            pk1 = V(lambda v: v.tensor_copy(out=ee[:, 0:1], in_=m1[:]), [mxa, mk2_])
            pk2 = V(lambda v: v.tensor_copy(out=ee[:, 1:2], in_=m2[:]), [mx2, pk1])
            exw = S.op("scalar", lambda s: s.activation(out=ee[:], in_=ee[:], func=AF.Exp),
                       deps=[pk2])
            sw = V(lambda v: v.tensor_reduce(out=sm[:], in_=ee[:], axis=AX, op=AluOpType.add), [exw])
            rw_ = V(lambda v: v.reciprocal(out=rc[:], in_=sm[:]), [sw])
            ww = V(lambda v: v.tensor_scalar(out=w12[:], in0=ee[:], scalar1=rc[:], scalar2=None,
                                             op0=AluOpType.mult), [rw_, exw])
            q1 = V(lambda v: v.tensor_scalar(out=mk1[:], in0=mk1[:], scalar1=w12[:, 0:1], scalar2=None,
                                             op0=AluOpType.mult), [ww, mk])
            q2 = V(lambda v: v.scalar_tensor_tensor(out=cmb[:], in0=mk2[:], scalar=w12[:, 1:2],
                                                    in1=mk1[:], op0=AluOpType.mult, op1=AluOpType.add),
                   [q1, mk2_])
            fin = V(lambda v: v.tensor_tensor(out=cmb[:], in0=cmb[:], in1=screp[:], op=AluOpType.mult),
                    [q2, t_screp])
        else:
            fin = V(lambda v: v.tensor_scalar(out=cmb[:], in0=mk1[:], scalar1=m1[:], scalar2=None,
                                              op0=AluOpType.mult), [mk, mxa])
        trm = S.op("tensor", lambda t: t.transpose(psT[0:8, 0:128], cmb[:], id_s[:]),
                   deps=[fin, t_set, last_store[ti % 2]])
        evc = V(lambda v, ti=ti: v.tensor_copy(out=crow[ti % 2][:], in_=psT[0:8, 0:128]), [trm])
        st = S.dma("sync", lambda s, sl=sl, ti=ti: s.dma_start(out=comb[:, sl], in_=crow[ti % 2][:]),
                   deps=[evc], group="cst" + str(ti % 2))
        last_store[ti % 2] = st
        prev_chain = fin

    if standalone:
        S.finalize([last_store[0], last_store[1]])
        return p.done()
    return [last_store[0], last_store[1]]



def build_moe(act, p=None, S=None, io=None):
    TM = 512
    NTS = NTOK // TM
    AFUNC = AF.Silu if act == "silu" else AF.Gelu
    standalone = p is None
    if standalone:
        p = Prog("moe_" + act)
        S = Sched(p)
    nc = p.nc
    xn = _phase_io(p, io, "xn", [D, NTOK], BF16)
    w1 = _phase_io(p, io, "w1", [E, D, FF])
    w2 = _phase_io(p, io, "w2", [E, FF, D])
    comb = _phase_io(p, io, "comb", [E, NTOK])
    ymoe = _phase_io(p, io, "ymoe", [D, NTOK], out=True)
    w1bf = nc.dram_tensor("sc_w1bf_" + act, [E, DC, 128, DC, TM], BF16, kind="Internal").ap()
    w2bf = nc.dram_tensor("sc_w2bf_" + act, [E, DC, 128, 4, D], BF16, kind="Internal").ap()

    w2f = p.sb("w2f", [128, FC, D], BF16)
    w1f = p.sb("w1f", [128, DC, FF], BF16)
    xnt = [p.sb(f"xnt{i}", [128, DC, TM], BF16) for i in range(2)]
    hsb = p.sb("hsb", [128, FC, TM], BF16)
    cstb = p.sb("cstb", [128, DC, TM], BF16)
    cbr = p.sb("cbr", [128, TM])
    comb_sb = p.sb("comb_sb", [1, TM])
    onesr = p.sb("onesr", [1, 128])
    yt = [p.sb("yt0", [128, DC, TM])]
    yps = [p.ps(f"yps{i}", [128, TM]) for i in range(8)]

    t_on = S.op("vector", lambda v: v.memset(onesr[:], 1.0))

    last_w1_use = [None]
    last_xn_use = [None, None]
    last_silu = [None] * 4
    last_yevac = [None] * 8
    last_yt_use = [None]
    last_phaseB = [None]
    last_cl = [None]
    last_yev_last = [None]
    last_cst = [None]
    cast_tok = {}
    region_last = {}
    prev_acc = {}

    def cast_expert(e):
        for kind in ("w1", "w2"):
            for fc in range(DC):
                if kind == "w1":
                    src = w1.rearrange("e (dc p) f -> e p dc f", p=128)[e, :, :, fc * TM:(fc + 1) * TM]
                    dst = w1bf[e, fc]
                    cview = lambda: cstb[:]
                else:
                    src = w2.rearrange("e (fc fp p) o -> e fc p fp o", fc=DC, p=128)[e, fc]
                    dst = w2bf[e, fc]
                    cview = lambda: cstb[:].rearrange("p a b -> p (a b)").rearrange("p (x y) -> p x y", x=4)
                ld = S.dma("gpsimd", lambda g, src=src, cview=cview: g.dma_start(
                    out=cview(), in_=src), deps=[last_cst[0]], group="cst")
                wr = S.dma("gpsimd", lambda g, dst=dst, cview=cview: g.dma_start(
                    out=dst, in_=cview()), deps=[ld], group="cwr")
                last_cst[0] = wr
                cast_tok[(e, kind, fc)] = wr

    for e in range(E):
        if e == 0:
            cast_expert(0)
        if e + 1 < E:
            cast_expert(e + 1)
        w1ld, w2ld = [], []
        for fc in range(DC):
            w1ld.append(S.dma("sync", lambda s, e=e, fc=fc: s.dma_start(
                out=w1f[:, :, fc * TM:(fc + 1) * TM], in_=w1bf[e, fc]),
                deps=[cast_tok[(e, "w1", fc)], last_w1_use[0]], group="w1l"))
        for fc in range(DC):
            w2ld.append(S.dma("sync", lambda s, e=e, fc=fc: s.dma_start(
                out=w2f[:, fc * 4:(fc + 1) * 4, :], in_=w2bf[e, fc]),
                deps=[cast_tok[(e, "w2", fc)], last_phaseB[0]], group="w2l"))
        for ts in range(NTS):
            gi = e * NTS + ts
            xb = gi % 2
            sl = slice(ts * TM, (ts + 1) * TM)
            t_xn = S.dma("sync", lambda s, sl=sl, xb=xb: s.dma_start(
                out=xnt[xb][:], in_=xn.rearrange("(c p) t -> p c t", p=128)[:, :, sl]),
                deps=[last_xn_use[xb]], group="xn" + str(xb))
            t_cl = S.dma("sync", lambda s, e=e, sl=sl: s.dma_start(
                out=comb_sb[:], in_=comb[e:e + 1, sl]),
                deps=[last_cl[0]], group="cmb")
            silu_done = []
            first_hb = [True] * 4
            for fc in range(DC):
                for fp in range(4):
                    fpg = fc * 4 + fp
                    hb = fpg % 4
                    mmt = None
                    for dc in range(DC):
                        deps = [w1ld[fc], t_xn]
                        if dc == 0:
                            deps.append(last_silu[hb])
                            if first_hb[hb]:
                                deps.append(last_yevac[hb])
                                first_hb[hb] = False
                        mmt = S.op("tensor", lambda t, fpg=fpg, dc=dc, hb=hb, xb=xb: t.matmul(
                            yps[hb][:], w1f[:, dc, fpg * 128:(fpg + 1) * 128],
                            xnt[xb][:, dc, :],
                            start=(dc == 0), stop=(dc == DC - 1)), deps=deps)
                    sv = S.op("scalar", lambda s, fpg=fpg, hb=hb: s.activation(
                        out=hsb[:, fpg, :], in_=yps[hb][:], func=AFUNC),
                        deps=[mmt, last_phaseB[0]])
                    last_silu[hb] = sv
                    silu_done.append(sv)
            last_xn_use[xb] = mmt
            if ts == NTS - 1:
                last_w1_use[0] = mmt
            t_cbm = S.op("tensor", lambda t: t.matmul(
                yps[4][:], onesr[:], comb_sb[:],
                start=True, stop=True), deps=[t_on, t_cl, last_yevac[4]])
            e_cb1 = S.op("vector", lambda v: v.tensor_copy(
                out=cbr[:], in_=yps[4][:]), deps=[t_cbm, last_yev_last[0]])
            last_cl[0] = t_cbm
            chain_last = [None] * 8
            for fpg in range(FC):
                for dp in range(DC):
                    deps = [silu_done[-1], w2ld[fpg // 4]]
                    if fpg == 0:
                        deps.append(last_yevac[dp])
                        if dp == 4:
                            deps.append(e_cb1)
                    mmt = S.op("tensor", lambda t, dp=dp, fpg=fpg: t.matmul(
                        yps[dp][:], w2f[:, fpg, dp * 128:(dp + 1) * 128],
                        hsb[:, fpg, :],
                        start=(fpg == 0), stop=(fpg == FC - 1)), deps=deps)
                    chain_last[dp] = mmt
            for dp in range(DC):
                ev = S.op("vector", lambda v, dp=dp: v.tensor_tensor(
                    out=yt[0][:, dp, :], in0=yps[dp][:], in1=cbr[:],
                    op=AluOpType.mult),
                    deps=[chain_last[dp], e_cb1, last_yt_use[0]])
                last_yevac[dp] = ev
                last_yev_last[0] = ev
            acc = S.dma("gpsimd", lambda g, e=e, ts=ts: g.dma_start(
                out=ymoe.rearrange("(c p) t -> p c t", p=128)[:, :, ts * TM:(ts + 1) * TM],
                in_=yt[0][:], accum_op=(AluOpType.add if e > 0 else AluOpType.bypass)),
                deps=[last_yevac[0], last_yevac[1], last_yevac[2], last_yevac[3],
                      last_yevac[4], last_yevac[5], last_yevac[6], last_yevac[7],
                      prev_acc.get(ts)],
                group=f"acc{ts % 4}")
            prev_acc[ts] = acc
            last_yt_use[0] = acc
            region_last[ts % 4] = acc
            last_phaseB[0] = mmt
    if standalone:
        S.finalize(list(region_last.values()))
        return p.done()
    return list(region_last.values())



def build_ep(p=None, S=None, io=None):
    standalone = p is None
    if standalone:
        p = Prog("ep")
        S = Sched(p)
    x = _phase_io(p, io, "x", [D, NTOK])
    sT = _phase_io(p, io, "sT", [D, NTOK])
    ym = _phase_io(p, io, "ym", [D, NTOK])
    x1 = _phase_io(p, io, "x1", [D, NTOK], out=True)
    NB = 3
    xt = [p.sb(f"xt{i}", [128, DC, TK]) for i in range(NB)]
    st_ = [p.sb(f"st{i}", [128, DC, TK]) for i in range(NB)]
    yt = [p.sb(f"yt{i}", [128, DC, TK]) for i in range(NB)]
    gt = [p.sb(f"gt{i}", [128, DC, TK]) for i in range(NB)]
    last = {}
    outs = []
    ra = lambda a: a.rearrange("(c p) t -> p c t", p=128)
    for tk in range(NTK):
        b = tk % NB
        sl = slice(tk * TK, (tk + 1) * TK)
        tx = S.dma("sync", lambda s, sl=sl, b=b: s.dma_start(
            out=xt[b][:], in_=ra(x)[:, :, sl]), deps=[last.get(("o1", b))], group="ex" + str(b))
        ts_ = S.dma("sync", lambda s, sl=sl, b=b: s.dma_start(
            out=st_[b][:], in_=ra(sT)[:, :, sl]), deps=[last.get(("o0", b))], group="es" + str(b))
        ty = S.dma("gpsimd", lambda g, sl=sl, b=b: g.dma_start(
            out=yt[b][:], in_=ra(ym)[:, :, sl]), deps=[last.get(("g", b))], group="ey" + str(b))
        tg = S.op("scalar", lambda s, b=b: s.activation(out=gt[b][:], in_=yt[b][:], func=AF.Silu),
                  deps=[ty, last.get(("o1", b))])
        o0 = S.op("vector", lambda v, b=b: v.tensor_tensor(out=gt[b][:], in0=st_[b][:], in1=gt[b][:],
                                                           op=AluOpType.mult),
                  deps=[tg, ts_])
        o1 = S.op("vector", lambda v, b=b: v.tensor_tensor(out=gt[b][:], in0=xt[b][:], in1=gt[b][:],
                                                           op=AluOpType.add),
                  deps=[o0, tx, last.get(("st", b))])
        so = S.dma("gpsimd", lambda g, sl=sl, b=b: g.dma_start(
            out=ra(x1)[:, :, sl], in_=gt[b][:]), deps=[o1], group="eo" + str(b))
        last[("g", b)] = tg
        last[("o0", b)] = o0
        last[("o1", b)] = o1
        last[("st", b)] = so
        outs.append(so)
    if standalone:
        S.finalize(outs[-NB:])
        return p.done()
    return outs[-NB:]



def build_l1gate(p=None, S=None, io=None):
    WC = 24
    standalone = p is None
    if standalone:
        p = Prog("l1gate")
        S = Sched(p)
    xn = _phase_io(p, io, "xn", [D, NTOK], BF16)
    win = _phase_io(p, io, "win", [D, WIDTH])
    binr = _phase_io(p, io, "binr", [1, WIDTH])
    taps = _phase_io(p, io, "taps", [128, WC, 3])
    sbr = _phase_io(p, io, "sbr", [128, WC])
    vT = _phase_io(p, io, "vT", [D, NTOK], out=True)
    x0T = _phase_io(p, io, "x0T", [D, NTOK], out=True)

    win_b = p.sb("win_b", [128, DC, WIDTH], BF16)
    bin_s = p.sb("bin_s", [1, WIDTH])
    taps_s = p.sb("taps_s", [128, WC, 3])
    sb_s = p.sb("sb_s", [128, WC])
    onesT = p.sb("onesT", [1, TK])
    xnt = [p.sb(f"xnt{i}", [128, DC, TK], BF16) for i in range(2)]
    u = [p.sb(f"u{i}", [128, T]) for i in range(3)]
    uc = [p.sb(f"uc{i}", [128, T]) for i in range(3)]
    psA = p.ps("psA", [128, TK])
    psB = p.ps("psB", [128, TK])

    S.dma("gpsimd", lambda g: g.dma_start(out=win_b[:], in_=win.rearrange("(c p) o -> p c o", p=128)), group="setup")
    S.dma("sync", lambda s: s.dma_start(out=bin_s[:], in_=binr[:]), group="setup")
    S.dma("sync", lambda s: s.dma_start(out=taps_s[:], in_=taps[:]), group="setup")
    t_set = S.dma("sync", lambda s: s.dma_start(out=sb_s[:], in_=sbr[:]), group="setup")
    t_setg = S.ops["gpsimd"][-1]
    t_on = S.op("vector", lambda v: v.memset(onesT[:], 1.0))

    NTB = T // TK
    last_ev = {0: None, 1: None}
    last_u_use = [None, None, None]
    last_uc_use = [None, None, None]
    last_xn_use = [None, None]
    outs = []
    for b in range(BPC):
        for j in range(DC):
            cols = [j, 8 + j, 16 + j]
            uev = {}
            for tc in range(NTB):
                xb = tc % 2
                sl = slice(b * T + tc * TK, b * T + (tc + 1) * TK)
                t_x = S.dma("sync", lambda s, sl=sl, xb=xb: s.dma_start(
                    out=xnt[xb][:], in_=xn.rearrange("(c p) t -> p c t", p=128)[:, :, sl]),
                    deps=[last_xn_use[xb]], group="xn" + str(xb))
                for ci, wc in enumerate(cols):
                    pj = (tc * 3 + ci) % 2
                    ps_ = psA if pj == 0 else psB
                    war = last_ev[pj]
                    mmt = None
                    for dc in range(DC):
                        deps = [t_x, t_setg]
                        if dc == 0:
                            deps.append(war)
                        mmt = S.op("tensor", lambda t, dc=dc, wc=wc, xb=xb, ps_=ps_: t.matmul(
                            ps_[:], win_b[:, dc, wc * 128:(wc + 1) * 128], xnt[xb][:, dc, :],
                            start=(dc == 0), stop=False), deps=deps)
                    mmt = S.op("tensor", lambda t, wc=wc, ps_=ps_: t.matmul(
                        ps_[:], bin_s[:, wc * 128:(wc + 1) * 128], onesT[:],
                        start=False, stop=True), deps=[t_set, t_on])
                    ev = S.op("scalar", lambda s, ci=ci, tc=tc, ps_=ps_: s.copy(
                        out=u[ci][:, tc * TK:(tc + 1) * TK], in_=ps_[:]),
                        deps=[mmt, last_u_use[ci]])
                    uev[(ci, tc)] = ev
                    last_ev[pj] = ev
                last_xn_use[xb] = mmt
            conv_done = []
            for ci, wc in enumerate(cols):
                dep_all = [uev[(ci, tc)] for tc in range(NTB)]
                c1 = S.op("vector", lambda v, ci=ci, wc=wc: v.tensor_scalar(
                    out=uc[ci][:], in0=u[ci][:], scalar1=taps_s[:, wc, 2:3], scalar2=sb_s[:, wc:wc + 1],
                    op0=AluOpType.mult, op1=AluOpType.add),
                    deps=dep_all + [t_set, last_uc_use[ci]])
                c2 = S.op("vector", lambda v, ci=ci, wc=wc: v.scalar_tensor_tensor(
                    out=uc[ci][:, 1:T], in0=u[ci][:, 0:T - 1], scalar=taps_s[:, wc, 1:2],
                    in1=uc[ci][:, 1:T], op0=AluOpType.mult, op1=AluOpType.add), deps=[c1])
                c3 = S.op("vector", lambda v, ci=ci, wc=wc: v.scalar_tensor_tensor(
                    out=uc[ci][:, 2:T], in0=u[ci][:, 0:T - 2], scalar=taps_s[:, wc, 0:1],
                    in1=uc[ci][:, 2:T], op0=AluOpType.mult, op1=AluOpType.add), deps=[c2])
                conv_done.append(c3)
                last_u_use[ci] = c3
            vv = S.op("vector", lambda v: v.tensor_tensor(
                out=uc[2][:], in0=uc[2][:], in1=uc[1][:], op=AluOpType.mult),
                deps=[conv_done[1], conv_done[2]])
            sv = S.dma("gpsimd", lambda g, b=b, j=j: g.dma_start(
                out=vT.rearrange("(c p) t -> c p t", p=128)[j, :, b * T:(b + 1) * T],
                in_=uc[2][:]), deps=[vv], group="vst")
            sx = S.dma("gpsimd", lambda g, b=b, j=j: g.dma_start(
                out=x0T.rearrange("(c p) t -> c p t", p=128)[j, :, b * T:(b + 1) * T],
                in_=uc[0][:]), deps=[conv_done[0]], group="xst")
            last_uc_use[0] = sx
            last_uc_use[1] = vv
            last_uc_use[2] = sv
            outs += [sv, sx]
    if standalone:
        S.finalize(outs[-2:])
        return p.done()
    return outs[-2:]



def build_filter(p=None, S=None, io=None):
    standalone = p is None
    if standalone:
        p = Prog("filt")
        S = Sched(p)
    peT = _phase_io(p, io, "peT", [3, T])
    w1f = _phase_io(p, io, "w1f", [3, 64])
    b1r = _phase_io(p, io, "b1r", [1, 64])
    f1c = _phase_io(p, io, "f1c", [64, 1])
    w2f = _phase_io(p, io, "w2f", [64, 64])
    b2r = _phase_io(p, io, "b2r", [1, 64])
    f2c = _phase_io(p, io, "f2c", [64, 1])
    w3f = _phase_io(p, io, "w3f", [64, D])
    b3r = _phase_io(p, io, "b3r", [1, D])
    thr = _phase_io(p, io, "thr", [1, T])
    dlt = _phase_io(p, io, "dlt", [128, DC])
    kT = _phase_io(p, io, "kT", [D, T], out=True)

    pe_s = p.sb("pe_s", [3, T])
    w1_s = p.sb("w1_s", [3, 64])
    b1_s = p.sb("b1_s", [1, 64])
    f1_s = p.sb("f1_s", [64, 1])
    w2_s = p.sb("w2_s", [64, 64])
    b2_s = p.sb("b2_s", [1, 64])
    f2_s = p.sb("f2_s", [64, 1])
    w3_s = p.sb("w3_s", [64, D])
    b3_s = p.sb("b3_s", [1, D])
    th_s = p.sb("th_s", [1, T])
    dl_s = p.sb("dl_s", [128, DC])
    onesT = p.sb("onesT", [1, TK])
    h1 = p.sb("h1", [64, T])
    h2 = p.sb("h2", [64, T])
    tint = p.sb("tint", [64, T], I32)
    trnd = p.sb("trnd", [64, T])
    k0t = p.sb("k0t", [128, TK])
    dect = p.sb("dect", [128, TK])
    ktile = p.sb("ktile", [128, TK])
    psH = p.ps("psH", [64, TK])
    psK = p.ps("psK", [128, TK])
    psD = p.ps("psD", [128, TK])

    onesr = p.sb("onesr", [1, 128])

    for src, dst in [(peT, pe_s), (w1f, w1_s), (b1r, b1_s), (f1c, f1_s),
                     (w2f, w2_s), (b2r, b2_s), (f2c, f2_s), (w3f, w3_s),
                     (b3r, b3_s), (thr, th_s), (dlt, dl_s)]:
        t_set = S.dma("sync", lambda s, src=src, dst=dst: s.dma_start(out=dst[:], in_=src[:]),
                      group="setup")
    t_on = S.op("vector", lambda v: v.memset(onesT[:], 1.0))
    t_on2 = S.op("vector", lambda v: v.memset(onesr[:], 1.0))
    TWO_PI = float(2.0 * np.pi)

    def mlp_stage(wt, bt, ft, rhs_buf, out_buf, last_tok):
        toks = []
        for tc in range(T // TK):
            sl = slice(tc * TK, (tc + 1) * TK)
            m1 = S.op("tensor", lambda t, sl=sl: t.matmul(
                psH[:], wt[:], rhs_buf[:, sl], start=True, stop=False),
                deps=[t_set, last_tok, toks[-1] if toks else None])
            m2 = S.op("tensor", lambda t: t.matmul(
                psH[:], bt[:], onesT[:], start=False, stop=True), deps=[t_on])
            a1 = S.op("vector", lambda v, sl=sl: v.tensor_scalar(
                out=out_buf[:, sl], in0=psH[:], scalar1=ft[:], scalar2=None,
                op0=AluOpType.mult), deps=[m2])
            a2 = S.op("vector", lambda v, sl=sl: v.tensor_scalar(
                out=tint[:, sl], in0=out_buf[:, sl], scalar1=float(1.0 / TWO_PI), scalar2=None,
                op0=AluOpType.mult), deps=[a1])
            a3 = S.op("vector", lambda v, sl=sl: v.tensor_copy(
                out=trnd[:, sl], in_=tint[:, sl]), deps=[a2])
            a4 = S.op("vector", lambda v, sl=sl: v.scalar_tensor_tensor(
                out=out_buf[:, sl], in0=trnd[:, sl], scalar=-TWO_PI, in1=out_buf[:, sl],
                op0=AluOpType.mult, op1=AluOpType.add), deps=[a3, a1])
            a5 = S.op("scalar", lambda s, sl=sl: s.activation(
                out=out_buf[:, sl], in_=out_buf[:, sl], func=AF.Sin), deps=[a4])
            toks.append(a5)
        return toks

    h1_toks = mlp_stage(w1_s, b1_s, f1_s, pe_s, h1, None)
    h2_toks = mlp_stage(w2_s, b2_s, f2_s, h1, h2, h1_toks[-1])

    outs = []
    last_k = None
    last_d = None
    last_kt = None
    for dc in range(DC):
        for tc in range(T // TK):
            sl = slice(tc * TK, (tc + 1) * TK)
            mk1 = S.op("tensor", lambda t, dc=dc, sl=sl: t.matmul(
                psK[:], w3_s[:, dc * 128:(dc + 1) * 128], h2[:, sl], start=True, stop=False),
                deps=[h2_toks[-1], t_set, last_k])
            mk2 = S.op("tensor", lambda t, dc=dc: t.matmul(
                psK[:], b3_s[:, dc * 128:(dc + 1) * 128], onesT[:], start=False, stop=True),
                deps=[t_on])
            md = S.op("tensor", lambda t, sl=sl: t.matmul(
                psD[:], onesr[:], th_s[:, sl], start=True, stop=True),
                deps=[t_on2, t_set, last_d])
            de = S.op("scalar", lambda s, dc=dc: s.activation(
                out=dect[:], in_=psD[:], func=AF.Exp, scale=dl_s[:, dc:dc + 1]),
                deps=[md, last_kt])
            kt = S.op("vector", lambda v: v.tensor_tensor(
                out=ktile[:], in0=psK[:], in1=dect[:], op=AluOpType.mult),
                deps=[mk2, de, last_kt])
            so = S.dma("gpsimd", lambda g, dc=dc, sl=sl: g.dma_start(
                out=kT.rearrange("(c p) t -> c p t", p=128)[dc, :, sl], in_=ktile[:]),
                deps=[kt], group="kst")
            last_k = kt
            last_d = de
            last_kt = so
            outs.append(so)
    if standalone:
        S.finalize([outs[-1]])
        return p.done()
    return [outs[-1]]



def build_conv(p=None, S=None, io=None):
    standalone = p is None
    if standalone:
        p = Prog("conv")
        S = Sched(p)
    nc = p.nc
    vT = _phase_io(p, io, "vT", [D, NTOK])
    kT = _phase_io(p, io, "kT", [D, T])
    G1c = _phase_io(p, io, "G1c", [64, 128])
    G1s = _phase_io(p, io, "G1s", [64, 128])
    G1Kc = _phase_io(p, io, "G1Kc", [64, 128])
    G1Ks = _phase_io(p, io, "G1Ks", [64, 128])
    Lm = _phase_io(p, io, "Lm", [128, 128])
    LKm = _phase_io(p, io, "LKm", [128, 128])
    Linv = _phase_io(p, io, "Linv", [128, 128])
    Twc8 = _phase_io(p, io, "Twc8", [128, 512])
    Tws8 = _phase_io(p, io, "Tws8", [128, 512])
    TwKc8 = _phase_io(p, io, "TwKc8", [128, 512])
    TwKs8 = _phase_io(p, io, "TwKs8", [128, 512])
    TwIc4 = _phase_io(p, io, "TwIc4", [64, 512])
    TwIs4 = _phase_io(p, io, "TwIs4", [64, 512])
    GBc = _phase_io(p, io, "GBc", [128, 64])
    GBs = _phase_io(p, io, "GBs", [128, 64])
    ident = _phase_io(p, io, "ident", [128, 128])
    ycT = _phase_io(p, io, "ycT", [D, NTOK], out=True)
    Md = nc.dram_tensor("Mdr", [D, 128, 128], F32, kind="Internal").ap()
    Msw = nc.dram_tensor("Mswr", [D, 128, 128], F32, kind="Internal").ap()

    cst = {}
    for nm, src, shape in [("G1c", G1c, [64, 128]), ("G1s", G1s, [64, 128]),
                           ("G1Kc", G1Kc, [64, 128]), ("G1Ks", G1Ks, [64, 128]),
                           ("Lm", Lm, [128, 128]), ("LKm", LKm, [128, 128]),
                           ("Linv", Linv, [128, 128]),
                           ("Twc8", Twc8, [128, 512]), ("Tws8", Tws8, [128, 512]),
                           ("TwKc8", TwKc8, [128, 512]), ("TwKs8", TwKs8, [128, 512]),
                           ("TwIc4", TwIc4, [64, 512]), ("TwIs4", TwIs4, [64, 512]),
                           ("GBc", GBc, [128, 64]), ("GBs", GBs, [128, 64]),
                           ("ident", ident, [128, 128])]:
        cst[nm] = (p.sb("c_" + nm, shape), src)

    V1c = p.sb("V1c", [64, 512])
    V1n = p.sb("V1n", [64, 512])
    Bp = p.sb("Bp", [128, 1024])
    Rsb = p.sb("Rsb", [128, 512])
    Msb = p.sb("Msb", [128, 512])
    Mdup = p.sb("Mdup", [128, 512])
    Mswp = p.sb("Mswp", [128, 512])
    Pp = p.sb("Pp", [128, 512])
    E1 = p.sb("E1", [128, 512])
    E2 = p.sb("E2", [128, 512])
    Dp = p.sb("Dp", [128, 512])
    D2 = p.sb("D2", [128, 512])
    osb = p.sb("osb", [64, 512])
    ta = p.sb("ta", [64, 512])
    tb = p.sb("tb", [64, 512])
    tc2 = p.sb("tc2", [64, 512])
    td2 = p.sb("td2", [64, 512])

    psF0 = p.ps("psF0", [128, 512])
    psF1 = p.ps("psF1", [128, 512])
    psT = p.ps("psT", [128, 512])
    psX = p.ps("psX", [128, 512])
    psC = p.ps("psC", [128, 512])
    psT2 = p.ps("psT2", [128, 512])
    psO = p.ps("psO", [64, 512])

    t_set = None
    for nm, (dst, src) in cst.items():
        t_set = S.dma("sync", lambda s, dst=dst, src=src: s.dma_start(out=dst[:], in_=src[:]),
                      group="setup")
    C = lambda nm: cst[nm][0]

    war = {}

    def fwd_block(sig, cb8, kpath, b):
        c0 = cb8 * 8
        src = kT if kpath else vT
        toff = 0 if kpath else b * T
        t_v1 = S.dma("gpsimd", lambda g, c0=c0, b=b, src=src, kpath=kpath: g.dma_start(
            out=V1c[:], in_=(src.rearrange("c (n1 n2) -> n1 c n2", n2=64)[:, c0:c0 + 8, :]
                             if kpath else
                             src.rearrange("c (bb n1 n2) -> bb n1 c n2", bb=BPC, n2=64)[b, :, c0:c0 + 8, :])),
            deps=[war.get("V1c")], group="v1")
        ro = S.op("vector", lambda v: v.tensor_copy(
            out=V1n.rearrange("p (n c) -> p n c", c=8),
            in_=V1c.rearrange("p (c n) -> p n c", n=64)), deps=[t_v1, war.get("V1n")])
        war["V1c"] = ro
        g1c, g1s = ("G1Kc", "G1Ks") if kpath else ("G1c", "G1s")
        m1 = S.op("tensor", lambda t: t.matmul(psF0[:], C(g1c)[:], V1n[:], start=True, stop=True),
                  deps=[ro, t_set, war.get("psF0")])
        m2 = S.op("tensor", lambda t: t.matmul(psF1[:], C(g1s)[:], V1n[:], start=True, stop=True),
                  deps=[ro, t_set, war.get("psF1")])
        war["V1n"] = m2
        twc, tws = ("TwKc8", "TwKs8") if kpath else ("Twc8", "Tws8")
        bdst = lambda ri: Bp.rearrange("p (c x n) -> p n c x", x=2, n=64)[:, :, :, ri]
        w1_ = S.op("vector", lambda v: v.tensor_tensor(out=E1[:], in0=psF0[:], in1=C(twc)[:],
                                                       op=AluOpType.mult), deps=[m1, war.get("E1"), t_set])
        w2_ = S.op("vector", lambda v: v.tensor_tensor(out=E2[:], in0=psF1[:], in1=C(tws)[:],
                                                       op=AluOpType.mult), deps=[m2, war.get("E2"), t_set])
        w3_ = S.op("vector", lambda v: v.tensor_tensor(out=bdst(0), in0=E1[:], in1=E2[:],
                                                       op=AluOpType.subtract),
                   deps=[w1_, w2_, war.get("Bp")])
        w4_ = S.op("vector", lambda v: v.tensor_tensor(out=E1[:], in0=psF0[:], in1=C(tws)[:],
                                                       op=AluOpType.mult), deps=[w3_, m1])
        w5_ = S.op("vector", lambda v: v.tensor_tensor(out=E2[:], in0=psF1[:], in1=C(twc)[:],
                                                       op=AluOpType.mult), deps=[w3_, m2])
        w6_ = S.op("vector", lambda v: v.tensor_tensor(out=bdst(1), in0=E1[:], in1=E2[:],
                                                       op=AluOpType.add), deps=[w4_, w5_])
        war["psF0"] = w4_
        war["psF1"] = w5_
        war["E1"] = w6_
        war["E2"] = w6_
        tw_done = w6_
        out_toks = []
        for hf in range(2):
            trs = []
            for cc in range(4):
                ci = hf * 4 + cc
                trs.append(S.op("tensor", lambda t, ci=ci, cc=cc: t.transpose(
                    psT[:, cc * 128:(cc + 1) * 128], Bp[:, ci * 128:(ci + 1) * 128], C("ident")[:]),
                    deps=[tw_done, t_set, war.get("psT")]))
            evT = S.op("scalar", lambda s: s.copy(out=Rsb[:], in_=psT[:]),
                       deps=[trs[-1], war.get("Rsb")])
            war["psT"] = evT
            lmat = "LKm" if kpath else "Lm"
            mX = S.op("tensor", lambda t: t.matmul(psX[:], C(lmat)[:], Rsb[:], start=True, stop=True),
                      deps=[evT, t_set, war.get("psX")])
            war["Rsb"] = mX
            if kpath:
                evX = S.op("scalar", lambda s: s.copy(out=Msb[:], in_=psX[:]),
                           deps=[mX, war.get("Msb")])
                war["psX"] = evX
                s1 = S.dma("gpsimd", lambda g, c0=c0, hf=hf: g.dma_start(
                    out=Md.rearrange("c r k -> r c k")[:, c0 + hf * 4:c0 + hf * 4 + 4, :],
                    in_=Msb[:]), deps=[evX], group="mst")
                s2a = S.dma("gpsimd", lambda g, c0=c0, hf=hf: g.dma_start(
                    out=Msw.rearrange("c r k -> r c k")[0:64, c0 + hf * 4:c0 + hf * 4 + 4, :],
                    in_=Msb[64:128, :]), deps=[evX], group="mst")
                s2b = S.dma("gpsimd", lambda g, c0=c0, hf=hf: g.dma_start(
                    out=Msw.rearrange("c r k -> r c k")[64:128, c0 + hf * 4:c0 + hf * 4 + 4, :],
                    in_=Msb[0:64, :]), deps=[evX], group="mst")
                war["Msb"] = s2b
                out_toks.append(s2b)
                continue
            t_md = S.dma("sync", lambda s, c0=c0, hf=hf: s.dma_start(
                out=Mdup[:], in_=Md.rearrange("c r k -> r c k")[:, c0 + hf * 4:c0 + hf * 4 + 4, :]),
                deps=[war.get("Mdup"), mstore_done[0]], group="mld")
            t_ms = S.dma("sync", lambda s, c0=c0, hf=hf: s.dma_start(
                out=Mswp[:], in_=Msw.rearrange("c r k -> r c k")[:, c0 + hf * 4:c0 + hf * 4 + 4, :]),
                deps=[war.get("Mswp"), mstore_done[0]], group="mld")
            pa = S.op("vector", lambda v: v.tensor_tensor(out=ta[:], in0=psX[0:64, :], in1=Mdup[0:64, :],
                                                          op=AluOpType.mult),
                      deps=[mX, t_md, war.get("ta")])
            pb = S.op("vector", lambda v: v.tensor_tensor(out=tb[:], in0=psX[64:128, :], in1=Mdup[64:128, :],
                                                          op=AluOpType.mult),
                      deps=[mX, t_md, war.get("tb")])
            pc = S.op("vector", lambda v: v.tensor_tensor(out=tc2[:], in0=psX[0:64, :], in1=Mswp[0:64, :],
                                                          op=AluOpType.mult),
                      deps=[mX, t_ms, war.get("tc2")])
            pd = S.op("vector", lambda v: v.tensor_tensor(out=td2[:], in0=psX[64:128, :], in1=Mswp[64:128, :],
                                                          op=AluOpType.mult),
                      deps=[mX, t_ms, war.get("td2")])
            war["psX"] = pd
            war["Mdup"] = pb
            war["Mswp"] = pd
            p3 = S.op("vector", lambda v: v.tensor_tensor(out=Pp[0:64, :], in0=ta[:],
                                                          in1=tb[:], op=AluOpType.subtract),
                      deps=[pa, pb, war.get("Pp")])
            p4 = S.op("vector", lambda v: v.tensor_tensor(out=Pp[64:128, :], in0=tc2[:],
                                                          in1=td2[:], op=AluOpType.add),
                      deps=[pc, pd, p3])
            war["ta"] = p3
            war["tb"] = p3
            war["tc2"] = p4
            war["td2"] = p4
            mC = S.op("tensor", lambda t: t.matmul(psC[:], C("Linv")[:], Pp[:], start=True, stop=True),
                      deps=[p4, t_set, war.get("psC")])
            war["Pp"] = mC
            qa = S.op("vector", lambda v: v.tensor_tensor(out=ta[:], in0=psC[0:64, :], in1=C("TwIc4")[:],
                                                          op=AluOpType.mult),
                      deps=[mC, war.get("ta")])
            qb = S.op("vector", lambda v: v.tensor_tensor(out=tb[:], in0=psC[64:128, :], in1=C("TwIs4")[:],
                                                          op=AluOpType.mult),
                      deps=[mC, war.get("tb")])
            qc = S.op("vector", lambda v: v.tensor_tensor(out=tc2[:], in0=psC[0:64, :], in1=C("TwIs4")[:],
                                                          op=AluOpType.mult),
                      deps=[mC, war.get("tc2")])
            qd = S.op("vector", lambda v: v.tensor_tensor(out=td2[:], in0=psC[64:128, :], in1=C("TwIc4")[:],
                                                          op=AluOpType.mult),
                      deps=[mC, war.get("td2")])
            war["psC"] = qd
            q3 = S.op("vector", lambda v: v.tensor_tensor(out=Dp[0:64, :], in0=ta[:],
                                                          in1=tb[:], op=AluOpType.subtract),
                      deps=[qa, qb, war.get("Dp")])
            q4 = S.op("vector", lambda v: v.tensor_tensor(out=Dp[64:128, :], in0=tc2[:],
                                                          in1=td2[:], op=AluOpType.add),
                      deps=[qc, qd, q3])
            war["ta"] = q3
            war["tb"] = q3
            war["tc2"] = q4
            war["td2"] = q4
            trs2 = []
            for cc in range(4):
                trs2.append(S.op("tensor", lambda t, cc=cc: t.transpose(
                    psT2[:, cc * 128:(cc + 1) * 128], Dp[:, cc * 128:(cc + 1) * 128], C("ident")[:]),
                    deps=[q4, t_set, war.get("psT2")]))
            evT2 = S.op("scalar", lambda s: s.copy(out=D2[:], in_=psT2[:]),
                        deps=[trs2[-1], war.get("D2")])
            war["psT2"] = evT2
            war["Dp"] = trs2[-1]
            mB = None
            for cc in range(4):
                ci = hf * 4 + cc
                mB = S.op("tensor", lambda t, cc=cc, ci=ci: t.matmul(
                    psO[:, ci * 64:(ci + 1) * 64], C("GBc")[:], D2[:, cc * 128:cc * 128 + 64],
                    start=True, stop=False), deps=[evT2, t_set, war.get("psO")])
                mB = S.op("tensor", lambda t, cc=cc, ci=ci: t.matmul(
                    psO[:, ci * 64:(ci + 1) * 64], C("GBs")[:], D2[:, cc * 128 + 64:(cc + 1) * 128],
                    start=False, stop=True), deps=[])
            war["D2"] = mB
            out_toks.append(mB)
        return out_toks

    mstore_done = [None]
    for cb8 in range(D // 8):
        toks = fwd_block("k", cb8, True, 0)
    mstore_done[0] = toks[-1]

    st_last = None
    for b in range(BPC):
        for cb8 in range(D // 8):
            toks = fwd_block("v", cb8, False, b)
            evO = S.op("scalar", lambda s: s.copy(
                out=osb.rearrange("p (c n) -> p c n", n=64),
                in_=psO.rearrange("p (c n) -> p c n", n=64)), deps=[toks[-1], war.get("osb")])
            war["psO"] = evO
            st_ = S.dma("gpsimd", lambda g, cb8=cb8, b=b: g.dma_start(
                out=ycT.rearrange("c (bb n1 n2) -> bb n1 c n2", bb=BPC, n2=64)[b, :, cb8 * 8:cb8 * 8 + 8, :],
                in_=osb.rearrange("p (c n) -> p c n", n=64)), deps=[evO], group="yst")
            war["osb"] = st_
            st_last = st_

    if standalone:
        S.finalize([st_last])
        return p.done()
    return [st_last]



def build_l1out(p=None, S=None, io=None):
    standalone = p is None
    if standalone:
        p = Prog("l1out")
        S = Sched(p)
    yc = _phase_io(p, io, "yc", [D, NTOK])
    x0 = _phase_io(p, io, "x0", [D, NTOK])
    fb = _phase_io(p, io, "fb", [128, DC])
    wo = _phase_io(p, io, "wo", [D, D])
    bor = _phase_io(p, io, "bor", [1, D])
    s1 = _phase_io(p, io, "s1", [D, NTOK], out=True)

    wo_b = p.sb("wo_b", [128, DC, D], BF16)
    fb_s = p.sb("fb_s", [128, DC])
    bo_s = p.sb("bo_s", [1, D])
    onesT = p.sb("onesT", [1, TK])
    yct = [p.sb(f"yct{i}", [128, DC, TK]) for i in range(2)]
    x0t = [p.sb(f"x0t{i}", [128, DC, TK]) for i in range(2)]
    ygb = p.sb("ygb", [128, DC, TK], BF16)
    s1t = [p.sb(f"s1t{i}", [128, TK]) for i in range(2)]
    psA = p.ps("psA", [128, TK])[:]
    psB = p.ps("psB", [128, TK])[:]

    S.dma("gpsimd", lambda g: g.dma_start(out=wo_b[:], in_=wo.rearrange("(c p) o -> p c o", p=128)), group="setup")
    S.dma("sync", lambda s: s.dma_start(out=fb_s[:], in_=fb[:]), group="setup")
    t_set = S.dma("sync", lambda s: s.dma_start(out=bo_s[:], in_=bor[:]), group="setup")
    t_setg = S.ops["gpsimd"][-1]
    t_on = S.op("vector", lambda v: v.memset(onesT[:], 1.0))

    ra = lambda a: a.rearrange("(c p) t -> p c t", p=128)
    last_g = {}
    outs = []
    for tk in range(NTK):
        b = tk % 2
        sl = slice(tk * TK, (tk + 1) * TK)
        ty = S.dma("gpsimd", lambda g, sl=sl, b=b: g.dma_start(
            out=yct[b][:], in_=ra(yc)[:, :, sl]), deps=[last_g.get(("yg", b))], group="y" + str(b))
        tx = S.dma("sync", lambda s, sl=sl, b=b: s.dma_start(
            out=x0t[b][:], in_=ra(x0)[:, :, sl]), deps=[last_g.get(("yg", b))], group="x" + str(b))
        gops = []
        for dc in range(DC):
            gops.append(S.op("vector", lambda v, dc=dc, b=b: v.scalar_tensor_tensor(
                out=ygb[:, dc, :], in0=yct[b][:, dc, :], scalar=fb_s[:, dc:dc + 1],
                in1=x0t[b][:, dc, :], op0=AluOpType.add, op1=AluOpType.mult),
                deps=[ty, tx, t_set, last_g.get("mm")]))
        last_g[("yg", b)] = gops[-1]
        mm_last = None
        for oc in range(DC):
            ps_ = psA if oc % 2 == 0 else psB
            for dc in range(DC):
                deps = [gops[-1], t_setg]
                if dc == 0:
                    deps.append(last_g.get(("ev", oc % 2)))
                mm_last = S.op("tensor", lambda t, oc=oc, dc=dc, ps_=ps_: t.matmul(
                    ps_, wo_b[:, dc, oc * 128:(oc + 1) * 128], ygb[:, dc, :],
                    start=(dc == 0), stop=False), deps=deps)
            mm_last = S.op("tensor", lambda t, oc=oc, ps_=ps_: t.matmul(
                ps_, bo_s[:, oc * 128:(oc + 1) * 128], onesT[:], start=False, stop=True),
                deps=[t_set, t_on])
            ev = S.op("scalar", lambda s, oc=oc, ps_=ps_: s.copy(out=s1t[oc % 2][:], in_=ps_),
                      deps=[mm_last, last_g.get(("st", oc % 2))])
            st_ = S.dma("sync", lambda s, oc=oc, sl=sl: s.dma_start(
                out=s1.rearrange("(c p) t -> c p t", p=128)[oc, :, sl], in_=s1t[oc % 2][:]),
                deps=[ev], group="s" + str(oc % 2))
            last_g[("ev", oc % 2)] = ev
            last_g[("st", oc % 2)] = st_
            outs.append(st_)
        last_g["mm"] = mm_last
    if standalone:
        S.finalize(outs[-2:])
        return p.done()
    return outs[-2:]


def l1out_inmaps(yc_shards, x0_shards, inputs):
    ac = np.ascontiguousarray
    f32 = np.float32
    fb = ac(np.asarray(inputs["L1_attn_filter_bias"], dtype=f32).reshape(8, 128).T)
    wo = ac(inputs["L1_attn_W_out"], dtype=f32)
    bor = ac(np.asarray(inputs["L1_attn_b_out"], dtype=f32).reshape(1, -1))
    return [{"yc": np.ascontiguousarray(y), "x0": np.ascontiguousarray(x),
             "fb": fb, "wo": wo, "bor": bor} for y, x in zip(yc_shards, x0_shards)]


def run_l1out(yc_shards, x0_shards, inputs):
    nc = _get("l1out", build_l1out)
    res = run_bass_kernel_spmd(nc, l1out_inmaps(yc_shards, x0_shards, inputs),
                               core_ids=list(range(NCORES)))
    return [r["s1"] for r in res.results]



def _shard_x(xfull):
    shards = []
    for c in range(NCORES):
        xc = xfull[c * BPC:(c + 1) * BPC].reshape(BPC * T, D)
        shards.append(np.ascontiguousarray(xc.T.astype(np.float32)))
    return shards


_PROG_CACHE = {}


def _get(name, fn):
    if name not in _PROG_CACHE:
        _PROG_CACHE[name] = fn()
    return _PROG_CACHE[name]


def _kv_perm():
    k_cols = (np.arange(16)[:, None] * 128 + np.arange(64)[None, :]).ravel()
    v_cols = (np.arange(16)[:, None] * 128 + 64 + np.arange(64)[None, :]).ravel()
    return np.concatenate([k_cols, v_cols])


def l0att_inmaps(xn_shards, inputs):
    f32 = np.float32
    ac = np.ascontiguousarray
    wq = ac(inputs["L0_attn_Wq"], dtype=f32)
    wd = ac(inputs["L0_attn_Wdown"], dtype=f32)
    wu0 = np.asarray(inputs["L0_attn_Wup"], dtype=f32)
    wo = ac(inputs["L0_attn_Wout"], dtype=f32)
    perm = _kv_perm()
    wu = ac(wu0[:, perm])
    bu = ac(np.asarray(inputs["L0_attn_bup"], dtype=f32)[perm])
    bqr = ac(np.asarray(inputs["L0_attn_bq"], dtype=f32).reshape(1, -1))
    bdr = ac(np.asarray(inputs["L0_attn_bdown"], dtype=f32).reshape(1, -1))
    bur = ac(bu.reshape(1, -1))
    bor = ac(np.asarray(inputs["L0_attn_bout"], dtype=f32).reshape(1, -1))
    ropec, ropes = _rope_tables()
    base = {"wq": wq, "wd": wd, "wu": wu, "wo": wo, "bqr": bqr, "bdr": bdr,
            "bur": bur, "bor": bor, "ropec": ropec, "ropes": ropes}
    return [dict(base, xn=np.ascontiguousarray(xs)) for xs in xn_shards]


def run_l0att(xn_shards, inputs):
    nc = _get("l0att", build_l0att)
    res = run_bass_kernel_spmd(nc, l0att_inmaps(xn_shards, inputs), core_ids=list(range(NCORES)))
    return [r["s0"] for r in res.results]


def router_inmaps(kind, xn_shards, inputs):
    ac = np.ascontiguousarray
    f32 = np.float32
    if kind == "l0":
        rw = ac(np.asarray(inputs["L0_ffn_router_w"], dtype=f32).T)
        brr = np.zeros((1, E), f32)
        sc8 = ac(np.asarray(inputs["L0_ffn_expert_scale"], dtype=f32).reshape(1, E))
    else:
        rw = ac(inputs["L1_ffn_Wr"], dtype=f32)
        brr = ac(np.asarray(inputs["L1_ffn_br"], dtype=f32).reshape(1, E))
        sc8 = np.ones((1, E), f32)
    ident = np.eye(128, dtype=f32)
    base = {"rw": rw, "brr": brr, "sc8": sc8, "ident": ident}
    return [dict(base, xn=np.ascontiguousarray(xs)) for xs in xn_shards]


def run_router(kind, xn_shards, inputs):
    nc = _get("rt_" + kind, lambda: build_router(kind))
    res = run_bass_kernel_spmd(nc, router_inmaps(kind, xn_shards, inputs), core_ids=list(range(NCORES)))
    return [r["comb"] for r in res.results]


def moe_inmaps(act, xn_shards, comb_shards, inputs):
    ac = np.ascontiguousarray
    f32 = np.float32
    if act == "silu":
        w1 = ac(np.asarray(inputs["L0_ffn_w1"], dtype=f32).transpose(0, 2, 1))
        w2 = ac(np.asarray(inputs["L0_ffn_w2"], dtype=f32).transpose(0, 2, 1))
    else:
        w1 = ac(inputs["L1_ffn_W1"], dtype=f32)
        w2 = ac(inputs["L1_ffn_W2"], dtype=f32)
    return [{"xn": np.ascontiguousarray(xs), "w1": w1, "w2": w2,
             "comb": np.ascontiguousarray(cb)} for xs, cb in zip(xn_shards, comb_shards)]


def run_moe(act, xn_shards, comb_shards, inputs):
    nc = _get("moe_" + act, lambda: build_moe(act))
    res = run_bass_kernel_spmd(nc, moe_inmaps(act, xn_shards, comb_shards, inputs),
                               core_ids=list(range(NCORES)))
    return [r["ymoe"] for r in res.results]


def run_ep(x_shards, s_shards, y_shards):
    nc = _get("ep", build_ep)
    im = [{"x": np.ascontiguousarray(a), "sT": np.ascontiguousarray(b),
           "ym": np.ascontiguousarray(c)} for a, b, c in zip(x_shards, s_shards, y_shards)]
    res = run_bass_kernel_spmd(nc, im, core_ids=list(range(NCORES)))
    return [r["x1"] for r in res.results]


def l1gate_inmaps(xn_shards, inputs):
    ac = np.ascontiguousarray
    f32 = np.float32
    win = ac(inputs["L1_attn_W_in"], dtype=f32)
    binr = ac(np.asarray(inputs["L1_attn_b_in"], dtype=f32).reshape(1, -1))
    sw = np.asarray(inputs["L1_attn_short_w"], dtype=f32)
    taps = ac(sw.reshape(24, 128, 3).transpose(1, 0, 2))
    sbr = ac(np.asarray(inputs["L1_attn_short_b"], dtype=f32).reshape(24, 128).T)
    base = {"win": win, "binr": binr, "taps": taps, "sbr": sbr}
    return [dict(base, xn=np.ascontiguousarray(xs)) for xs in xn_shards]


def run_l1gate(xn_shards, inputs):
    nc = _get("l1gate", build_l1gate)
    res = run_bass_kernel_spmd(nc, l1gate_inmaps(xn_shards, inputs), core_ids=list(range(NCORES)))
    return [(r["vT"], r["x0T"]) for r in res.results]


def filter_inmaps(inputs):
    ac = np.ascontiguousarray
    f32 = np.float32
    peT, deltas, th = _filter_consts()
    m = {"peT": ac(peT), "thr": ac(th.reshape(1, -1)),
         "dlt": ac(-(deltas.reshape(8, 128).T)),
         "w1f": ac(inputs["L1_attn_W1"], dtype=f32),
         "b1r": ac(np.asarray(inputs["L1_attn_b1"], dtype=f32).reshape(1, -1)),
         "f1c": ac(np.asarray(inputs["L1_attn_f1"], dtype=f32).reshape(-1, 1)),
         "w2f": ac(inputs["L1_attn_W2"], dtype=f32),
         "b2r": ac(np.asarray(inputs["L1_attn_b2"], dtype=f32).reshape(1, -1)),
         "f2c": ac(np.asarray(inputs["L1_attn_f2"], dtype=f32).reshape(-1, 1)),
         "w3f": ac(inputs["L1_attn_W3"], dtype=f32),
         "b3r": ac(np.asarray(inputs["L1_attn_b3"], dtype=f32).reshape(1, -1))}
    return [dict(m) for _ in range(NCORES)]


def run_filter(inputs):
    nc = _get("filt", build_filter)
    res = run_bass_kernel_spmd(nc, filter_inmaps(inputs), core_ids=list(range(NCORES)))
    return [r["kT"] for r in res.results]


def conv_inmaps(v_shards, k_shards):
    c = _fft_consts()
    ac = np.ascontiguousarray
    base = {"G1c": c["G1c"], "G1s": c["G1s"], "G1Kc": c["G1Kc"], "G1Ks": c["G1Ks"],
            "Lm": c["L"], "LKm": c["LK"], "Linv": c["Linv"],
            "Twc8": ac(np.repeat(c["Twc"], 8, axis=1)),
            "Tws8": ac(np.repeat(c["Tws"], 8, axis=1)),
            "TwKc8": ac(np.repeat(c["TwKc"], 8, axis=1)),
            "TwKs8": ac(np.repeat(c["TwKs"], 8, axis=1)),
            "TwIc4": ac(np.tile(c["TwIc2"][:64], (1, 4))),
            "TwIs4": ac(np.tile(c["TwIs2"][:64], (1, 4))),
            "GBc": c["GBc"], "GBs": c["GBs"], "ident": np.eye(128, dtype=np.float32)}
    return [dict(base, vT=np.ascontiguousarray(v), kT=np.ascontiguousarray(k))
            for v, k in zip(v_shards, k_shards)]


def run_conv(v_shards, k_shards):
    nc = _get("conv", build_conv)
    res = run_bass_kernel_spmd(nc, conv_inmaps(v_shards, k_shards), core_ids=list(range(NCORES)))
    return [r["ycT"] for r in res.results]


def run_ln(x_shards):
    nc = _get("ln", build_ln)
    res = run_bass_kernel_spmd(nc, [{"x": np.ascontiguousarray(s)} for s in x_shards],
                               core_ids=list(range(NCORES)))
    return [r["xn"] for r in res.results], [r["xn32"] for r in res.results]



_FUSED_SPEC = [
    ("x", [D, NTOK], "f"),
    ("a_wq", [D, D], "f"), ("a_wd", [D, 256], "f"), ("a_wu", [256, 2048], "f"),
    ("a_wo", [D, D], "f"), ("a_bqr", [1, D], "f"), ("a_bdr", [1, 256], "f"),
    ("a_bur", [1, 2048], "f"), ("a_bor", [1, D], "f"),
    ("a_ropec", [128, T], "f"), ("a_ropes", [128, T], "f"),
    ("r0_rw", [D, E], "f"), ("r0_brr", [1, E], "f"), ("r0_sc8", [1, E], "f"),
    ("r0_ident", [128, 128], "f"),
    ("r1_rw", [D, E], "f"), ("r1_brr", [1, E], "f"), ("r1_sc8", [1, E], "f"),
    ("m0_w1", [E, D, FF], "f"), ("m0_w2", [E, FF, D], "f"),
    ("m1_w1", [E, D, FF], "f"), ("m1_w2", [E, FF, D], "f"),
    ("g_win", [D, WIDTH], "f"), ("g_binr", [1, WIDTH], "f"),
    ("g_taps", [128, 24, 3], "f"), ("g_sbr", [128, 24], "f"),
    ("f_peT", [3, T], "f"), ("f_w1f", [3, 64], "f"), ("f_b1r", [1, 64], "f"),
    ("f_f1c", [64, 1], "f"), ("f_w2f", [64, 64], "f"), ("f_b2r", [1, 64], "f"),
    ("f_f2c", [64, 1], "f"), ("f_w3f", [64, D], "f"), ("f_b3r", [1, D], "f"),
    ("f_thr", [1, T], "f"), ("f_dlt", [128, DC], "f"),
    ("c_G1c", [64, 128], "f"), ("c_G1s", [64, 128], "f"),
    ("c_G1Kc", [64, 128], "f"), ("c_G1Ks", [64, 128], "f"),
    ("c_Lm", [128, 128], "f"), ("c_LKm", [128, 128], "f"), ("c_Linv", [128, 128], "f"),
    ("c_Twc8", [128, 512], "f"), ("c_Tws8", [128, 512], "f"),
    ("c_TwKc8", [128, 512], "f"), ("c_TwKs8", [128, 512], "f"),
    ("c_TwIc4", [64, 512], "f"), ("c_TwIs4", [64, 512], "f"),
    ("c_GBc", [128, 64], "f"), ("c_GBs", [128, 64], "f"),
    ("o_fb", [128, DC], "f"), ("o_wo", [D, D], "f"), ("o_bor", [1, D], "f"),
]


def build_fused():
    p = Prog("mega")
    S = Sched(p)
    ext = {}
    for nm, shape, _ in _FUSED_SPEC:
        ext[nm] = p.dram_in(nm, shape, F32)
    out = p.dram_out("out", [D, NTOK], F32)
    tmp = {
        "xn": p.dram_tmp("t_xn", [D, NTOK], BF16),
        "xn32": p.dram_tmp("t_xn32", [D, NTOK]),
        "s0": p.dram_tmp("t_s0", [D, NTOK]),
        "cb0": p.dram_tmp("t_cb0", [E, NTOK]),
        "ym0": p.dram_tmp("t_ym0", [D, NTOK]),
        "x1": p.dram_tmp("t_x1", [D, NTOK]),
        "xn1": p.dram_tmp("t_xn1", [D, NTOK], BF16),
        "xn132": p.dram_tmp("t_xn132", [D, NTOK]),
        "vT": p.dram_tmp("t_vT", [D, NTOK]),
        "x0T": p.dram_tmp("t_x0T", [D, NTOK]),
        "kT": p.dram_tmp("t_kT", [D, T]),
        "ycT": p.dram_tmp("t_ycT", [D, NTOK]),
        "s1": p.dram_tmp("t_s1", [D, NTOK]),
        "cb1": p.dram_tmp("t_cb1", [E, NTOK]),
        "ym1": p.dram_tmp("t_ym1", [D, NTOK]),
    }
    E_ = ext
    phases = [
        ("ln0", build_ln, {"x": E_["x"], "xn": tmp["xn"], "xn32": tmp["xn32"]}),
        ("att", build_l0att, {"xn": tmp["xn"], "s0": tmp["s0"],
                              "wq": E_["a_wq"], "wd": E_["a_wd"], "wu": E_["a_wu"], "wo": E_["a_wo"],
                              "bqr": E_["a_bqr"], "bdr": E_["a_bdr"], "bur": E_["a_bur"],
                              "bor": E_["a_bor"], "ropec": E_["a_ropec"], "ropes": E_["a_ropes"]}),
        ("rt0", lambda p=None, S=None, io=None: build_router("l0", p, S, io),
         {"xn": tmp["xn32"], "rw": E_["r0_rw"], "brr": E_["r0_brr"], "sc8": E_["r0_sc8"],
          "ident": E_["r0_ident"], "comb": tmp["cb0"]}),
        ("moe0", lambda p=None, S=None, io=None: build_moe("silu", p, S, io),
         {"xn": tmp["xn"], "w1": E_["m0_w1"], "w2": E_["m0_w2"], "comb": tmp["cb0"],
          "ymoe": tmp["ym0"]}),
        ("ep0", build_ep, {"x": E_["x"], "sT": tmp["s0"], "ym": tmp["ym0"], "x1": tmp["x1"]}),
        ("ln1", build_ln, {"x": tmp["x1"], "xn": tmp["xn1"], "xn32": tmp["xn132"]}),
        ("gate", build_l1gate, {"xn": tmp["xn1"], "win": E_["g_win"], "binr": E_["g_binr"],
                                "taps": E_["g_taps"], "sbr": E_["g_sbr"],
                                "vT": tmp["vT"], "x0T": tmp["x0T"]}),
        ("filt", build_filter, {"peT": E_["f_peT"], "w1f": E_["f_w1f"], "b1r": E_["f_b1r"],
                                "f1c": E_["f_f1c"], "w2f": E_["f_w2f"], "b2r": E_["f_b2r"],
                                "f2c": E_["f_f2c"], "w3f": E_["f_w3f"], "b3r": E_["f_b3r"],
                                "thr": E_["f_thr"], "dlt": E_["f_dlt"], "kT": tmp["kT"]}),
        ("conv", build_conv, {"vT": tmp["vT"], "kT": tmp["kT"], "ycT": tmp["ycT"],
                              "ident": E_["r0_ident"],
                              **{k[2:]: E_[k] for k in ext if k.startswith("c_")}}),
        ("l1o", build_l1out, {"yc": tmp["ycT"], "x0": tmp["x0T"], "fb": E_["o_fb"],
                              "wo": E_["o_wo"], "bor": E_["o_bor"], "s1": tmp["s1"]}),
        ("rt1", lambda p=None, S=None, io=None: build_router("l1", p, S, io),
         {"xn": tmp["xn132"], "rw": E_["r1_rw"], "brr": E_["r1_brr"], "sc8": E_["r1_sc8"],
          "ident": E_["r0_ident"], "comb": tmp["cb1"]}),
        ("moe1", lambda p=None, S=None, io=None: build_moe("gelu", p, S, io),
         {"xn": tmp["xn1"], "w1": E_["m1_w1"], "w2": E_["m1_w2"], "comb": tmp["cb1"],
          "ymoe": tmp["ym1"]}),
        ("ep1", build_ep, {"x": tmp["x1"], "sT": tmp["s1"], "ym": tmp["ym1"], "x1": out}),
    ]
    toks = None
    for name, fn, io in phases:
        p.begin_phase(name)
        toks = fn(p=p, S=S, io=io)
        S.next_phase()
        p.end_phase()
    S.emit_tail(toks)
    return p.done()


def fused_inmaps(inputs):
    x = np.asarray(inputs["L0_attn_x"])
    shards = _shard_x(x)
    aw = l0att_inmaps([shards[0]], inputs)[0]
    r0 = router_inmaps("l0", [shards[0]], inputs)[0]
    r1 = router_inmaps("l1", [shards[0]], inputs)[0]
    m0 = moe_inmaps("silu", [shards[0]], [np.zeros((E, NTOK), np.float32)], inputs)[0]
    m1 = moe_inmaps("gelu", [shards[0]], [np.zeros((E, NTOK), np.float32)], inputs)[0]
    gw = l1gate_inmaps([shards[0]], inputs)[0]
    fw = filter_inmaps(inputs)[0]
    cw = conv_inmaps([shards[0]], [np.zeros((D, T), np.float32)])[0]
    ow = l1out_inmaps([shards[0]], [shards[0]], inputs)[0]
    base = {}
    for src, pref, keys in [
        (aw, "a_", ["wq", "wd", "wu", "wo", "bqr", "bdr", "bur", "bor", "ropec", "ropes"]),
        (r0, "r0_", ["rw", "brr", "sc8", "ident"]),
        (r1, "r1_", ["rw", "brr", "sc8"]),
        (m0, "m0_", ["w1", "w2"]),
        (m1, "m1_", ["w1", "w2"]),
        (gw, "g_", ["win", "binr", "taps", "sbr"]),
        (fw, "f_", ["peT", "w1f", "b1r", "f1c", "w2f", "b2r", "f2c", "w3f", "b3r", "thr", "dlt"]),
        (cw, "c_", ["G1c", "G1s", "G1Kc", "G1Ks", "Lm", "LKm", "Linv", "Twc8", "Tws8",
                    "TwKc8", "TwKs8", "TwIc4", "TwIs4", "GBc", "GBs"]),
        (ow, "o_", ["fb", "wo", "bor"]),
    ]:
        for k in keys:
            base[pref + k] = src[k]
    return [dict(base, x=np.ascontiguousarray(s)) for s in shards]


def kernel_fused(**inputs):
    nc = _get("fused", build_fused)
    im = fused_inmaps(inputs)
    res = run_bass_kernel_spmd(nc, im, core_ids=list(range(NCORES)))
    full = np.empty((16, T, D), np.float32)
    for c in range(NCORES):
        full[c * BPC:(c + 1) * BPC] = np.asarray(res.results[c]["out"]).T.reshape(BPC, T, D)
    return full


def kernel(**inputs):
    return kernel_fused(**inputs)


def kernel_multilaunch(**inputs):
    x = np.asarray(inputs["L0_attn_x"])
    shards = _shard_x(x)
    A = lambda lst: [np.asarray(a) for a in lst]
    xn, xn32 = run_ln(shards)
    xn = A(xn); xn32 = A(xn32)
    s0 = A(run_l0att(xn, inputs))
    comb0 = A(run_router("l0", xn32, inputs))
    ym0 = A(run_moe("silu", xn, comb0, inputs))
    x1 = A(run_ep(shards, s0, ym0))
    xn1, xn132 = run_ln(x1)
    xn1 = A(xn1); xn132 = A(xn132)
    gates = run_l1gate(xn1, inputs)
    vsh = A([g[0] for g in gates])
    x0sh = A([g[1] for g in gates])
    kTs = A(run_filter(inputs))
    yc = A(run_conv(vsh, kTs))
    s1 = A(run_l1out(yc, x0sh, inputs))
    comb1 = A(run_router("l1", xn132, inputs))
    ym1 = A(run_moe("gelu", xn1, comb1, inputs))
    out = A(run_ep(x1, s1, ym1))
    full = np.empty((16, T, D), np.float32)
    for c in range(NCORES):
        full[c * BPC:(c + 1) * BPC] = np.asarray(out[c]).T.reshape(BPC, T, D)
    return full
